# version 03
# baseline (speedup 1.0000x reference)
import numpy as np
import bass_rust as _bass_rust
import concourse.bass as bass
import concourse.mybir as mybir
from concourse.bass_utils import run_bass_kernel_spmd
from concourse.library_config import mlp as _LIB_MLP

F32 = mybir.dt.float32
I16 = mybir.dt.int16
I32 = mybir.dt.int32
AF = mybir.ActivationFunctionType
OP = mybir.AluOpType
AX = mybir.AxisListType.X

DM = 1024
DQ = 512
NDT = 4
NST = 16
CH = 512
EF = 8
BIGV = 100000.0
NG = 4
SUP = 4


class _BassD(bass.Bass):
    _defer_register_allocation = True


class Sched:
    def __init__(self):
        self.ops = []
        self.cnt = {}

    def op(self, eng, fn, wait=(), inc=None, k=1):
        self.ops.append((eng, fn, tuple(wait), inc, k))
        if inc:
            self.cnt[inc] = self.cnt.get(inc, 0) + k
            return (inc, self.cnt[inc])
        return None

    def drain(self, eng, inc=None):
        return self.op(eng, "drain", inc=inc, k=1)

    def tok(self, sem):
        return (sem, self.cnt.get(sem, 0))

    def all_toks(self):
        return [(s, v) for s, v in self.cnt.items()]

    def bump(self, sem, k):
        self.cnt[sem] = self.cnt.get(sem, 0) + k
        return (sem, self.cnt[sem])

    def emit(self, name, eng, sems):
        hw = {}
        for e, fn, waits, inc, k in self.ops:
            if e != name:
                continue
            for (s, v) in waits:
                if v > 0 and hw.get(s, 0) < v:
                    eng.wait_ge(sems[s], v)
                    hw[s] = v
            if fn == "drain":
                inst = eng.drain()
            else:
                inst = fn(eng)
            if inc is not None and inst is not None:
                inst.then_inc(sems[inc], k)


def rap(ap_or_handle, off, pattern):
    if isinstance(ap_or_handle, bass.AP):
        return bass.AP(ap_or_handle.tensor, ap_or_handle.offset + off, pattern)
    return bass.AP(ap_or_handle, off, pattern)


def build_nc(L=8192, dbg=False, mixer_only=False, ffn_only=False):
    NCH = L // CH
    OWN = L // 4
    NOTL = OWN // 128
    NS = OWN + EF * 128
    NTILE = NS // 128
    NT128 = NTILE * 128
    CAPIO = OWN // 128 + 1
    NGC4 = OWN // 512
    assert NOTL == 16 * (L // 8192) or L == 8192 or True

    nc = _BassD("TRN2", target_bir_lowering=False)

    def din(name, shape, dt=F32):
        return nc.dram_tensor(name, shape, dt, kind="ExternalInput").ap()

    x_in = din("x_b", [L, DM])
    w_inproj = din("w_inproj", [128, 8192])
    w_xproj = din("w_xproj", [2, 128, 4 * 96])
    w_dt = din("w_dt", [2, 64, DQ])
    w_conv = din("w_conv", [2, 128, NDT * 4])
    b_conv = din("b_conv", [2, 128, NDT])
    b_dt = din("b_dt", [2, 128, NDT])
    a_log = din("a_log", [2, 128, NDT * NST])
    d_vec = din("d_vec", [2, 128, NDT])
    w_out = din("w_out", [128, 16384])
    w_gred = din("w_gred", [128, 128])
    wg_in = din("wg_in", [8, 16])
    w1_d = din("w1_d", [EF, 128, 8 * 32 * 128])
    w2_d = din("w2_d", [EF, 128, 32 * 8 * 128])
    b1h_in = din("b1h", [32, 1024])
    b2h_in = din("b2h", [8, 1024])
    ident_in = din("ident_in", [128, 128])
    sel16_in = din("sel16_in", [16, 128])
    sel2_in = din("sel2_in", [128, 16])
    iotaE_in = din("iotaE_in", [128, 1])
    iotaPAD_in = din("iotaPAD_in", [8, CAPIO])
    iotaTILE_in = din("iotaTILE_in", [8, NTILE])
    colk0_in = din("colk0_in", [1, 128])
    iota8m_in = din("iota8m_in", [128, 8])
    selb16_in = din("selb16_in", [16, 16 * 128])
    qmeta_in = din("qmeta_in", [1, 6], I32)

    out = nc.dram_tensor("out", [OWN, DM], F32, kind="ExternalOutput").ap()
    dbg_out = {}
    if dbg:
        for k, shp in [("y", [NCH, 128, 2 * NDT * CH]), ("md", [NCH, 128, 2 * NDT * CH]),
                       ("xt", [NCH, 128, 2 * NDT * CH]), ("xdbl", [NCH, 192, CH]),
                       ("score", [128, NOTL]), ("idx", [128, NOTL]),
                       ("destT", [16, 128]), ("etile", [1, NTILE])]:
            dbg_out[k] = nc.dram_tensor("dbg_" + k, shp, F32, kind="ExternalOutput").ap()

    xn_dram = nc.dram_tensor("xn_dram", [L, DM], F32)
    uf_dram = nc.dram_tensor("uf_dram", [NCH, 128, NDT * CH], F32)
    ub_dram = nc.dram_tensor("ub_dram", [NCH, 128, NDT * CH], F32)
    ar_in = nc.dram_tensor("ar_in", [96, CH], F32)
    ar_out = nc.dram_tensor("ar_out", [96, CH], F32)
    xz_dram = nc.dram_tensor("xz_dram", [NCH, 128, 8 * CH], F32)
    agi_d = [nc.dram_tensor(f"agi{i}", [DQ, CH], F32) for i in range(2)]
    ago_d = nc.dram_tensor("ago", [NCH, 4 * DQ, CH], F32)
    xg_dram = nc.dram_tensor("xg_dram", [NS, DM], F32)
    h_dram = nc.dram_tensor("h_dram", [32, 128, NT128], F32)
    so_dram = nc.dram_tensor("so_dram", [NS, DM], F32)
    g_dram = nc.dram_tensor("g_dram", [128, NOTL, DM], F32)
    s_dram = nc.dram_tensor("s_dram", [OWN, DM], F32)
    xr_dram = nc.dram_tensor("xr_dram", [NCH, 16, CH], F32)

    groups = [[0, 1, 2, 3], [4, 5, 6, 7]]

    A = nc.alloc_sbuf_tensor
    ar1 = A("ar1", [128, 16384], F32)
    ar2 = A("ar2", [128, 20480], F32)
    sb_winp = A("sb_winp", [128, 8192], F32)
    sb_wxp = A("sb_wxp", [128, 2 * 4 * 96], F32)
    sb_wdt = A("sb_wdt", [64, 2 * DQ], F32)
    sb_wcv = A("sb_wcv", [128, 2 * NDT * 4], F32)
    sb_bcv = A("sb_bcv", [128, 2 * NDT], F32)
    sb_bdtn = A("sb_bdtn", [128, 2 * NDT], F32)
    sb_apos = A("sb_apos", [128, 2 * NDT * NST], F32)
    sb_dvec = A("sb_dvec", [128, 2 * NDT], F32)
    sb_wgr = A("sb_wgr", [128, 128], F32)
    sb_wg = A("sb_wg", [8, 16], F32)
    sb_wguT = A("sb_wguT", [16, 8], F32)
    sb_gn = A("sb_gn", [8, 2], F32)
    sb_id = A("sb_id", [128, 128], F32)
    sb_sel16 = A("sb_sel16", [16, 128], F32)
    sb_sel2 = A("sb_sel2", [128, 16], F32)
    sb_iotaE = A("sb_iotaE", [128, 1], F32)
    sb_iPAD = A("sb_iPAD", [8, CAPIO], F32)
    sb_iTILE = A("sb_iTILE", [8, NTILE], F32)
    sb_colk0 = A("sb_colk0", [1, 128], F32)
    sb_i8m = A("sb_i8m", [128, 8], F32)
    sb_ones = A("sb_ones", [1, 128], F32)
    sb_selb = A("sb_selb", [16, 16 * 128], F32)
    sb_ones8 = A("sb_ones8", [8, 1], F32)
    sb_zer = A("sb_zer", [128, 512], F32)
    sb_sq = A("sb_sq", [128, 512], F32)
    sb_hold = A("sb_hold", [128, 2 * NDT * 4], F32)
    sb_xdbl = A("sb_xdbl", [96, 4 * CH], F32)
    sb_hin = A("sb_hin", [128, 2 * NDT * NST], F32)
    sb_stat = A("sb_stat", [128, 24], F32)
    sb_score = A("sb_score", [128, NOTL], F32)
    sb_cnt2 = A("sb_cnt2", [128, 1], F32)
    sb_g8 = A("sb_g8", [8, 1], F32)
    sb_p8c = A("sb_p8c", [8, CAPIO], F32)
    sb_p8 = A("sb_p8", [8, 1], F32)
    sb_r8 = A("sb_r8", [1, 8], F32)
    sb_eb = A("sb_eb", [1, 8], F32)
    sb_eb8 = A("sb_eb8", [8, 1], F32)
    sb_base2 = A("sb_base2", [128, 1], F32)
    sb_qm = A("sb_qm", [1, 6], I32)
    sb_sidx = sb_apos[:, 0:(NOTL * 8) // 2].bitcast(I16)
    sb_sidxT16 = sb_apos[:, 64:64 + (NOTL * 8) // 2].bitcast(I16)
    sb_etil_i = sb_hin[0:1, 0:NTILE].bitcast(I32)
    sb_cst = A("sb_cst", [128, 2], F32)
    nc.const_aps.register(F32, 1e-5, sb_cst[:, 0:1])
    nc.const_aps.register(F32, -1.0, sb_cst[:, 1:2])

    ps = nc.alloc_psum_tensor("ps", [128, 4096], F32)
    ps_a = ps[:, 0:512]
    ps_b = ps[:, 512:1024]
    ps_t = ps[:, 1024:1536]
    ps_s = ps[:, 1536:2048]
    ps_c = ps[:, 2048:2560]
    ps_d = ps[:, 2560:3072]
    ps_f = ps[:, 0:NT128]
    ps_x = ps[:, 3072:4096]

    def v1(c, w):
        return ar1[:, c:c + w]

    def v2(c, w):
        return ar2[:, c:c + w]

    B_brep = v1(0, NG * CH)
    B_crep = v1(2048, NG * CH)
    B_dbu = v1(4096, NG * CH)
    B_h = v1(6144, NG * CH)
    B_hc = v1(8192, NG * CH)
    B_u1 = v1(8192, NDT * CH)
    B_da = v1(10240, NG * CH)
    B_tmp = v1(10240, NDT * CH)
    B_u2 = v1(10240, NDT * CH)
    B_md = v1(12288, NDT * CH)
    B_dxp = v1(14336, NDT * CH)
    B_xz = [v2(0, 8 * CH), v2(4096, 8 * CH)]
    B_xt = v2(8192, 2 * NDT * CH)
    B_xtm = v2(12288, 4 * DM)
    B_sz = v2(12288, 2 * NDT * CH)
    B_xnfm = v2(16384, 8 * CH)
    B_y = v2(16384, 2 * NDT * CH)

    C_wout = v1(0, 16384)
    C_agl = v2(0, 16 * CH)
    C_sfm = v2(8192, 8 * CH)
    C_stm = v2(12288, 4 * DM)
    C_ua = v2(16384, NDT * CH)
    C_ub = v2(18432, NDT * CH)

    D_ssrc = v2(0, NOTL * DM)
    D_w1b = v1(0, 8192)
    D_hst = v1(8192, NT128)
    D_xg = v1(8192 + NT128, DM)
    D_t1b = v1(8192 + NT128 + DM, 256)
    B1R_OFF = 8192 + NT128 + DM + 256
    sb_b1r = ar1[0:1, B1R_OFF:B1R_OFF + 1024]
    sb_b2r = sb_xdbl[0:1, 0:1024]
    XGF_LO = 20480
    D_h3 = v1(0, SUP * NT128)
    D_w2b = v1(SUP * NT128, SUP * 1024)
    D_otm = v1(0, 1024)
    D_ga = v2(0, NOTL * DM)
    sb_wgu = ar1[0:8, 0:16]
    sb_xext = v1(0, CH + 8)
    sb_bB = sb_xdbl[0:16, 2 * CH:3 * CH]
    sb_bC = sb_xdbl[0:16, 3 * CH:4 * CH]
    sb_xrc = sb_xdbl[0:16, CH:2 * CH]
    D_xro = ar1[0:16, 0:OWN]
    sb_idxT = ar1[0:16, 2048:2176]
    sb_mask2 = v1(2176, 128)
    sb_cum2 = v1(2304, 128)
    sb_crow = ar1[0:1, 2432:2560]
    sb_kin = ar1[0:1, 2560:2688]
    sb_kex = ar1[0:1, 2688:2816]
    sb_stt2 = v1(2816, 128)
    sb_logit = v1(2944, NOTL * 8)
    sb_mneg = v1(3072, NOTL)
    sb_mpos = v1(3072 + NOTL, NOTL)
    sb_ssum = v1(3072 + 2 * NOTL, NOTL)
    sb_idx = v1(3072 + 3 * NOTL, NOTL)
    sb_mtmp = v1(3072 + 4 * NOTL, 8)
    sb_et8 = ar1[0:8, 3208:3208 + NTILE]
    sb_etil = ar1[0:1, 3240:3240 + NTILE]
    E_g = v1(0, DM)
    E_x = v1(1024, DM)
    E_s = v1(2048, DM)
    E_t1 = v1(3072, DM)

    def _split(col, w, base, ovf):
        if col + w <= XGF_LO:
            return ar2[:, col:col + w]
        c = col - XGF_LO + base
        return ovf[:, c:c + w]

    xgf_blk = lambda col, w=128: _split(col, w, 0, sb_winp)
    acc_blk = lambda col, w=128: _split(col, w, 4096, sb_winp)

    S = Sched()

    def dma(dst, src, wait=(), eng="sync", inc="LD"):
        return S.op(eng, (lambda d, s: (lambda e: e.dma_start(out=d, in_=s)))(dst, src),
                    wait=wait, inc=inc, k=16)

    def sto(dst, src, wait=(), eng="sync", inc="ST"):
        return dma(dst, src, wait=wait, eng=eng, inc=inc)

    def gdma(dst, src, wait=()):
        return dma(dst, src, wait=wait, eng="gpsimd", inc="GD")

    def pe(fn, wait=()):
        return S.op("tensor", fn, wait=wait, inc="PE")

    def sc(fn, wait=()):
        return S.op("scalar", fn, wait=wait, inc="SC")

    def ve(fn, wait=()):
        return S.op("vector", fn, wait=wait, inc="VE")

    def gp(fn, wait=()):
        return S.op("gpsimd", fn, wait=wait, inc="GP")

    for d, s in [
        (sb_winp[:, :], w_inproj), (sb_wxp[:, :].rearrange("p (d c) -> p d c", d=2), w_xproj.rearrange("d p c -> p d c")),
        (sb_wdt[:, :].rearrange("p (d c) -> p d c", d=2), w_dt.rearrange("d p c -> p d c")),
        (sb_wcv[:, :].rearrange("p (d c) -> p d c", d=2), w_conv.rearrange("d p c -> p d c")),
        (sb_bcv[:, :].rearrange("p (d c) -> p d c", d=2), b_conv.rearrange("d p c -> p d c")),
        (sb_bdtn[:, :].rearrange("p (d c) -> p d c", d=2), b_dt.rearrange("d p c -> p d c")),
        (sb_apos[:, :].rearrange("p (d c) -> p d c", d=2), a_log.rearrange("d p c -> p d c")),
        (sb_dvec[:, :].rearrange("p (d c) -> p d c", d=2), d_vec.rearrange("d p c -> p d c")),
        (sb_wgr[:, :], w_gred), (sb_wg[:, :], wg_in),
        (sb_id[:, :], ident_in), (sb_sel16[:, :], sel16_in),
        (sb_sel2[:, :], sel2_in), (sb_iotaE[:, :], iotaE_in),
        (sb_iPAD[:, :], iotaPAD_in), (sb_iTILE[:, :], iotaTILE_in),
        (sb_colk0[:, :], colk0_in), (sb_i8m[:, :], iota8m_in),
        (sb_selb[:, :], selb16_in),
        (sb_qm[:, :], qmeta_in),
    ]:
        dma(d, s, wait=[S.tok("LA")], inc="LA")
    tA = [S.tok("LA")]
    zer_dram = nc.dram_tensor("zer_dram", [128, 4096], F32)

    ve(lambda e: e.memset(sb_cst[:, 0:1], 1e-5))
    ve(lambda e: e.memset(sb_cst[:, 1:2], -1.0))
    ve(lambda e: e.memset(sb_ones[:, :], 1.0))
    ve(lambda e: e.memset(sb_ones8[:, :], 1.0))
    ve(lambda e: e.memset(sb_zer[:, :], 0.0))
    S.drain("vector")
    tzm = S.tok("VE")
    for zk in range(8):
        dma(zer_dram[:, 512 * zk:512 * (zk + 1)], sb_zer[:, :],
            wait=[tzm, S.tok("Zf")], inc="Zf")
    tzf = [S.tok("Zf")]
    ve(lambda e: e.memset(sb_hin[:, :], 0.0))
    ve(lambda e: e.memset(sb_hold[:, :], 0.0))
    ve(lambda e: e.tensor_scalar(out=sb_bdtn[:, :], in0=sb_bdtn[:, :], scalar1=-1.0,
                                 scalar2=None, op0=OP.mult), wait=tA)
    sc(lambda e: e.activation(out=sb_apos[:, :], in_=sb_apos[:, :], func=AF.Exp),
       wait=tA)
    sc(lambda e: e.activation(out=sb_wgu[:, :], in_=sb_wg[:, :], func=AF.Square,
                              accum_out=sb_gn[:, 0:1]))
    S.drain("scalar")
    sc(lambda e: e.activation(out=sb_gn[:, 1:2], in_=sb_gn[:, 0:1], func=AF.Sqrt))
    S.drain("scalar")
    tA_sc = S.tok("SC")
    ve(lambda e: e.tensor_scalar_max(sb_gn[:, 1:2], sb_gn[:, 1:2], 1e-4), wait=[tA_sc])
    S.drain("vector")
    ve(lambda e: e.reciprocal(out=sb_gn[:, 0:1], in_=sb_gn[:, 1:2]))
    S.drain("vector")
    ve(lambda e: e.tensor_scalar(out=sb_wgu[:, :], in0=sb_wg[:, :],
                                 scalar1=sb_gn[:, 0:1], scalar2=None, op0=OP.mult))
    S.drain("vector")
    tAv = S.tok("VE")
    pe(lambda e: e.transpose(ps_t[0:16, 0:8], sb_wgu[:, :], sb_id[0:8, 0:8]),
       wait=[tAv])
    tp = S.tok("PE")
    sc(lambda e: e.copy(out=sb_wguT[:, :], in_=ps_t[0:16, 0:8]), wait=[tp])
    S.drain("scalar")
    tA_done = S.tok("SC")

    xz_tok = {}
    for pass_ in ((0, 1) if not ffn_only else ()):
        dir_ = pass_
        for s_ in range(NCH):
            c = s_ if pass_ == 0 else NCH - 1 - s_
            ci = 0
            if pass_ == 0:
                tld = dma(B_xtm.rearrange("p (t d) -> p t d", d=DM),
                          x_in[CH * c:CH * (c + 1), :].rearrange("(t p) d -> p t d", p=128),
                          wait=[S.tok("VE"), S.tok("SC"), S.tok("PE"),
                                S.tok("Sxn"), S.tok("GP")], inc="Lx")
                for tt in range(4):
                    ve(lambda e, tt=tt: e.reduce_sum(sb_stat[:, 2 * tt:2 * tt + 1],
                                                     B_xtm[:, DM * tt:DM * (tt + 1)], axis=AX),
                       wait=[tld, tAv])
                    sc(lambda e, tt=tt: e.activation(out=sb_sq[:, :],
                                                     in_=B_xtm[:, DM * tt:DM * tt + 512],
                                                     func=AF.Square,
                                                     accum_out=sb_stat[:, 2 * tt + 1:2 * tt + 2]),
                       wait=[tld, tA_done])
                    S.drain("scalar")
                    sc(lambda e, tt=tt: e.activation(out=sb_sq[:, :],
                                                     in_=B_xtm[:, DM * tt + 512:DM * (tt + 1)],
                                                     func=AF.Square,
                                                     accum_out=sb_stat[:, 16 + tt:17 + tt]),
                       wait=[tld])
                    S.drain("scalar")
                S.drain("vector")
                tve, tsc = S.tok("VE"), S.tok("SC")
                ve(lambda e: e.tensor_tensor(out=sb_stat[:, 1:8:2], in0=sb_stat[:, 1:8:2],
                                             in1=sb_stat[:, 16:20], op=OP.add),
                   wait=[tve, tsc])
                S.drain("vector")
                ve(lambda e: e.tensor_scalar(out=sb_stat[:, 8:12], in0=sb_stat[:, 0:8:2],
                                             scalar1=1.0 / DM, scalar2=None, op0=OP.mult))
                S.drain("vector")
                ve(lambda e: e.tensor_tensor(out=sb_stat[:, 12:16], in0=sb_stat[:, 8:12],
                                             in1=sb_stat[:, 8:12], op=OP.mult))
                ve(lambda e: e.tensor_scalar(out=sb_stat[:, 0:8:2], in0=sb_stat[:, 1:8:2],
                                             scalar1=1.0 / DM, scalar2=None, op0=OP.mult))
                S.drain("vector")
                ve(lambda e: e.tensor_tensor(out=sb_stat[:, 12:16], in0=sb_stat[:, 0:8:2],
                                             in1=sb_stat[:, 12:16], op=OP.subtract))
                S.drain("vector")
                tve = S.tok("VE")
                sc(lambda e: e.activation(out=sb_stat[:, 0:8:2], in_=sb_stat[:, 12:16],
                                          func=AF.Sqrt, bias=1e-5), wait=[tve])
                S.drain("scalar")
                tsc = S.tok("SC")
                ve(lambda e: e.reciprocal(out=sb_stat[:, 12:16], in_=sb_stat[:, 0:8:2]),
                   wait=[tsc])
                S.drain("vector")
                for tt in range(4):
                    ve(lambda e, tt=tt: e.tensor_scalar(
                        out=B_xtm[:, DM * tt:DM * (tt + 1)],
                        in0=B_xtm[:, DM * tt:DM * (tt + 1)],
                        scalar1=sb_stat[:, 8 + tt:9 + tt],
                        scalar2=sb_stat[:, 12 + tt:13 + tt],
                        op0=OP.subtract, op1=OP.mult))
                S.drain("vector")
                txn = S.tok("VE")
                sto(xn_dram[CH * c:CH * (c + 1), :].rearrange("(t p) d -> p t d", p=128),
                    B_xtm.rearrange("p (t d) -> p t d", d=DM), wait=[txn],
                    eng="sync", inc="Sxn")
                for tt in range(4):
                    for kc in range(8):
                        pe(lambda e, tt=tt, kc=kc: e.transpose(
                            ps_t[:, 128 * (kc % 4):128 * (kc % 4) + 128],
                            B_xtm[:, DM * tt + 128 * kc:DM * tt + 128 * (kc + 1)],
                            sb_id[:, :]),
                           wait=[txn] + ([S.tok("SC")] if kc % 4 == 0 else []))
                        if kc % 4 == 3:
                            tp = S.tok("PE")
                            extra2 = ([S.tok("VE")] +
                                      ([S.tok("SG")] if dbg else [])
                                      if tt == 0 and kc == 3 else [])
                            sc(lambda e, tt=tt, kc=kc: e.copy(
                                out=B_xnfm.rearrange("p (k t) -> p k t", t=CH)
                                    [:, kc - 3:kc + 1, 128 * tt:128 * (tt + 1)],
                                in_=ps_t[:, 0:512].rearrange("p (k t) -> p k t", t=128)),
                               wait=[tp] + extra2)
                S.drain("scalar")
                ttr = S.tok("SC")
                for m in range(8):
                    bank = ps_a if m % 2 == 0 else ps_b
                    for kc in range(8):
                        pe(lambda e, m=m, kc=kc, bank=bank: e.matmul(
                            bank,
                            sb_winp[:, 1024 * kc + 128 * m:1024 * kc + 128 * (m + 1)],
                            B_xnfm[:, CH * kc:CH * (kc + 1)],
                            start=(kc == 0), stop=(kc == 7)),
                           wait=[ttr] + tA + ([S.tok("SC")] if kc == 0 else []))
                    tp = S.tok("PE")
                    extra = ([S.tok("VE"), S.tok("GP"), S.tok("Sxz")]
                             if m == 0 else [])
                    sc(lambda e, m=m, bank=bank, ci=ci: e.copy(
                        out=B_xz[ci][:, CH * m:CH * (m + 1)], in_=bank),
                       wait=[tp] + extra)
                S.drain("scalar")
                t_xz = S.tok("SC")
                xz_tok[c] = sto(xz_dram[c, :, :], B_xz[ci][:, :], wait=[t_xz],
                                eng="sync", inc="Sxz")
                for kc in range(8):
                    pe(lambda e, kc=kc: e.matmul(
                        ps_s[0:16, :], sb_wgr[:, 16 * kc:16 * (kc + 1)],
                        B_xnfm[:, CH * kc:CH * (kc + 1)],
                        start=(kc == 0), stop=(kc == 7)),
                       wait=[ttr] + tA + ([S.tok("SC")] if kc == 0 else []))
                tp = S.tok("PE")
                sc(lambda e: e.copy(out=sb_xrc[:, :], in_=ps_s[0:16, :]),
                   wait=[tp, S.tok("Sxr")])
                S.drain("scalar")
                txr = S.tok("SC")
                sto(xr_dram[c, :, :], sb_xrc[:, :], wait=[txr], eng="sync",
                    inc="Sxr")
            else:
                tld = dma(B_xz[ci][:, :], xz_dram[c, :, :],
                          wait=[S.tok("VE"), S.tok("SC"), S.tok("GP"),
                                xz_tok[c], S.tok("Su")], inc="Lz")
                t_xz = tld

            for dt in range(NDT):
                srcx = B_xz[ci][:, CH * dt:CH * (dt + 1)]
                if dir_ == 1:
                    srcx = srcx[:, ::-1]
                ve(lambda e, srcx=srcx: e.tensor_copy(out=sb_xext[:, 3:3 + CH], in_=srcx),
                   wait=[t_xz, S.tok("SC")])
                ve(lambda e, dir_=dir_, dt=dt: e.tensor_copy(
                    out=sb_xext[:, 0:3],
                    in_=sb_hold[:, 4 * (NDT * dir_ + dt):4 * (NDT * dir_ + dt) + 3]))
                S.drain("vector")
                ve(lambda e, dir_=dir_, dt=dt: e.tensor_copy(
                    out=sb_hold[:, 4 * (NDT * dir_ + dt):4 * (NDT * dir_ + dt) + 3],
                    in_=sb_xext[:, CH:CH + 3]))
                ve(lambda e, dir_=dir_, dt=dt: e.tensor_scalar(
                    out=sb_sq[:, 0:CH], in0=sb_xext[:, 0:CH],
                    scalar1=sb_wcv[:, 4 * (NDT * dir_ + dt):4 * (NDT * dir_ + dt) + 1],
                    scalar2=None, op0=OP.mult))
                for k in range(1, 4):
                    S.drain("vector")
                    ve(lambda e, k=k, dir_=dir_, dt=dt: e.scalar_tensor_tensor(
                        out=sb_sq[:, 0:CH], in0=sb_xext[:, k:k + CH],
                        scalar=sb_wcv[:, 4 * (NDT * dir_ + dt) + k:
                                      4 * (NDT * dir_ + dt) + k + 1],
                        in1=sb_sq[:, 0:CH], op0=OP.mult, op1=OP.add))
                S.drain("vector")
                tve = S.tok("VE")
                sc(lambda e, dir_=dir_, dt=dt: e.activation(
                    out=B_xt[:, CH * dt:CH * (dt + 1)],
                    in_=sb_sq[:, 0:CH], func=AF.Silu,
                    bias=sb_bcv[:, NDT * dir_ + dt:NDT * dir_ + dt + 1]),
                   wait=[tve] + ([S.tok("SG")] if dbg and dt == 0 else []))
                S.drain("scalar")
            t_xt = S.tok("SC")
            for kc in range(NDT):
                pe(lambda e, dir_=dir_, kc=kc: e.matmul(
                    ps_s[0:96, :],
                    sb_wxp[:, 96 * (4 * dir_ + kc):96 * (4 * dir_ + kc + 1)],
                    B_xt[:, CH * kc:CH * (kc + 1)],
                    start=(kc == 0), stop=(kc == 3)),
                   wait=[t_xt] + ([S.tok("SC")] if kc == 0 else []))
            tp = S.tok("PE")
            sc(lambda e: e.copy(out=sb_xdbl[:, 0:CH], in_=ps_s[0:96, :]),
               wait=[tp, S.tok("Lxd"), S.tok("LbB"), S.tok("LbC"), S.tok("CC")])
            S.drain("scalar")
            tsc = S.tok("SC")
            sto(ar_in.ap(), sb_xdbl[:, 0:CH], wait=[tsc, S.tok("CC")],
                eng="sync", inc="Sar")
            tst = S.tok("Sar")
            S.op("gpsimd", lambda e: e.collective_compute(
                "AllReduce", OP.add, replica_groups=groups,
                ins=[ar_in.ap().opt()], outs=[ar_out.ap().opt()]),
                wait=[tst, S.tok("Lxd"), S.tok("LbB"), S.tok("LbC")],
                inc="CC", k=1)
            tcc = S.tok("CC")
            txda = dma(sb_xdbl[:, 0:CH], ar_out.ap(),
                       wait=[tcc, S.tok("SC"), S.tok("PE")], inc="Lxd")
            tbB = dma(sb_bB[:, 0:CH], ar_out[64:80, :],
                      wait=[tcc, S.tok("PE")], inc="LbB")
            tbC = dma(sb_bC[:, 0:CH], ar_out[80:96, :], wait=[tcc], inc="LbC")
            txd = [txda, tbB, tbC]
            if dbg:
                sto(dbg_out["xdbl"][c, 96 * dir_:96 * (dir_ + 1), :],
                    ar_out.ap(), wait=[tcc, S.tok("SG")], eng="sync", inc="SG")

            for dt in range(NDT):
                pe(lambda e, dir_=dir_, dt=dt: e.matmul(
                    ps_s, sb_wdt[:, DQ * dir_ + 128 * dt:DQ * dir_ + 128 * (dt + 1)],
                    sb_xdbl[0:64, 0:CH],
                    start=True, stop=True),
                   wait=txd + [S.tok("SC")])
                tp = S.tok("PE")
                extra = ([S.tok("VE"), S.tok("GP")] +
                         ([S.tok("SG")] if dbg else []) if dt == 0 else [])
                sc(lambda e, dir_=dir_, dt=dt: e.activation(
                    out=B_tmp[:, CH * dt:CH * (dt + 1)], in_=ps_s,
                    func=AF.Sigmoid, scale=-1.0,
                    bias=sb_bdtn[:, NDT * dir_ + dt:NDT * dir_ + dt + 1]),
                   wait=[tp] + extra)
                S.drain("scalar")
            for dt in range(NDT):
                sc(lambda e, dt=dt: e.activation(
                    out=B_md[:, CH * dt:CH * (dt + 1)],
                    in_=B_tmp[:, CH * dt:CH * (dt + 1)], func=AF.Ln))
            S.drain("scalar")
            tmd = S.tok("SC")
            ve(lambda e, dir_=dir_: e.scalar_tensor_tensor(
                out=B_dxp, in0=B_md, scalar=-1.0,
                in1=B_xt[:, 0:NDT * CH],
                op0=OP.mult, op1=OP.mult), wait=[tmd, t_xt])
            S.drain("vector")
            tdxp = S.tok("VE")
            if dbg:
                sto(dbg_out["md"][c, :, CH * NDT * dir_:CH * NDT * (dir_ + 1)],
                    B_md[:, :], wait=[tmd, S.tok("SG")], eng="sync", inc="SG")
                sto(dbg_out["xt"][c, :, CH * NDT * dir_:CH * NDT * (dir_ + 1)],
                    B_xt[:, 0:NDT * CH], wait=[t_xt, S.tok("SG")], eng="sync",
                    inc="SG")
            for dt in range(NDT):
                for g in range(NST // NG):
                    for j in range(NG):
                        n = NG * g + j
                        for srcbc, buf, bank in ((sb_bB, B_brep, ps_c),
                                                 (sb_bC, B_crep, ps_d)):
                            pe(lambda e, n=n, srcbc=srcbc, bank=bank: e.matmul(
                                bank, sb_selb[:, 128 * n:128 * (n + 1)],
                                srcbc[:, 0:CH],
                                start=True, stop=True),
                               wait=txd + [S.tok("SC")])
                            tp = S.tok("PE")
                            extra = [S.tok("GP"), S.tok("VE")] if j == 0 else []
                            sc(lambda e, j=j, buf=buf, bank=bank: e.copy(
                                out=buf[:, CH * j:CH * (j + 1)], in_=bank),
                               wait=[tp] + extra)
                    S.drain("scalar")
                    trep = S.tok("SC")
                    for j in range(NG):
                        n = NG * g + j
                        sc(lambda e, n=n, j=j, dt=dt, dir_=dir_: e.activation(
                            out=B_da[:, CH * j:CH * (j + 1)],
                            in_=B_md[:, CH * dt:CH * (dt + 1)], func=AF.Exp,
                            scale=sb_apos[:, NST * (NDT * dir_ + dt) + n:
                                          NST * (NDT * dir_ + dt) + n + 1]),
                           wait=[tmd] + ([S.tok("VE")] if j == 0 else []))
                    S.drain("scalar")
                    tda = S.tok("SC")
                    gp(lambda e, dt=dt: e.tensor_tensor(
                        out=B_dbu.rearrange("p (j t) -> p j t", t=CH),
                        in0=B_dxp[:, CH * dt:CH * (dt + 1)].unsqueeze(1)
                            .broadcast_to([128, NG, CH]),
                        in1=B_brep.rearrange("p (j t) -> p j t", t=CH),
                        op=OP.mult), wait=[trep, tdxp, S.tok("VE")])
                    S.drain("gpsimd")
                    tdbu = S.tok("GP")
                    for j in range(NG):
                        n = NG * g + j
                        col = NST * (NDT * dir_ + dt) + n
                        ve(lambda e, j=j, col=col: e.tensor_tensor_scan(
                            out=B_h[:, CH * j:CH * (j + 1)],
                            data0=B_da[:, CH * j:CH * (j + 1)],
                            data1=B_dbu[:, CH * j:CH * (j + 1)],
                            initial=sb_hin[:, col:col + 1],
                            op0=OP.mult, op1=OP.add), wait=[tda, tdbu])
                    S.drain("vector")
                    for j in range(NG):
                        n = NG * g + j
                        col = NST * (NDT * dir_ + dt) + n
                        ve(lambda e, j=j, col=col: e.tensor_copy(
                            out=sb_hin[:, col:col + 1],
                            in_=B_h[:, CH * j + CH - 1:CH * (j + 1)]))
                    ve(lambda e: e.tensor_tensor(out=B_hc, in0=B_h, in1=B_crep,
                                                 op=OP.mult),
                       wait=[trep] + ([S.tok("Su")] if dt == 0 and g == 0
                                      else []))
                    S.drain("vector")
                    thc = S.tok("VE")
                    if g == 0:
                        ve(lambda e, dt=dt: e.tensor_reduce(
                            out=B_y[:, CH * dt:CH * (dt + 1)],
                            in_=rap(B_hc, 0, [[16384, 128], [1, CH], [CH, NG]]),
                            axis=AX, op=OP.add),
                           wait=[thc, S.tok("GP")] +
                                ([S.tok("SG")] if dbg else []))
                        S.drain("vector")
                    else:
                        ve(lambda e: e.tensor_reduce(
                            out=sb_sq[:, 0:CH],
                            in_=rap(B_hc, 0, [[16384, 128], [1, CH], [CH, NG]]),
                            axis=AX, op=OP.add), wait=[thc, S.tok("GP")])
                        S.drain("vector")
                        tvr = S.tok("VE")
                        gp(lambda e, dt=dt: e.tensor_tensor(
                            out=B_y[:, CH * dt:CH * (dt + 1)],
                            in0=B_y[:, CH * dt:CH * (dt + 1)],
                            in1=sb_sq[:, 0:CH], op=OP.add), wait=[tvr])
                        S.drain("gpsimd")
                tydt = S.tok("GP")
                ve(lambda e, dt=dt, dir_=dir_: e.scalar_tensor_tensor(
                    out=B_y[:, CH * dt:CH * (dt + 1)],
                    in0=B_xt[:, CH * dt:CH * (dt + 1)],
                    scalar=sb_dvec[:, NDT * dir_ + dt:NDT * dir_ + dt + 1],
                    in1=B_y[:, CH * dt:CH * (dt + 1)],
                    op0=OP.mult, op1=OP.add), wait=[tydt])
                S.drain("vector")
            tyy = S.tok("VE")
            if dbg:
                sto(dbg_out["y"][c, :, CH * NDT * dir_:CH * NDT * (dir_ + 1)],
                    B_y[:, 0:NDT * CH], wait=[tyy, S.tok("SG")], eng="sync",
                    inc="SG")
            sc(lambda e, ci=ci: e.activation(
                out=B_sz[:, 0:NDT * CH],
                in_=B_xz[ci][:, NDT * CH:2 * NDT * CH], func=AF.Silu),
               wait=[t_xz, tyy, S.tok("Sxn")])
            S.drain("scalar")
            tsz = S.tok("SC")
            if dir_ == 0:
                ve(lambda e: e.tensor_tensor(out=B_u1, in0=B_y[:, 0:NDT * CH],
                                             in1=B_sz[:, 0:NDT * CH], op=OP.mult),
                   wait=[tsz, tyy, S.tok("GP"), S.tok("Su")])
            else:
                ve(lambda e: e.tensor_tensor(
                    out=B_u1, in0=B_y[:, 0:NDT * CH],
                    in1=rap(B_sz, CH - 1, [[20480, 128], [CH, NDT], [-1, CH]]),
                    op=OP.mult), wait=[tsz, tyy, S.tok("GP"), S.tok("Su")])
            S.drain("vector")
            tu = S.tok("VE")
            if pass_ == 0:
                sto(uf_dram[c, :, :], B_u1[:, :], wait=[tu], eng="sync", inc="Su")
            else:
                sto(ub_dram[s_, :, :], B_u1[:, :], wait=[tu], eng="sync",
                    inc="Su")

    tB_st = [S.tok("Su")]

    for c in (range(NCH) if not ffn_only else ()):
        buf = c % 2
        t1 = dma(C_ua[:, :], uf_dram[c, :, :],
                 wait=tB_st + [S.tok("VE"), S.tok("Sag")] +
                      ([S.tok("SG")] if dbg else []),
                 inc="Luf")
        t2 = dma(C_ub[:, :], ub_dram[NCH - 1 - c, :, :], wait=tB_st, inc="Lub")
        ve(lambda e: e.tensor_tensor(
            out=C_ua[:, :], in0=C_ua[:, :],
            in1=rap(C_ub, CH - 1, [[20480, 128], [CH, NDT], [-1, CH]]),
            op=OP.add), wait=[t1, t2])
        S.drain("vector")
        tcmb = S.tok("VE")
        tst = sto(agi_d[buf].ap().rearrange("(d p) t -> p d t", p=128),
                  C_ua[:, :].rearrange("p (d t) -> p d t", t=CH),
                  wait=[tcmb, S.tok("CC")], eng="sync", inc="Sag")
        S.op("gpsimd", (lambda b, c: (lambda e: e.collective_compute(
            "AllGather", OP.bypass, replica_groups=groups,
            ins=[agi_d[b].ap().opt()],
            outs=[ago_d[c, :, :].opt()])))(buf, c),
            wait=[tst], inc="CC", k=1)
    tC1 = S.tok("CC")

    if mixer_only:
        CHJ = min(CH, OWN)
        TJ = CHJ // 128
        twout = dma(C_wout[:, :], w_out,
                    wait=[S.tok("SC"), S.tok("VE"), S.tok("PE")], inc="Wout")

        def gp_agl_m(j):
            def f(e):
                r = e.alloc_register()
                e.reg_load(r, sb_qm[0:1, 4:5])
                if j:
                    e.reg_add(r, r, j * (4 * DQ) * CH)
                return e.dma_start(
                    out=C_agl[:, :].rearrange("p (k t) -> p k t", t=CHJ)
                        [:, 0:16, :],
                    in_=bass.AP(ago_d.ap().tensor, r,
                                [[CH, 128], [128 * CH, 16], [1, CHJ]]))
            return f

        for j in range(OWN // CHJ):
            S.op("gpsimd", gp_agl_m(j), wait=[tC1, S.tok("PE"), S.tok("SC")],
                 inc="Ga", k=16)
            tagl = S.tok("Ga")
            for m in range(8):
                bank = ps_a if m % 2 == 0 else ps_b
                for kc in range(16):
                    pe((lambda m, kc, bank: (lambda e: e.matmul(
                        bank[:, 0:CHJ],
                        C_wout[:, 1024 * kc + 128 * m:1024 * kc + 128 * (m + 1)],
                        C_agl[:, CHJ * kc:CHJ * (kc + 1)],
                        start=(kc == 0), stop=(kc == 15))))(m, kc, bank),
                       wait=[twout, tagl] + ([S.tok("SC")] if kc == 0 else []))
                tp = S.tok("PE")
                sc((lambda m, bank: (lambda e: e.mul(
                    out=C_sfm[:, CHJ * m:CHJ * (m + 1)],
                    in_=bank[:, 0:CHJ], mul=0.5)))(m, bank),
                   wait=[tp])
            S.drain("scalar")
            tsf = S.tok("SC")
            for tt in range(TJ):
                for kc in range(8):
                    pe((lambda tt, kc: (lambda e: e.transpose(
                        ps_t[:, 128 * (kc % 4):128 * (kc % 4) + 128],
                        C_sfm[:, CHJ * kc + 128 * tt:CHJ * kc + 128 * (tt + 1)],
                        sb_id[:, :])))(tt, kc),
                       wait=[tsf] + ([S.tok("SC")] if kc % 4 == 0 else []))
                    if kc % 4 == 3:
                        tp = S.tok("PE")
                        sc((lambda tt, kc: (lambda e: e.copy(
                            out=C_stm.rearrange("p (t d) -> p t d", d=DM)
                                [:, tt:tt + 1, 128 * (kc - 3):128 * (kc + 1)],
                            in_=ps_t[:, 0:512].unsqueeze(1))))(tt, kc),
                           wait=[tp] + ([S.tok("SoM")] if tt == 0 else []))
            S.drain("scalar")
            tstm = S.tok("SC")
            sto(out[CHJ * j:CHJ * (j + 1), :].rearrange("(t p) d -> p t d", p=128),
                C_stm.rearrange("p (t d) -> p t d", d=DM)[:, 0:TJ, :],
                wait=[tstm], eng="sync", inc="SoM")
        S.op("sync", lambda e: e.nop(), wait=S.all_toks())
        nc._dbg_out = dbg_out
        return _finish(nc, S, mybir)

    REGS = {}

    def regs_load(ename, mult):
        def f(e):
            for t in range(NTILE):
                r = e.alloc_register()
                e.reg_load(r, sb_etil_i[0:1, t:t + 1])
                if mult != 1:
                    e.reg_mul(r, r, mult)
                REGS[(ename, t)] = r
            rtmp = e.alloc_register()
            REGS[(ename, "tmp")] = rtmp
            return None
        return f

    def gp_xro(e):
        r = e.alloc_register()
        e.reg_load(r, sb_qm[0:1, 3:4])
        return e.dma_start(
            out=D_xro.rearrange("p (c t) -> p c t", t=CH),
            in_=bass.AP(xr_dram.ap().tensor, r,
                        [[CH, 16], [16 * CH, OWN // CH], [1, CH]]))

    S.op("gpsimd", gp_xro,
         wait=tA + tB_st + [S.tok("Sxr"), S.tok("SC"), S.tok("VE"), S.tok("PE")],
         inc="Gx", k=16)
    txro = S.tok("Gx")

    for tt in range(NOTL):
        pe(lambda e, tt=tt: e.matmul(
            ps_s[:, 8 * tt:8 * (tt + 1)],
            D_xro[:, 128 * tt:128 * (tt + 1)], sb_wguT[:, :],
            start=True, stop=True),
           wait=[txro, tA_done] + ([S.tok("SC")] if tt == 0 else []))
    tp = S.tok("PE")
    sc(lambda e: e.copy(out=sb_logit[:, :], in_=ps_s[:, 0:8 * NOTL]), wait=[tp])
    S.drain("scalar")
    tlg = S.tok("SC")
    for tt in range(NOTL):
        ve(lambda e, tt=tt: e.reduce_max(sb_mneg[:, tt:tt + 1],
                                         sb_logit[:, 8 * tt:8 * (tt + 1)],
                                         axis=AX, negate=True), wait=[tlg])
    S.drain("vector")
    tve = S.tok("VE")
    for tt in range(NOTL):
        sc(lambda e, tt=tt: e.activation(
            out=sb_sq[:, 0:8], in_=sb_logit[:, 8 * tt:8 * (tt + 1)], func=AF.Exp,
            bias=sb_mneg[:, tt:tt + 1], accum_out=sb_ssum[:, tt:tt + 1]),
           wait=[tve])
        S.drain("scalar")
    tsc = S.tok("SC")
    ve(lambda e: e.reciprocal(out=sb_score[:, :], in_=sb_ssum[:, :]), wait=[tsc])
    ve(lambda e: e.tensor_scalar(out=sb_mpos[:, :], in0=sb_mneg[:, :], scalar1=-1.0,
                                 scalar2=None, op0=OP.mult))
    S.drain("vector")
    for tt in range(NOTL):
        ve(lambda e, tt=tt: e.tensor_scalar(
            out=sb_mtmp[:, :], in0=sb_logit[:, 8 * tt:8 * (tt + 1)],
            scalar1=sb_mpos[:, tt:tt + 1], scalar2=None, op0=OP.is_equal))
        S.drain("vector")
        ve(lambda e: e.tensor_tensor(out=sb_mtmp[:, :], in0=sb_mtmp[:, :],
                                     in1=sb_i8m[:, :], op=OP.mult))
        S.drain("vector")
        ve(lambda e, tt=tt: e.tensor_reduce(out=sb_idx[:, tt:tt + 1],
                                            in_=sb_mtmp[:, :], axis=AX, op=OP.min))
        S.drain("vector")
    ve(lambda e: e.tensor_scalar(out=sb_idx[:, :], in0=sb_idx[:, :], scalar1=BIGV,
                                 scalar2=None, op0=OP.add))
    S.drain("vector")
    tidx = S.tok("VE")
    if dbg:
        sto(dbg_out["score"], sb_score[:, :], wait=[tidx, S.tok("SG")],
            eng="sync", inc="SG")
        sto(dbg_out["idx"], sb_idx[:, :], wait=[tidx, S.tok("SG")], eng="sync",
            inc="SG")

    assert NOTL == 16
    pe(lambda e: e.transpose(ps_t[0:16, 0:128], sb_idx[:, :], sb_id[:, :]),
       wait=[tidx, S.tok("SC")])
    tp = S.tok("PE")
    sc(lambda e: e.copy(out=sb_idxT[:, :], in_=ps_t[0:16, 0:128]), wait=[tp])
    S.drain("scalar")
    tsc = S.tok("SC")
    pe(lambda e: e.matmul(ps_t[:, 0:128], sb_sel16[:, :], sb_idxT[:, :],
                          start=True, stop=True), wait=[tsc] + tA)
    tp = S.tok("PE")
    ve(lambda e: e.tensor_scalar(out=sb_mask2[:, :], in0=ps_t[:, 0:128],
                                 scalar1=sb_iotaE[:, 0:1], scalar2=None,
                                 op0=OP.is_equal), wait=[tp])
    S.drain("vector")
    ve(lambda e: e.tensor_tensor_scan(out=sb_cum2[:, :], data0=sb_mask2[:, :],
                                      data1=sb_zer[:, 0:128], initial=0.0,
                                      op0=OP.add, op1=OP.add))
    S.drain("vector")
    ve(lambda e: e.tensor_copy(out=sb_cnt2[:, :], in_=sb_cum2[:, 127:128]))
    S.drain("vector")
    tve = S.tok("VE")
    pe(lambda e: e.transpose(ps_t[0:1, 0:128], sb_cnt2[:, :], sb_id[:, :]),
       wait=[tve, S.tok("SC")])
    tp = S.tok("PE")
    sc(lambda e: e.copy(out=sb_crow[:, :], in_=ps_t[0:1, 0:128]), wait=[tp])
    S.drain("scalar")
    tsc = S.tok("SC")
    ve(lambda e: e.tensor_tensor_scan(out=sb_kin[:, :], data0=sb_colk0[:, :],
                                      data1=sb_crow[:, :], initial=0.0,
                                      op0=OP.mult, op1=OP.add), wait=[tsc])
    S.drain("vector")
    ve(lambda e: e.tensor_tensor(out=sb_kex[:, :], in0=sb_kin[:, :],
                                 in1=sb_crow[:, :], op=OP.subtract))
    ve(lambda e: e.tensor_copy(out=sb_r8[:, :],
                               in_=rap(sb_kin, 15, [[16384, 1], [16, 8]])))
    S.drain("vector")
    tve = S.tok("VE")
    pe(lambda e: e.transpose(ps_t[0:8, 0:1], sb_r8[:, :], sb_id[0:1, 0:1]),
       wait=[tve, S.tok("SC")])
    tp = S.tok("PE")
    sc(lambda e: e.copy(out=sb_g8[:, :], in_=ps_t[0:8, 0:1]), wait=[tp])
    S.drain("scalar")
    tsc = S.tok("SC")
    ve(lambda e: e.tensor_scalar(out=sb_p8c[:, :], in0=sb_iPAD[:, :],
                                 scalar1=sb_g8[:, 0:1], scalar2=None, op0=OP.is_lt),
       wait=[tsc])
    S.drain("vector")
    ve(lambda e: e.reduce_sum(sb_p8[:, :], sb_p8c[:, :], axis=AX))
    S.drain("vector")
    ve(lambda e: e.tensor_scalar(out=sb_p8[:, :], in0=sb_p8[:, :], scalar1=128.0,
                                 scalar2=None, op0=OP.mult))
    S.drain("vector")
    tve = S.tok("VE")
    pe(lambda e: e.transpose(ps_t[0:1, 0:8], sb_p8[:, :], sb_id[0:8, 0:8]),
       wait=[tve, S.tok("SC")])
    tp = S.tok("PE")
    sc(lambda e: e.copy(out=sb_r8[:, :], in_=ps_t[0:1, 0:8]), wait=[tp])
    S.drain("scalar")
    tsc = S.tok("SC")
    ve(lambda e: e.tensor_tensor_scan(out=sb_eb[:, :], data0=sb_ones[0:1, 0:8],
                                      data1=sb_r8[:, :], initial=0.0,
                                      op0=OP.mult, op1=OP.add), wait=[tsc])
    S.drain("vector")
    ve(lambda e: e.tensor_tensor(out=sb_eb[:, :], in0=sb_eb[:, :], in1=sb_r8[:, :],
                                 op=OP.subtract))
    S.drain("vector")
    tve = S.tok("VE")
    pe(lambda e: e.transpose(ps_t[0:8, 0:1], sb_eb[:, :], sb_id[0:1, 0:1]),
       wait=[tve, S.tok("SC")])
    tp = S.tok("PE")
    sc(lambda e: e.copy(out=sb_eb8[:, :], in_=ps_t[0:8, 0:1]), wait=[tp])
    S.drain("scalar")
    tsc8 = S.tok("SC")
    ve(lambda e: e.tensor_tensor(
        out=sb_kin[:, :], in0=sb_kex[:, :],
        in1=rap(sb_eb, 0, [[8, 1], [1, 8], [0, 16]]),
        op=OP.add), wait=[tve])
    S.drain("vector")
    tve = S.tok("VE")
    pe(lambda e: e.transpose(ps_t[:, 0:1], sb_kin[:, :], sb_id[0:1, 0:1]),
       wait=[tve, S.tok("SC")])
    tp = S.tok("PE")
    sc(lambda e: e.copy(out=sb_base2[:, :], in_=ps_t[:, 0:1]), wait=[tp])
    S.drain("scalar")
    tsc = S.tok("SC")
    ve(lambda e: e.tensor_scalar(out=sb_stt2[:, :], in0=sb_cum2[:, :],
                                 scalar1=sb_base2[:, 0:1], scalar2=-1.0,
                                 op0=OP.add, op1=OP.add), wait=[tsc])
    S.drain("vector")
    ve(lambda e: e.tensor_tensor(out=sb_stt2[:, :], in0=sb_stt2[:, :],
                                 in1=sb_mask2[:, :], op=OP.mult))
    S.drain("vector")
    tve = S.tok("VE")
    pe(lambda e: e.matmul(ps_t[0:16, 0:128], sb_sel2[:, :], sb_stt2[:, :],
                          start=True, stop=True), wait=[tve, S.tok("SC")])
    tp = S.tok("PE")
    sc(lambda e: e.copy(out=sb_sidxT16[0:16, 0:128], in_=ps_t[0:16, 0:128]),
       wait=[tp])
    S.drain("scalar")
    tsidxT = S.tok("SC")
    ROW16 = 2 * 128
    for cl in range(16):
        S.op("sync",
             (lambda d, s: (lambda e: e.dma_start(
                 out=d, in_=s, allow_slow_non_contiguous=True)))(
                 rap(sb_sidx, cl * ROW16, [[ROW16, 1], [8, 16], [1, 8]]),
                 rap(sb_sidxT16, cl, [[ROW16, 16], [16, 8]])),
             wait=[tsidxT, S.tok("Pm")] + ([S.tok("GD")] if cl == 0 else []),
             inc="Pm", k=16)
    tsidx = S.tok("Pm")
    if dbg:
        ve(lambda e: e.tensor_copy(out=sb_sq[0:16, 0:128], in_=sb_sidx[0:16, 0:128]),
           wait=[tsidx])
        S.drain("vector")
        tvd = S.tok("VE")
        sto(dbg_out["destT"], sb_sq[0:16, 0:128], wait=[tvd, S.tok("SG")],
            eng="sync", inc="SG")
    ve(lambda e: e.tensor_scalar(out=sb_et8[:, :], in0=sb_iTILE[:, :],
                                 scalar1=sb_eb8[:, 0:1], scalar2=None, op0=OP.is_ge),
       wait=[tsc8])
    S.drain("vector")
    tve = S.tok("VE")
    pe(lambda e: e.matmul(ps_t[0:1, 0:NTILE], sb_ones8[:, :], sb_et8[:, :],
                          start=True, stop=True), wait=[tve, S.tok("SC")])
    tp = S.tok("PE")
    sc(lambda e: e.add(out=sb_etil[:, :], in_=ps_t[0:1, 0:NTILE], add=-1.0),
       wait=[tp])
    S.drain("scalar")
    sc(lambda e: e.copy(out=sb_etil_i[:, :], in_=sb_etil[:, :]))
    S.drain("scalar")
    tet = S.tok("SC")
    if dbg:
        sto(dbg_out["etile"], sb_etil[:, :], wait=[tet, S.tok("SG")],
            eng="sync", inc="SG")
    for k in range(1, 8):
        dma(sb_sidx[16 * k:16 * (k + 1), :], sb_sidx[0:16, :],
            wait=[tsidx, S.tok("Rp")], eng="gpsimd", inc="Rp")
    tsrep = [S.tok("Rp")]

    S.op("gpsimd", lambda e: e.load_library(_LIB_MLP), wait=[S.tok("GP")])
    tzero = []
    zoff = 0
    zi = 0
    while zoff < NS:
        zrows = min(512, NS - zoff)
        tzero = [dma(
            xg_dram[zoff:zoff + zrows, :].rearrange("(a p) d -> p a d", p=128),
            zer_dram[:, 0:8 * zrows].rearrange("p (a d) -> p a d", d=DM),
            wait=tzf + [S.tok("ZG")], inc="ZG")]
        zoff += zrows
        zi += 1

    def gp_ssrc(cg, bb):
        def f(e):
            r = e.alloc_register()
            e.reg_load(r, sb_qm[0:1, 0:1])
            if cg or bb:
                e.reg_add(r, r, (512 * cg + 128 * bb) * DM)
            return e.dma_start(
                out=rap(D_ssrc, 4096 * cg + 1024 * bb,
                        [[20480, 128], [1, DM]]),
                in_=bass.AP(xn_dram.ap().tensor, r,
                            [[1, 128 * DM]]).rearrange("(p d) -> p d", d=DM))
        return f

    for cg in range(NGC4):
        for bb in range(4):
            S.op("gpsimd", gp_ssrc(cg, bb),
                 wait=(tB_st + [S.tok("Sxn"), S.tok("VE"), S.tok("SC"),
                                S.tok("CC")] if bb == 0 else []) +
                      [S.tok(f"Gs{cg}")],
                 inc=f"Gs{cg}", k=16)
    tssrc = [S.tok(f"Gs{cg}") for cg in range(NGC4)]
    for cg in range(NGC4):
        S.op("gpsimd", (lambda cg: (lambda e: e.dma_scatter_add(
            xg_dram.ap(),
            D_ssrc[:, 4096 * cg:4096 * (cg + 1)].rearrange("p (b d) -> p b d", d=DM),
            sb_sidx[:, 32 * cg:32 * (cg + 1)], 512, 512, DM)))(cg),
            wait=tssrc + tzero + tsrep, inc=f"Sc{cg}", k=16)
    tscat = [S.tok(f"Sc{cg}") for cg in range(NGC4)]

    for t in range(NTILE):
        tld = dma(D_xg[:, :], xg_dram[128 * t:128 * (t + 1), :],
                  wait=tscat + [S.tok("SC")] + ([S.tok("VE")] if t == 0 else []),
                  inc="Lxg")
        for kc in range(8):
            pe(lambda e, kc=kc: e.transpose(
                ps_x[:, 128 * kc:128 * (kc + 1)],
                D_xg[:, 128 * kc:128 * (kc + 1)], sb_id[:, :]),
               wait=[tld] + ([S.tok("SC")] if kc == 0 else []))
        tp = S.tok("PE")
        for kc in range(8):
            sc(lambda e, kc=kc, t=t: e.copy(
                out=xgf_blk(kc * NT128 + t * 128),
                in_=ps_x[:, 128 * kc:128 * (kc + 1)]), wait=[tp])
        S.drain("scalar")
    txgf = S.tok("SC")

    S.op("tensor", regs_load("tensor", 128), wait=[tet])

    w1v = w1_d.rearrange("e p c -> p e c")
    for mc in range(32):
        tw = []
        for kc in range(8):
            tw.append(dma(D_w1b[:, 1024 * kc:1024 * (kc + 1)]
                          .rearrange("p (e c) -> p e c", c=128),
                          w1v[:, :, 4096 * kc + 128 * mc:4096 * kc + 128 * (mc + 1)],
                          wait=([S.tok("PE")] +
                                ([S.tok("SC"), S.tok("VE"), S.tok("GP"),
                                  S.tok("SG")]
                                 if mc == 0 else [])) if kc == 0 else (),
                          inc=f"W{kc}"))
        tb1 = dma(sb_b1r, b1h_in[mc:mc + 1, :],
                  wait=[S.tok("PE")] +
                       ([S.tok("SC"), S.tok("VE"), S.tok("GP")] if mc == 0
                        else []), inc="B1")
        for t in range(NTILE):
            for kc in range(8):
                def mm1(t, kc):
                    def f(e):
                        rt = REGS[("tensor", "tmp")]
                        e.reg_add(rt, REGS[("tensor", t)],
                                  D_w1b.offset + 1024 * kc)
                        return e.matmul(
                            ps_f[:, 128 * t:128 * (t + 1)],
                            xgf_blk(kc * NT128 + t * 128),
                            bass.AP(D_w1b.tensor, rt, [[16384, 128], [1, 128]]),
                            start=(kc == 0), stop=False)
                    return f
                pe(mm1(t, kc),
                   wait=[tw[kc], txgf] + ([S.tok("SC")] if t == 0 and kc == 0 else []))
            def mb1(t):
                def f(e):
                    rt = REGS[("tensor", "tmp")]
                    e.reg_add(rt, REGS[("tensor", t)], B1R_OFF)
                    return e.matmul(
                        ps_f[:, 128 * t:128 * (t + 1)], sb_ones[0:1, 0:128],
                        bass.AP(ar1, rt, [[16384, 1], [1, 128]]),
                        start=False, stop=True)
                return f
            pe(mb1(t), wait=[tb1])
        for t in range(NTILE):
            tp = S.tok("PE")
            sc((lambda t: (lambda e: e.copy(
                out=D_t1b[:, 128 * (t % 2):128 * (t % 2 + 1)],
                in_=ps_f[:, 128 * t:128 * (t + 1)])))(t),
               wait=[tp] + ([S.tok("Sh")] if t in (0, 1) else []))
            S.drain("scalar")
            tev = S.tok("SC")
            pe((lambda t: (lambda e: e.transpose(
                ps_x[:, 128 * (t % 2):128 * (t % 2 + 1)],
                D_t1b[:, 128 * (t % 2):128 * (t % 2 + 1)], sb_id[:, :])))(t),
               wait=[tev])
            tpx = S.tok("PE")
            sc((lambda t: (lambda e: e.activation(
                out=D_hst[:, 128 * t:128 * (t + 1)],
                in_=ps_x[:, 128 * (t % 2):128 * (t % 2 + 1)],
                func=AF.Gelu)))(t), wait=[tpx, S.tok("Sh")])
        S.drain("scalar")
        tsc = S.tok("SC")
        sto(h_dram[mc, :, :], D_hst[:, :], wait=[tsc], eng="sync", inc="Sh")
    tS1 = S.tok("Sh")

    w2v = w2_d.rearrange("e p c -> p e c")
    nsup = (32 + SUP - 1) // SUP
    for sup in range(nsup):
        kcs = list(range(SUP * sup, min(SUP * (sup + 1), 32)))
        tld = []
        for i, kc2 in enumerate(kcs):
            tld.append(dma(D_h3[:, NT128 * i:NT128 * (i + 1)], h_dram[kc2, :, :],
                           wait=[tS1, S.tok("PE")] if i == 0 else [tS1],
                           inc=f"H{i}"))
        for mc2 in range(8):
            tw = []
            for i, kc2 in enumerate(kcs):
                tw.append(dma(D_w2b[:, 1024 * i:1024 * (i + 1)]
                              .rearrange("p (e c) -> p e c", c=128),
                              w2v[:, :, 1024 * kc2 + 128 * mc2:
                                  1024 * kc2 + 128 * (mc2 + 1)],
                              wait=[S.tok("PE")] if i == 0 else (), inc=f"V{i}"))
            if sup == 0:
                tb2 = dma(sb_b2r, b2h_in[mc2:mc2 + 1, :],
                          wait=[S.tok("PE")], inc="B2")
            for t in range(NTILE):
                if sup == 0:
                    def mb2(t):
                        def f(e):
                            return e.matmul(
                                ps_f[:, 128 * t:128 * (t + 1)],
                                sb_ones[0:1, 0:128],
                                bass.AP(sb_xdbl, REGS[("tensor", t)],
                                        [[4 * CH, 1], [1, 128]]),
                                start=True, stop=False)
                        return f
                    pe(mb2(t), wait=[tb2] +
                       ([S.tok("VE")] if t == 0 else []))
                for i in range(len(kcs)):
                    def mm2(t, i, last, st):
                        def f(e):
                            rt = REGS[("tensor", "tmp")]
                            e.reg_add(rt, REGS[("tensor", t)],
                                      D_w2b.offset + 1024 * i)
                            return e.matmul(
                                ps_f[:, 128 * t:128 * (t + 1)],
                                D_h3[:, NT128 * i + 128 * t:
                                     NT128 * i + 128 * (t + 1)],
                                bass.AP(D_w2b.tensor, rt,
                                        [[16384, 128], [1, 128]]),
                                start=st, stop=last)
                        return f
                    pe(mm2(t, i, i == len(kcs) - 1, sup != 0 and i == 0),
                       wait=[tw[i], tld[i]] +
                            ([S.tok("VE")] if sup != 0 and i == 0 and t == 0
                             else []))
            tp = S.tok("PE")
            if sup == 0:
                ve((lambda mc2: (lambda e: e.tensor_copy(
                    out=acc_blk(mc2 * NT128, NT128), in_=ps_f[:, 0:NT128])))(mc2),
                   wait=[tp])
            else:
                ve((lambda mc2: (lambda e: e.scalar_tensor_tensor(
                    out=acc_blk(mc2 * NT128, NT128), in0=ps_f[:, 0:NT128],
                    scalar=1.0, in1=acc_blk(mc2 * NT128, NT128),
                    op0=OP.mult, op1=OP.add)))(mc2), wait=[tp])
            S.drain("vector")
    tS2 = S.tok("VE")

    for t in range(NTILE):
        dst = so_dram[128 * t:128 * (t + 1), :]
        sto(dst[:, 0:768].rearrange("s (b d) -> s b d", d=128),
            bass.AP(ar2, 128 * t, [[20480, 128], [NT128, 6], [1, 128]]),
            wait=[tS2, S.tok("So")], eng="sync", inc="So")
        sto(dst[:, 768:1024].rearrange("s (b d) -> s b d", d=128),
            bass.AP(sb_winp, 2048 + 128 * t, [[8192, 128], [NT128, 2], [1, 128]]),
            wait=[tS2, S.tok("So")], eng="sync", inc="So")
    tSO = S.tok("So")

    for cg in range(NGC4):
        S.op("gpsimd", (lambda cg: (lambda e: e.dma_gather(
            D_ga[:, 4096 * cg:4096 * (cg + 1)].rearrange("p (b d) -> p b d", d=DM),
            so_dram.ap(), sb_sidx[:, 32 * cg:32 * (cg + 1)], 512, 512, DM,
            elem_step=DM)))(cg),
            wait=[tSO] + tsrep + [S.tok("VE")], inc=f"Gg{cg}", k=16)
    tga = [S.tok(f"Gg{cg}") for cg in range(NGC4)]
    dma(g_dram.ap(), D_ga[:, :].rearrange("p (b d) -> p b d", d=DM), wait=tga,
        eng="gpsimd", inc="Sgd")
    tgd = S.tok("Sgd")

    twout = dma(C_wout[:, :], w_out, wait=[tS2, S.tok("SC"), S.tok("VE"), tgd],
                inc="Wout")

    def gp_agl(j):
        def f(e):
            r = e.alloc_register()
            e.reg_load(r, sb_qm[0:1, 4:5])
            if j:
                e.reg_add(r, r, j * (4 * DQ) * CH)
            return e.dma_start(
                out=C_agl[:, :].rearrange("p (k t) -> p k t", t=CH),
                in_=bass.AP(ago_d.ap().tensor, r,
                            [[CH, 128], [128 * CH, 16], [1, CH]]))
        return f

    for j in range(NGC4):
        S.op("gpsimd", gp_agl(j), wait=[tC1, tgd, S.tok("PE"), S.tok("SC")],
             inc="Ga", k=16)
        tagl = S.tok("Ga")
        for m in range(8):
            bank = ps_a if m % 2 == 0 else ps_b
            for kc in range(16):
                pe((lambda m, kc, bank: (lambda e: e.matmul(
                    bank, C_wout[:, 1024 * kc + 128 * m:1024 * kc + 128 * (m + 1)],
                    C_agl[:, CH * kc:CH * (kc + 1)],
                    start=(kc == 0), stop=(kc == 15))))(m, kc, bank),
                   wait=[twout, tagl] + ([S.tok("SC")] if kc == 0 else []))
            tp = S.tok("PE")
            sc((lambda m, bank: (lambda e: e.mul(out=C_sfm[:, CH * m:CH * (m + 1)],
                                                 in_=bank, mul=0.5)))(m, bank),
               wait=[tp])
        S.drain("scalar")
        tsf = S.tok("SC")
        for tt in range(4):
            for kc in range(8):
                pe((lambda tt, kc: (lambda e: e.transpose(
                    ps_t[:, 128 * (kc % 4):128 * (kc % 4) + 128],
                    C_sfm[:, CH * kc + 128 * tt:CH * kc + 128 * (tt + 1)],
                    sb_id[:, :])))(tt, kc),
                   wait=[tsf] + ([S.tok("SC")] if kc % 4 == 0 else []))
                if kc % 4 == 3:
                    tp = S.tok("PE")
                    sc((lambda tt, kc: (lambda e: e.copy(
                        out=C_stm.rearrange("p (t d) -> p t d", d=DM)
                            [:, tt:tt + 1, 128 * (kc - 3):128 * (kc + 1)],
                        in_=ps_t[:, 0:512].unsqueeze(1))))(tt, kc),
                       wait=[tp] + ([S.tok("Ss")] if tt == 0 else []))
        S.drain("scalar")
        tstm = S.tok("SC")
        sto(s_dram[CH * j:CH * (j + 1), :].rearrange("(t p) d -> p t d", p=128),
            C_stm.rearrange("p (t d) -> p t d", d=DM), wait=[tstm], eng="sync",
            inc="Ss")
    tC2 = S.tok("Ss")

    def gp_eg(tt):
        def f(e):
            return e.dma_start(
                out=E_g[:, :],
                in_=bass.AP(g_dram.ap().tensor, tt * DM,
                            [[NOTL * DM, 128], [1, DM]]))
        return f

    def gp_ex(tt):
        def f(e):
            r = e.alloc_register()
            e.reg_load(r, sb_qm[0:1, 0:1])
            if tt:
                e.reg_add(r, r, tt * 128 * DM)
            return e.dma_start(out=E_x[:, :],
                               in_=bass.AP(x_in.tensor, r, [[DM, 128], [1, DM]]))
        return f

    for tt in range(NOTL):
        S.op("gpsimd", gp_eg(tt),
             wait=[tgd, tC2, S.tok("VE"), S.tok("SC")], inc="Eg", k=16)
        S.op("gpsimd", gp_ex(tt), wait=[S.tok("VE")], inc="Ex", k=16)
        tEg = S.tok("Eg")
        tEx = S.tok("Ex")
        t2 = dma(E_s[:, :], s_dram[128 * tt:128 * (tt + 1), :],
                 wait=[tC2, S.tok("VE")], inc="Ls")
        ve((lambda tt: (lambda e: e.tensor_scalar(
            out=E_t1[:, :], in0=E_g[:, :], scalar1=sb_score[:, tt:tt + 1],
            scalar2=None, op0=OP.mult)))(tt), wait=[tEg, tidx, S.tok("Sout")])
        S.drain("vector")
        tve = S.tok("VE")
        sc(lambda e: e.activation(out=E_g[:, :], in_=E_t1[:, :], func=AF.Silu),
           wait=[tve])
        S.drain("scalar")
        tsc = S.tok("SC")
        ve(lambda e: e.tensor_tensor(out=E_t1[:, :], in0=E_s[:, :], in1=E_g[:, :],
                                     op=OP.mult), wait=[tsc, t2])
        S.drain("vector")
        ve(lambda e: e.tensor_tensor(out=E_t1[:, :], in0=E_t1[:, :], in1=E_x[:, :],
                                     op=OP.add), wait=[tEx])
        S.drain("vector")
        tve = S.tok("VE")
        sto(out[128 * tt:128 * (tt + 1), :], E_t1[:, :], wait=[tve], eng="sync",
            inc="Sout")
    S.op("sync", lambda e: e.nop(), wait=S.all_toks())

    nc._dbg_out = dbg_out
    return _finish(nc, S, mybir)


def _finish(nc, S, _mybir):
    global _SEMS
    sems = {k: nc.alloc_semaphore(name="s" + k) for k in S.cnt}
    _SEMS = sems
    with nc.Block() as blk:

        @blk.sync
        def _(e):
            S.emit("sync", e, sems)

        @blk.tensor
        def _(e):
            S.emit("tensor", e, sems)

        @blk.vector
        def _(e):
            S.emit("vector", e, sems)

        @blk.scalar
        def _(e):
            S.emit("scalar", e, sems)

        @blk.gpsimd
        def _(e):
            S.emit("gpsimd", e, sems)

    _bass_rust.alloc_regs(nc)
    _bass_rust.fuse_regops(nc)
    _bass_rust.replace_nops_with_events(nc)
    _bass_rust.fuse_nops(nc.main_func)
    _bass_rust.remove_dead_nops(nc.main_func)
    _mybir.codegen_inst_isa_subclasses(nc)
    return nc


_SEMS = {}


def _prep_inputs(ins, L=8192):
    x = np.ascontiguousarray(np.asarray(ins["L0_attn_hidden_states"], np.float32))
    inproj = np.asarray(ins["L0_attn_in_proj_w"], np.float32)
    outproj = np.asarray(ins["L0_attn_out_proj_w"], np.float32)
    OWN = L // 4
    NS = OWN + EF * 128
    NTILE = NS // 128
    CAPIO = OWN // 128 + 1

    def lhsT_tile(w, K, M):
        t = np.ascontiguousarray(w.T)
        return np.ascontiguousarray(
            t.reshape(K // 128, 128, M // 128, 128).transpose(1, 0, 2, 3)
            .reshape(128, -1))

    w1 = np.asarray(ins["L0_ffn_w1"], np.float32)
    w2 = np.asarray(ins["L0_ffn_w2"], np.float32)
    w1h = np.stack([
        np.ascontiguousarray(w1[e].reshape(8, 128, 32, 128).transpose(1, 0, 2, 3)
                             .reshape(128, -1)) for e in range(EF)])
    w2h = np.stack([
        np.ascontiguousarray(w2[e].reshape(32, 128, 8, 128).transpose(1, 0, 2, 3)
                             .reshape(128, -1)) for e in range(EF)])
    b1 = np.asarray(ins["L0_ffn_b1"], np.float32)
    b2 = np.asarray(ins["L0_ffn_b2"], np.float32)
    b1h = np.ascontiguousarray(b1.reshape(8, 32, 128).transpose(1, 0, 2).reshape(32, 1024))
    b2h = np.ascontiguousarray(b2.reshape(8, 8, 128).transpose(1, 0, 2).reshape(8, 1024))

    sel16 = np.zeros((16, 128), np.float32)
    for k in range(16):
        for e in range(8):
            sel16[k, e * 16 + k] = 1.0
    sel2 = np.zeros((128, 16), np.float32)
    for e in range(8):
        for k in range(16):
            sel2[e * 16 + k, k] = 1.0
    iotaE = np.repeat(np.arange(8, dtype=np.float32), 16)[:, None]
    iotaPAD = np.broadcast_to(np.arange(CAPIO, dtype=np.float32) * 128, (8, CAPIO)).copy()
    iotaTILE = np.broadcast_to(np.arange(NTILE, dtype=np.float32) * 128, (8, NTILE)).copy()
    colk0 = np.ones((1, 128), np.float32)
    colk0[0, 0::16] = 0.0
    selb16 = np.zeros((16, 16 * 128), np.float32)
    for n in range(16):
        selb16[n, 128 * n:128 * (n + 1)] = 1.0
    iota8m = np.broadcast_to(np.arange(8, dtype=np.float32) - BIGV, (128, 8)).copy()

    def dslice(w, q, dirb):
        return w[DQ * q:DQ * (q + 1)] if not dirb else w

    maps = []
    for core in range(8):
        b, q = core // 4, core % 4
        xrows = inproj[DQ * q:DQ * (q + 1)]
        zrows = inproj[2048 + DQ * q:2048 + DQ * (q + 1)]
        winp = lhsT_tile(np.concatenate([xrows, zrows]), 1024, 1024)

        def dirpack(wf, wb, fn):
            return np.stack([fn(wf), fn(wb)])

        xw = np.asarray(ins["L0_attn_x_proj_w"], np.float32)
        xwb = np.asarray(ins["L0_attn_x_proj_w_b"], np.float32)
        def xp(w):
            t = np.ascontiguousarray(w[:, DQ * q:DQ * (q + 1)].T)
            return np.ascontiguousarray(
                t.reshape(4, 128, 96).transpose(1, 0, 2).reshape(128, 4 * 96))
        wxp = np.stack([xp(xw), xp(xwb)])

        dtw = np.asarray(ins["L0_attn_dt_proj_w"], np.float32)
        dtwb = np.asarray(ins["L0_attn_dt_proj_w_b"], np.float32)
        def dtp(w):
            return np.ascontiguousarray(w[DQ * q:DQ * (q + 1)].T)
        wdt = np.stack([dtp(dtw), dtp(dtwb)])

        def q4(v):
            return np.ascontiguousarray(
                v[DQ * q:DQ * (q + 1)].reshape(4, 128).T)

        def q4k(v, K):
            return np.ascontiguousarray(
                v[DQ * q:DQ * (q + 1)].reshape(4, 128, K).transpose(1, 0, 2)
                .reshape(128, 4 * K))

        wcv = np.stack([q4k(np.asarray(ins["L0_attn_conv_w"], np.float32), 4),
                        q4k(np.asarray(ins["L0_attn_conv_w_b"], np.float32), 4)])
        bcv = np.stack([q4(np.asarray(ins["L0_attn_conv_b"], np.float32)),
                        q4(np.asarray(ins["L0_attn_conv_b_b"], np.float32))])
        bdt = np.stack([q4(np.asarray(ins["L0_attn_dt_proj_b"], np.float32)),
                        q4(np.asarray(ins["L0_attn_dt_proj_b_b"], np.float32))])
        alog = np.stack([q4k(np.asarray(ins["L0_attn_A_log"], np.float32), NST),
                         q4k(np.asarray(ins["L0_attn_A_b_log"], np.float32), NST)])
        dvec = np.stack([q4(np.asarray(ins["L0_attn_D"], np.float32)),
                         q4(np.asarray(ins["L0_attn_D_b"], np.float32))])
        wout = lhsT_tile(outproj, 2048, 1024)
        _wgr = np.ascontiguousarray(np.asarray(ins["L0_ffn_wg_red"], np.float32).T)
        wgred = np.ascontiguousarray(
            _wgr.reshape(8, 128, 16).transpose(1, 0, 2).reshape(128, 128))
        agoff = (q * OWN // 512) * (2048 * 512) + (q * OWN % 512)
        qmeta = np.array([[q * OWN * DM, q * OWN, q * OWN, q * OWN * 16,
                           agoff, 0]],
                         np.int32)
        maps.append({
            "x_b": np.ascontiguousarray(x[b, :L]),
            "w_inproj": winp, "w_xproj": wxp, "w_dt": wdt, "w_conv": wcv,
            "b_conv": bcv, "b_dt": bdt, "a_log": alog, "d_vec": dvec,
            "w_out": wout, "w_gred": wgred,
            "wg_in": np.asarray(ins["L0_ffn_wg"], np.float32),
            "w1_d": w1h, "w2_d": w2h, "b1h": b1h, "b2h": b2h,
            "ident_in": np.eye(128, dtype=np.float32),
            "sel16_in": sel16, "sel2_in": sel2, "iotaE_in": iotaE,
            "iotaPAD_in": iotaPAD, "iotaTILE_in": iotaTILE,
            "colk0_in": colk0, "iota8m_in": iota8m, "qmeta_in": qmeta,
            "selb16_in": selb16,
        })
    return maps


def kernel(**inputs):
    L = int(np.asarray(inputs["L0_attn_hidden_states"]).shape[1])
    nc = build_nc(L=L)
    maps = _prep_inputs(inputs, L=L)
    res = run_bass_kernel_spmd(nc, maps, core_ids=list(range(8)))
    OWN = L // 4
    full = np.zeros((2, L, DM), np.float32)
    for core in range(8):
        b, q = core // 4, core % 4
        full[b, OWN * q:OWN * (q + 1), :] = res.results[core]["out"]
    return full


# version 04
# speedup vs baseline: 1.2798x; 1.2798x over previous
import numpy as np
import bass_rust as _bass_rust
import concourse.bass as bass
import concourse.mybir as mybir
from concourse.bass_utils import run_bass_kernel_spmd
from concourse.library_config import mlp as _LIB_MLP

F32 = mybir.dt.float32
I16 = mybir.dt.int16
I32 = mybir.dt.int32
AF = mybir.ActivationFunctionType
OP = mybir.AluOpType
AX = mybir.AxisListType.X

DM = 1024
DQ = 512
NDT = 4
NST = 16
CH = 512
EF = 8
BIGV = 100000.0
NG = 4
SUP = 4


class _BassD(bass.Bass):
    _defer_register_allocation = True


class Sched:
    def __init__(self):
        self.ops = []
        self.cnt = {}

    def op(self, eng, fn, wait=(), inc=None, k=1):
        self.ops.append((eng, fn, tuple(wait), inc, k))
        if inc:
            self.cnt[inc] = self.cnt.get(inc, 0) + k
            return (inc, self.cnt[inc])
        return None

    def drain(self, eng, inc=None):
        return self.op(eng, "drain", inc=inc, k=1)

    def tok(self, sem):
        return (sem, self.cnt.get(sem, 0))

    def all_toks(self):
        return [(s, v) for s, v in self.cnt.items()]

    def bump(self, sem, k):
        self.cnt[sem] = self.cnt.get(sem, 0) + k
        return (sem, self.cnt[sem])

    def emit(self, name, eng, sems):
        hw = {}
        for e, fn, waits, inc, k in self.ops:
            if e != name:
                continue
            for (s, v) in waits:
                if v > 0 and hw.get(s, 0) < v:
                    eng.wait_ge(sems[s], v)
                    hw[s] = v
            if fn == "drain":
                inst = eng.drain()
            else:
                inst = fn(eng)
            if inc is not None and inst is not None:
                inst.then_inc(sems[inc], k)


def rap(ap_or_handle, off, pattern):
    if isinstance(ap_or_handle, bass.AP):
        return bass.AP(ap_or_handle.tensor, ap_or_handle.offset + off, pattern)
    return bass.AP(ap_or_handle, off, pattern)


def build_nc(L=8192, dbg=False, mixer_only=False, ffn_only=False):
    NCH = L // CH
    OWN = L // 4
    NOTL = OWN // 128
    NS = OWN + EF * 128
    NTILE = NS // 128
    NT128 = NTILE * 128
    CAPIO = OWN // 128 + 1
    NGC4 = OWN // 512
    assert NOTL == 16 * (L // 8192) or L == 8192 or True

    nc = _BassD("TRN2", target_bir_lowering=False)

    def din(name, shape, dt=F32):
        return nc.dram_tensor(name, shape, dt, kind="ExternalInput").ap()

    x_in = din("x_b", [L, DM])
    w_inproj = din("w_inproj", [128, 8192])
    w_xproj = din("w_xproj", [2, 128, 4 * 96])
    w_dt = din("w_dt", [2, 64, DQ])
    w_conv = din("w_conv", [2, 128, NDT * 4])
    b_conv = din("b_conv", [2, 128, NDT])
    b_dt = din("b_dt", [2, 128, NDT])
    a_log = din("a_log", [2, 128, NDT * NST])
    d_vec = din("d_vec", [2, 128, NDT])
    w_out = din("w_out", [128, 16384])
    w_gred = din("w_gred", [128, 128])
    wg_in = din("wg_in", [8, 16])
    w1_d = din("w1_d", [EF, 128, 8 * 32 * 128])
    w2_d = din("w2_d", [EF, 128, 32 * 8 * 128])
    b1h_in = din("b1h", [32, 1024])
    b2h_in = din("b2h", [8, 1024])
    ident_in = din("ident_in", [128, 128])
    sel16_in = din("sel16_in", [16, 128])
    sel2_in = din("sel2_in", [128, 16])
    iotaE_in = din("iotaE_in", [128, 1])
    iotaPAD_in = din("iotaPAD_in", [8, CAPIO])
    iotaTILE_in = din("iotaTILE_in", [8, NTILE])
    colk0_in = din("colk0_in", [1, 128])
    iota8m_in = din("iota8m_in", [128, 8])
    selb16_in = din("selb16_in", [16, 16 * 128])
    qmeta_in = din("qmeta_in", [1, 6], I32)

    out = nc.dram_tensor("out", [OWN, DM], F32, kind="ExternalOutput").ap()
    dbg_out = {}
    if dbg:
        for k, shp in [("y", [NCH, 128, 2 * NDT * CH]), ("md", [NCH, 128, 2 * NDT * CH]),
                       ("xt", [NCH, 128, 2 * NDT * CH]), ("xdbl", [NCH, 192, CH]),
                       ("score", [128, NOTL]), ("idx", [128, NOTL]),
                       ("destT", [16, 128]), ("etile", [1, NTILE])]:
            dbg_out[k] = nc.dram_tensor("dbg_" + k, shp, F32, kind="ExternalOutput").ap()

    xn_dram = nc.dram_tensor("xn_dram", [L, DM], F32)
    uf_dram = nc.dram_tensor("uf_dram", [NCH, 128, NDT * CH], F32)
    ub_dram = nc.dram_tensor("ub_dram", [NCH, 128, NDT * CH], F32)
    ar_in = nc.dram_tensor("ar_in", [96, CH], F32)
    ar_out = nc.dram_tensor("ar_out", [96, CH], F32)
    xz_dram = nc.dram_tensor("xz_dram", [NCH, 128, 8 * CH], F32)
    agi_d = [nc.dram_tensor(f"agi{i}", [DQ, CH], F32) for i in range(2)]
    ago_d = nc.dram_tensor("ago", [NCH, 4 * DQ, CH], F32)
    xg_dram = nc.dram_tensor("xg_dram", [NS, DM], F32)
    h_dram = nc.dram_tensor("h_dram", [32, 128, NT128], F32)
    so_dram = nc.dram_tensor("so_dram", [NS, DM], F32)
    g_dram = nc.dram_tensor("g_dram", [128, NOTL, DM], F32)
    s_dram = nc.dram_tensor("s_dram", [OWN, DM], F32)
    xr_dram = nc.dram_tensor("xr_dram", [NCH, 16, CH], F32)

    groups = [[0, 1, 2, 3], [4, 5, 6, 7]]

    A = nc.alloc_sbuf_tensor
    ar1 = A("ar1", [128, 16384], F32)
    ar2 = A("ar2", [128, 20480], F32)
    sb_winp = A("sb_winp", [128, 8192], F32)
    sb_wxp = A("sb_wxp", [128, 2 * 4 * 96], F32)
    sb_wdt = A("sb_wdt", [64, 2 * DQ], F32)
    sb_wcv = A("sb_wcv", [128, 2 * NDT * 4], F32)
    sb_bcv = A("sb_bcv", [128, 2 * NDT], F32)
    sb_bdtn = A("sb_bdtn", [128, 2 * NDT], F32)
    sb_apos = A("sb_apos", [128, 2 * NDT * NST], F32)
    sb_dvec = A("sb_dvec", [128, 2 * NDT], F32)
    sb_wgr = A("sb_wgr", [128, 128], F32)
    sb_wg = A("sb_wg", [8, 16], F32)
    sb_wguT = A("sb_wguT", [16, 8], F32)
    sb_gn = A("sb_gn", [8, 2], F32)
    sb_id = A("sb_id", [128, 128], F32)
    sb_sel16 = A("sb_sel16", [16, 128], F32)
    sb_sel2 = A("sb_sel2", [128, 16], F32)
    sb_iotaE = A("sb_iotaE", [128, 1], F32)
    sb_iPAD = A("sb_iPAD", [8, CAPIO], F32)
    sb_iTILE = A("sb_iTILE", [8, NTILE], F32)
    sb_colk0 = A("sb_colk0", [1, 128], F32)
    sb_i8m = A("sb_i8m", [128, 8], F32)
    sb_ones = A("sb_ones", [1, 128], F32)
    sb_selb = A("sb_selb", [16, 16 * 128], F32)
    sb_ones8 = A("sb_ones8", [8, 1], F32)
    sb_zer = A("sb_zer", [128, 512], F32)
    sb_sq = A("sb_sq", [128, 512], F32)
    sb_hold = A("sb_hold", [128, 2 * NDT * 4], F32)
    sb_xdbl = A("sb_xdbl", [96, 4 * CH], F32)
    sb_hin = A("sb_hin", [128, 2 * NDT * NST], F32)
    sb_stat = A("sb_stat", [128, 24], F32)
    sb_score = A("sb_score", [128, NOTL], F32)
    sb_cnt2 = A("sb_cnt2", [128, 1], F32)
    sb_g8 = A("sb_g8", [8, 1], F32)
    sb_p8c = A("sb_p8c", [8, CAPIO], F32)
    sb_p8 = A("sb_p8", [8, 1], F32)
    sb_r8 = A("sb_r8", [1, 8], F32)
    sb_eb = A("sb_eb", [1, 8], F32)
    sb_eb8 = A("sb_eb8", [8, 1], F32)
    sb_base2 = A("sb_base2", [128, 1], F32)
    sb_qm = A("sb_qm", [1, 6], I32)
    sb_sidx = sb_apos[:, 0:(NOTL * 8) // 2].bitcast(I16)
    sb_sidxT16 = sb_apos[:, 64:64 + (NOTL * 8) // 2].bitcast(I16)
    sb_etil_i = sb_hin[0:1, 0:NTILE].bitcast(I32)
    sb_cst = A("sb_cst", [128, 2], F32)
    nc.const_aps.register(F32, 1e-5, sb_cst[:, 0:1])
    nc.const_aps.register(F32, -1.0, sb_cst[:, 1:2])

    ps = nc.alloc_psum_tensor("ps", [128, 4096], F32)
    ps_a = ps[:, 0:512]
    ps_b = ps[:, 512:1024]
    ps_t = ps[:, 1024:1536]
    ps_s = ps[:, 1536:2048]
    ps_c = ps[:, 2048:2560]
    ps_d = ps[:, 2560:3072]
    ps_f = ps[:, 0:NT128]
    ps_x = ps[:, 3072:4096]

    def v1(c, w):
        return ar1[:, c:c + w]

    def v2(c, w):
        return ar2[:, c:c + w]

    B_brep = v1(0, NG * CH)
    B_crep = v1(2048, NG * CH)
    B_dbu = v1(4096, NG * CH)
    B_h = v1(6144, NG * CH)
    B_hc = v1(8192, NG * CH)
    B_u1 = v1(8192, NDT * CH)
    B_da = v1(10240, NG * CH)
    B_tmp = v1(10240, NDT * CH)
    B_u2 = v1(10240, NDT * CH)
    B_md = v1(12288, NDT * CH)
    B_dxp = v1(14336, NDT * CH)
    B_xz = [v2(0, 8 * CH), v2(4096, 8 * CH)]
    B_xt = v2(8192, 2 * NDT * CH)
    B_xtm = v2(12288, 4 * DM)
    B_sz = v2(12288, 2 * NDT * CH)
    B_xnfm = v2(16384, 8 * CH)
    B_y = v2(16384, 2 * NDT * CH)

    C_wout = v1(0, 16384)
    C_agl = v2(0, 16 * CH)
    C_sfm = v2(8192, 8 * CH)
    C_stm = v2(12288, 4 * DM)
    C_ua = v2(16384, NDT * CH)
    C_ub = v2(18432, NDT * CH)

    D_ssrc = v2(0, NOTL * DM)
    D_w1b = v1(0, 8192)
    D_hst = v1(8192, NT128)
    D_xg = v1(8192 + NT128, DM)
    D_t1b = v1(8192 + NT128 + DM, 256)
    B1R_OFF = 8192 + NT128 + DM + 256
    sb_b1r = ar1[0:1, B1R_OFF:B1R_OFF + 1024]
    sb_b2r = sb_xdbl[0:1, 0:1024]
    XGF_LO = 20480
    D_h3 = v1(0, SUP * NT128)
    D_w2b = v1(SUP * NT128, SUP * 1024)
    D_otm = v1(0, 1024)
    D_ga = v2(0, NOTL * DM)
    sb_wgu = ar1[0:8, 0:16]
    sb_xext = v1(0, CH + 8)
    sb_bB = sb_xdbl[0:16, 2 * CH:3 * CH]
    sb_bC = sb_xdbl[0:16, 3 * CH:4 * CH]
    sb_xrc = sb_xdbl[0:16, CH:2 * CH]
    D_xro = ar1[0:16, 0:OWN]
    sb_idxT = ar1[0:16, 2048:2176]
    sb_mask2 = v1(2176, 128)
    sb_cum2 = v1(2304, 128)
    sb_crow = ar1[0:1, 2432:2560]
    sb_kin = ar1[0:1, 2560:2688]
    sb_kex = ar1[0:1, 2688:2816]
    sb_stt2 = v1(2816, 128)
    sb_logit = v1(2944, NOTL * 8)
    sb_mneg = v1(3072, NOTL)
    sb_mpos = v1(3072 + NOTL, NOTL)
    sb_ssum = v1(3072 + 2 * NOTL, NOTL)
    sb_idx = v1(3072 + 3 * NOTL, NOTL)
    sb_mtmp = v1(3072 + 4 * NOTL, 8)
    sb_et8 = ar1[0:8, 3208:3208 + NTILE]
    sb_etil = ar1[0:1, 3240:3240 + NTILE]
    E_g = v1(0, DM)
    E_x = v1(1024, DM)
    E_s = v1(2048, DM)
    E_t1 = v1(3072, DM)

    def _split(col, w, base, ovf):
        if col + w <= XGF_LO:
            return ar2[:, col:col + w]
        c = col - XGF_LO + base
        return ovf[:, c:c + w]

    xgf_blk = lambda col, w=128: _split(col, w, 0, sb_winp)
    acc_blk = lambda col, w=128: _split(col, w, 4096, sb_winp)

    S = Sched()

    def dma(dst, src, wait=(), eng="sync", inc="LD"):
        return S.op(eng, (lambda d, s: (lambda e: e.dma_start(out=d, in_=s)))(dst, src),
                    wait=wait, inc=inc, k=16)

    def sto(dst, src, wait=(), eng="sync", inc="ST"):
        return dma(dst, src, wait=wait, eng=eng, inc=inc)

    def gdma(dst, src, wait=()):
        return dma(dst, src, wait=wait, eng="gpsimd", inc="GD")

    def pe(fn, wait=()):
        return S.op("tensor", fn, wait=wait, inc="PE")

    def sc(fn, wait=()):
        return S.op("scalar", fn, wait=wait, inc="SC")

    def ve(fn, wait=()):
        return S.op("vector", fn, wait=wait, inc="VE")

    def gp(fn, wait=()):
        return S.op("gpsimd", fn, wait=wait, inc="GP")

    for d, s in [
        (sb_winp[:, :], w_inproj), (sb_wxp[:, :].rearrange("p (d c) -> p d c", d=2), w_xproj.rearrange("d p c -> p d c")),
        (sb_wdt[:, :].rearrange("p (d c) -> p d c", d=2), w_dt.rearrange("d p c -> p d c")),
        (sb_wcv[:, :].rearrange("p (d c) -> p d c", d=2), w_conv.rearrange("d p c -> p d c")),
        (sb_bcv[:, :].rearrange("p (d c) -> p d c", d=2), b_conv.rearrange("d p c -> p d c")),
        (sb_bdtn[:, :].rearrange("p (d c) -> p d c", d=2), b_dt.rearrange("d p c -> p d c")),
        (sb_apos[:, :].rearrange("p (d c) -> p d c", d=2), a_log.rearrange("d p c -> p d c")),
        (sb_dvec[:, :].rearrange("p (d c) -> p d c", d=2), d_vec.rearrange("d p c -> p d c")),
        (sb_wgr[:, :], w_gred), (sb_wg[:, :], wg_in),
        (sb_id[:, :], ident_in), (sb_sel16[:, :], sel16_in),
        (sb_sel2[:, :], sel2_in), (sb_iotaE[:, :], iotaE_in),
        (sb_iPAD[:, :], iotaPAD_in), (sb_iTILE[:, :], iotaTILE_in),
        (sb_colk0[:, :], colk0_in), (sb_i8m[:, :], iota8m_in),
        (sb_selb[:, :], selb16_in),
        (sb_qm[:, :], qmeta_in),
    ]:
        dma(d, s, wait=[S.tok("LA")], inc="LA")
    tA = [S.tok("LA")]
    zer_dram = nc.dram_tensor("zer_dram", [128, 4096], F32)

    ve(lambda e: e.memset(sb_cst[:, 0:1], 1e-5))
    ve(lambda e: e.memset(sb_cst[:, 1:2], -1.0))
    ve(lambda e: e.memset(sb_ones[:, :], 1.0))
    ve(lambda e: e.memset(sb_ones8[:, :], 1.0))
    ve(lambda e: e.memset(sb_zer[:, :], 0.0))
    S.drain("vector")
    tzm = S.tok("VE")
    for zk in range(8):
        dma(zer_dram[:, 512 * zk:512 * (zk + 1)], sb_zer[:, :],
            wait=[tzm, S.tok("Zf")], inc="Zf")
    tzf = [S.tok("Zf")]
    ve(lambda e: e.memset(sb_hin[:, :], 0.0))
    ve(lambda e: e.memset(sb_hold[:, :], 0.0))
    ve(lambda e: e.tensor_scalar(out=sb_bdtn[:, :], in0=sb_bdtn[:, :], scalar1=-1.0,
                                 scalar2=None, op0=OP.mult), wait=tA)
    sc(lambda e: e.activation(out=sb_apos[:, :], in_=sb_apos[:, :], func=AF.Exp),
       wait=tA)
    sc(lambda e: e.activation(out=sb_wgu[:, :], in_=sb_wg[:, :], func=AF.Square,
                              accum_out=sb_gn[:, 0:1]))
    S.drain("scalar")
    sc(lambda e: e.activation(out=sb_gn[:, 1:2], in_=sb_gn[:, 0:1], func=AF.Sqrt))
    S.drain("scalar")
    tA_sc = S.tok("SC")
    ve(lambda e: e.tensor_scalar_max(sb_gn[:, 1:2], sb_gn[:, 1:2], 1e-4), wait=[tA_sc])
    S.drain("vector")
    ve(lambda e: e.reciprocal(out=sb_gn[:, 0:1], in_=sb_gn[:, 1:2]))
    S.drain("vector")
    ve(lambda e: e.tensor_scalar(out=sb_wgu[:, :], in0=sb_wg[:, :],
                                 scalar1=sb_gn[:, 0:1], scalar2=None, op0=OP.mult))
    S.drain("vector")
    tAv = S.tok("VE")
    pe(lambda e: e.transpose(ps_t[0:16, 0:8], sb_wgu[:, :], sb_id[0:8, 0:8]),
       wait=[tAv])
    tp = S.tok("PE")
    sc(lambda e: e.copy(out=sb_wguT[:, :], in_=ps_t[0:16, 0:8]), wait=[tp])
    S.drain("scalar")
    tA_done = S.tok("SC")

    xz_tok = {}
    for pass_ in ((0, 1) if not ffn_only else ()):
        dir_ = pass_
        for s_ in range(NCH):
            c = s_ if pass_ == 0 else NCH - 1 - s_
            ci = 0
            if pass_ == 0:
                tld = dma(B_xtm.rearrange("p (t d) -> p t d", d=DM),
                          x_in[CH * c:CH * (c + 1), :].rearrange("(t p) d -> p t d", p=128),
                          wait=[S.tok("VE"), S.tok("SC"), S.tok("PE"),
                                S.tok("Sxn"), S.tok("GP")], inc="Lx")
                for tt in range(4):
                    ve(lambda e, tt=tt: e.reduce_sum(sb_stat[:, 2 * tt:2 * tt + 1],
                                                     B_xtm[:, DM * tt:DM * (tt + 1)], axis=AX),
                       wait=[tld, tAv])
                    sc(lambda e, tt=tt: e.activation(out=sb_sq[:, :],
                                                     in_=B_xtm[:, DM * tt:DM * tt + 512],
                                                     func=AF.Square,
                                                     accum_out=sb_stat[:, 2 * tt + 1:2 * tt + 2]),
                       wait=[tld, tA_done])
                    S.drain("scalar")
                    sc(lambda e, tt=tt: e.activation(out=sb_sq[:, :],
                                                     in_=B_xtm[:, DM * tt + 512:DM * (tt + 1)],
                                                     func=AF.Square,
                                                     accum_out=sb_stat[:, 16 + tt:17 + tt]),
                       wait=[tld])
                    S.drain("scalar")
                S.drain("vector")
                tve, tsc = S.tok("VE"), S.tok("SC")
                ve(lambda e: e.tensor_tensor(out=sb_stat[:, 1:8:2], in0=sb_stat[:, 1:8:2],
                                             in1=sb_stat[:, 16:20], op=OP.add),
                   wait=[tve, tsc])
                S.drain("vector")
                ve(lambda e: e.tensor_scalar(out=sb_stat[:, 8:12], in0=sb_stat[:, 0:8:2],
                                             scalar1=1.0 / DM, scalar2=None, op0=OP.mult))
                S.drain("vector")
                ve(lambda e: e.tensor_tensor(out=sb_stat[:, 12:16], in0=sb_stat[:, 8:12],
                                             in1=sb_stat[:, 8:12], op=OP.mult))
                ve(lambda e: e.tensor_scalar(out=sb_stat[:, 0:8:2], in0=sb_stat[:, 1:8:2],
                                             scalar1=1.0 / DM, scalar2=None, op0=OP.mult))
                S.drain("vector")
                ve(lambda e: e.tensor_tensor(out=sb_stat[:, 12:16], in0=sb_stat[:, 0:8:2],
                                             in1=sb_stat[:, 12:16], op=OP.subtract))
                S.drain("vector")
                tve = S.tok("VE")
                sc(lambda e: e.activation(out=sb_stat[:, 0:8:2], in_=sb_stat[:, 12:16],
                                          func=AF.Sqrt, bias=1e-5), wait=[tve])
                S.drain("scalar")
                tsc = S.tok("SC")
                ve(lambda e: e.reciprocal(out=sb_stat[:, 12:16], in_=sb_stat[:, 0:8:2]),
                   wait=[tsc])
                S.drain("vector")
                for tt in range(4):
                    ve(lambda e, tt=tt: e.tensor_scalar(
                        out=B_xtm[:, DM * tt:DM * (tt + 1)],
                        in0=B_xtm[:, DM * tt:DM * (tt + 1)],
                        scalar1=sb_stat[:, 8 + tt:9 + tt],
                        scalar2=sb_stat[:, 12 + tt:13 + tt],
                        op0=OP.subtract, op1=OP.mult))
                S.drain("vector")
                txn = S.tok("VE")
                sto(xn_dram[CH * c:CH * (c + 1), :].rearrange("(t p) d -> p t d", p=128),
                    B_xtm.rearrange("p (t d) -> p t d", d=DM), wait=[txn],
                    eng="sync", inc="Sxn")
                for tt in range(4):
                    for kc in range(8):
                        pe(lambda e, tt=tt, kc=kc: e.transpose(
                            ps_t[:, 128 * (kc % 4):128 * (kc % 4) + 128],
                            B_xtm[:, DM * tt + 128 * kc:DM * tt + 128 * (kc + 1)],
                            sb_id[:, :]),
                           wait=[txn] + ([S.tok("SC")] if kc % 4 == 0 else []))
                        if kc % 4 == 3:
                            tp = S.tok("PE")
                            extra2 = ([S.tok("VE")] +
                                      ([S.tok("SG")] if dbg else [])
                                      if tt == 0 and kc == 3 else [])
                            sc(lambda e, tt=tt, kc=kc: e.copy(
                                out=B_xnfm.rearrange("p (k t) -> p k t", t=CH)
                                    [:, kc - 3:kc + 1, 128 * tt:128 * (tt + 1)],
                                in_=ps_t[:, 0:512].rearrange("p (k t) -> p k t", t=128)),
                               wait=[tp] + extra2)
                S.drain("scalar")
                ttr = S.tok("SC")
                for m in range(8):
                    bank = ps_a if m % 2 == 0 else ps_b
                    for kc in range(8):
                        pe(lambda e, m=m, kc=kc, bank=bank: e.matmul(
                            bank,
                            sb_winp[:, 1024 * kc + 128 * m:1024 * kc + 128 * (m + 1)],
                            B_xnfm[:, CH * kc:CH * (kc + 1)],
                            start=(kc == 0), stop=(kc == 7)),
                           wait=[ttr] + tA + ([S.tok("SC")] if kc == 0 else []))
                    tp = S.tok("PE")
                    extra = ([S.tok("VE"), S.tok("GP"), S.tok("Sxz")]
                             if m == 0 else [])
                    sc(lambda e, m=m, bank=bank, ci=ci: e.copy(
                        out=B_xz[ci][:, CH * m:CH * (m + 1)], in_=bank),
                       wait=[tp] + extra)
                S.drain("scalar")
                t_xz = S.tok("SC")
                xz_tok[c] = sto(xz_dram[c, :, :], B_xz[ci][:, :], wait=[t_xz],
                                eng="sync", inc="Sxz")
                for kc in range(8):
                    pe(lambda e, kc=kc: e.matmul(
                        ps_s[0:16, :], sb_wgr[:, 16 * kc:16 * (kc + 1)],
                        B_xnfm[:, CH * kc:CH * (kc + 1)],
                        start=(kc == 0), stop=(kc == 7)),
                       wait=[ttr] + tA + ([S.tok("SC")] if kc == 0 else []))
                tp = S.tok("PE")
                sc(lambda e: e.copy(out=sb_xrc[:, :], in_=ps_s[0:16, :]),
                   wait=[tp, S.tok("Sxr")])
                S.drain("scalar")
                txr = S.tok("SC")
                sto(xr_dram[c, :, :], sb_xrc[:, :], wait=[txr], eng="sync",
                    inc="Sxr")
            else:
                tld = dma(B_xz[ci][:, :], xz_dram[c, :, :],
                          wait=[S.tok("VE"), S.tok("SC"), S.tok("GP"),
                                xz_tok[c], S.tok("Su")], inc="Lz")
                t_xz = tld

            xext4 = [v1(520 * dt, 520) for dt in range(NDT)]
            sq4 = [v1(2080 + 512 * dt, 512) for dt in range(NDT)]
            for dt in range(NDT):
                srcx = B_xz[ci][:, CH * dt:CH * (dt + 1)]
                if dir_ == 1:
                    srcx = srcx[:, ::-1]
                ve(lambda e, srcx=srcx, dt=dt: e.tensor_copy(
                    out=xext4[dt][:, 3:3 + CH], in_=srcx),
                   wait=[t_xz] if dt == 0 else [])
            for dt in range(NDT):
                ve(lambda e, dir_=dir_, dt=dt: e.tensor_copy(
                    out=xext4[dt][:, 0:3],
                    in_=sb_hold[:, 4 * (NDT * dir_ + dt):
                                4 * (NDT * dir_ + dt) + 3]))
            S.drain("vector")
            for dt in range(NDT):
                ve(lambda e, dir_=dir_, dt=dt: e.tensor_copy(
                    out=sb_hold[:, 4 * (NDT * dir_ + dt):
                                4 * (NDT * dir_ + dt) + 3],
                    in_=xext4[dt][:, CH:CH + 3]))
            for dt in range(NDT):
                ve(lambda e, dir_=dir_, dt=dt: e.tensor_scalar(
                    out=sq4[dt][:, :], in0=xext4[dt][:, 0:CH],
                    scalar1=sb_wcv[:, 4 * (NDT * dir_ + dt):
                                   4 * (NDT * dir_ + dt) + 1],
                    scalar2=None, op0=OP.mult))
            S.drain("vector")
            for k in range(1, 4):
                for dt in range(NDT):
                    ve(lambda e, k=k, dir_=dir_, dt=dt: e.scalar_tensor_tensor(
                        out=sq4[dt][:, :], in0=xext4[dt][:, k:k + CH],
                        scalar=sb_wcv[:, 4 * (NDT * dir_ + dt) + k:
                                      4 * (NDT * dir_ + dt) + k + 1],
                        in1=sq4[dt][:, :], op0=OP.mult, op1=OP.add))
                S.drain("vector")
            tve = S.tok("VE")
            for dt in range(NDT):
                sc(lambda e, dir_=dir_, dt=dt: e.activation(
                    out=B_xt[:, CH * dt:CH * (dt + 1)],
                    in_=sq4[dt][:, :], func=AF.Silu,
                    bias=sb_bcv[:, NDT * dir_ + dt:NDT * dir_ + dt + 1]),
                   wait=[tve] + ([S.tok("SG")] if dbg and dt == 0 else []))
            S.drain("scalar")
            t_xt = S.tok("SC")
            for kc in range(NDT):
                pe(lambda e, dir_=dir_, kc=kc: e.matmul(
                    ps_s[0:96, :],
                    sb_wxp[:, 96 * (4 * dir_ + kc):96 * (4 * dir_ + kc + 1)],
                    B_xt[:, CH * kc:CH * (kc + 1)],
                    start=(kc == 0), stop=(kc == 3)),
                   wait=[t_xt] + ([S.tok("SC")] if kc == 0 else []))
            tp = S.tok("PE")
            sc(lambda e: e.copy(out=sb_xdbl[:, 0:CH], in_=ps_s[0:96, :]),
               wait=[tp, S.tok("Lxd"), S.tok("LbB"), S.tok("LbC"), S.tok("CC")])
            S.drain("scalar")
            tsc = S.tok("SC")
            sto(ar_in.ap(), sb_xdbl[:, 0:CH], wait=[tsc, S.tok("CC")],
                eng="sync", inc="Sar")
            tst = S.tok("Sar")
            S.op("gpsimd", lambda e: e.collective_compute(
                "AllReduce", OP.add, replica_groups=groups,
                ins=[ar_in.ap().opt()], outs=[ar_out.ap().opt()]),
                wait=[tst, S.tok("Lxd"), S.tok("LbB"), S.tok("LbC")],
                inc="CC", k=1)
            tcc = S.tok("CC")
            txda = dma(sb_xdbl[:, 0:CH], ar_out.ap(),
                       wait=[tcc, S.tok("SC"), S.tok("PE")], inc="Lxd")
            tbB = dma(sb_bB[:, 0:CH], ar_out[64:80, :],
                      wait=[tcc, S.tok("PE")], inc="LbB")
            tbC = dma(sb_bC[:, 0:CH], ar_out[80:96, :], wait=[tcc], inc="LbC")
            txd = [txda, tbB, tbC]
            if dbg:
                sto(dbg_out["xdbl"][c, 96 * dir_:96 * (dir_ + 1), :],
                    ar_out.ap(), wait=[tcc, S.tok("SG")], eng="sync", inc="SG")

            for dt in range(NDT):
                pe(lambda e, dir_=dir_, dt=dt: e.matmul(
                    ps_s, sb_wdt[:, DQ * dir_ + 128 * dt:DQ * dir_ + 128 * (dt + 1)],
                    sb_xdbl[0:64, 0:CH],
                    start=True, stop=True),
                   wait=txd + [S.tok("SC")])
                tp = S.tok("PE")
                extra = ([S.tok("VE"), S.tok("GP")] +
                         ([S.tok("SG")] if dbg else []) if dt == 0 else [])
                sc(lambda e, dir_=dir_, dt=dt: e.activation(
                    out=B_tmp[:, CH * dt:CH * (dt + 1)], in_=ps_s,
                    func=AF.Sigmoid, scale=-1.0,
                    bias=sb_bdtn[:, NDT * dir_ + dt:NDT * dir_ + dt + 1]),
                   wait=[tp] + extra)
                S.drain("scalar")
            for dt in range(NDT):
                sc(lambda e, dt=dt: e.activation(
                    out=B_md[:, CH * dt:CH * (dt + 1)],
                    in_=B_tmp[:, CH * dt:CH * (dt + 1)], func=AF.Ln))
            S.drain("scalar")
            tmd = S.tok("SC")
            ve(lambda e, dir_=dir_: e.scalar_tensor_tensor(
                out=B_dxp, in0=B_md, scalar=-1.0,
                in1=B_xt[:, 0:NDT * CH],
                op0=OP.mult, op1=OP.mult), wait=[tmd, t_xt])
            S.drain("vector")
            tdxp = S.tok("VE")
            if dbg:
                sto(dbg_out["md"][c, :, CH * NDT * dir_:CH * NDT * (dir_ + 1)],
                    B_md[:, :], wait=[tmd, S.tok("SG")], eng="sync", inc="SG")
                sto(dbg_out["xt"][c, :, CH * NDT * dir_:CH * NDT * (dir_ + 1)],
                    B_xt[:, 0:NDT * CH], wait=[t_xt, S.tok("SG")], eng="sync",
                    inc="SG")
            tok_hc_prev = None
            for dt in range(NDT):
                for g in range(NST // NG):
                    for j in range(NG):
                        n = NG * g + j
                        for off, srcbc in ((0, sb_bB), (2048, sb_bC)):
                            pe(lambda e, n=n, srcbc=srcbc, off=off, j=j:
                               e.matmul(
                                   ps[:, off + 512 * j:off + 512 * j + 512],
                                   sb_selb[:, 128 * n:128 * (n + 1)],
                                   srcbc[:, 0:CH],
                                   start=True, stop=True),
                               wait=txd +
                                    ([tok_hc_prev] if tok_hc_prev is not None
                                     and j == 0 and off == 0 else []) +
                                    ([S.tok("SC"), S.tok("VE"), S.tok("GP")]
                                     if tok_hc_prev is None and j == 0 and
                                     off == 0 else []))
                    tp = S.tok("PE")
                    for j in range(NG):
                        n = NG * g + j
                        sc(lambda e, n=n, j=j, dt=dt, dir_=dir_: e.activation(
                            out=B_da[:, CH * j:CH * (j + 1)],
                            in_=B_md[:, CH * dt:CH * (dt + 1)], func=AF.Exp,
                            scale=sb_apos[:, NST * (NDT * dir_ + dt) + n:
                                          NST * (NDT * dir_ + dt) + n + 1]),
                           wait=[tmd] +
                                ([tok_hc_prev] if tok_hc_prev is not None and
                                 j == 0 else []))
                    S.drain("scalar")
                    tda = S.tok("SC")
                    ve(lambda e, dt=dt: e.tensor_tensor(
                        out=B_dbu.rearrange("p (j t) -> p j t", t=CH),
                        in0=B_dxp[:, CH * dt:CH * (dt + 1)].unsqueeze(1)
                            .broadcast_to([128, NG, CH]),
                        in1=ps[:, 0:2048].rearrange("p (j t) -> p j t", t=CH),
                        op=OP.mult), wait=[tp, tdxp])
                    S.drain("vector")
                    for j in range(NG):
                        n = NG * g + j
                        col = NST * (NDT * dir_ + dt) + n
                        ve(lambda e, j=j, col=col: e.tensor_tensor_scan(
                            out=B_h[:, CH * j:CH * (j + 1)],
                            data0=B_da[:, CH * j:CH * (j + 1)],
                            data1=B_dbu[:, CH * j:CH * (j + 1)],
                            initial=sb_hin[:, col:col + 1],
                            op0=OP.mult, op1=OP.add), wait=[tda])
                    S.drain("vector")
                    for j in range(NG):
                        n = NG * g + j
                        col = NST * (NDT * dir_ + dt) + n
                        ve(lambda e, j=j, col=col: e.tensor_copy(
                            out=sb_hin[:, col:col + 1],
                            in_=B_h[:, CH * j + CH - 1:CH * (j + 1)]))
                    ve(lambda e: e.tensor_tensor(
                        out=B_hc, in0=B_h,
                        in1=ps[:, 2048:4096].rearrange("p (j t) -> p j t", t=CH),
                        op=OP.mult),
                       wait=([S.tok("Su")] if dt == 0 and g == 0 else []))
                    S.drain("vector")
                    tok_hc_prev = S.tok("VE")
                    thc = tok_hc_prev
                    if g == 0:
                        ve(lambda e, dt=dt: e.tensor_reduce(
                            out=B_y[:, CH * dt:CH * (dt + 1)],
                            in_=rap(B_hc, 0, [[16384, 128], [1, CH], [CH, NG]]),
                            axis=AX, op=OP.add),
                           wait=[thc, S.tok("GP")] +
                                ([S.tok("SG")] if dbg else []))
                        S.drain("vector")
                    else:
                        ve(lambda e: e.tensor_reduce(
                            out=sb_sq[:, 0:CH],
                            in_=rap(B_hc, 0, [[16384, 128], [1, CH], [CH, NG]]),
                            axis=AX, op=OP.add), wait=[thc, S.tok("GP")])
                        S.drain("vector")
                        tvr = S.tok("VE")
                        gp(lambda e, dt=dt: e.tensor_tensor(
                            out=B_y[:, CH * dt:CH * (dt + 1)],
                            in0=B_y[:, CH * dt:CH * (dt + 1)],
                            in1=sb_sq[:, 0:CH], op=OP.add), wait=[tvr])
                        S.drain("gpsimd")
                tydt = S.tok("GP")
                ve(lambda e, dt=dt, dir_=dir_: e.scalar_tensor_tensor(
                    out=B_y[:, CH * dt:CH * (dt + 1)],
                    in0=B_xt[:, CH * dt:CH * (dt + 1)],
                    scalar=sb_dvec[:, NDT * dir_ + dt:NDT * dir_ + dt + 1],
                    in1=B_y[:, CH * dt:CH * (dt + 1)],
                    op0=OP.mult, op1=OP.add), wait=[tydt])
                S.drain("vector")
            tyy = S.tok("VE")
            if dbg:
                sto(dbg_out["y"][c, :, CH * NDT * dir_:CH * NDT * (dir_ + 1)],
                    B_y[:, 0:NDT * CH], wait=[tyy, S.tok("SG")], eng="sync",
                    inc="SG")
            sc(lambda e, ci=ci: e.activation(
                out=B_sz[:, 0:NDT * CH],
                in_=B_xz[ci][:, NDT * CH:2 * NDT * CH], func=AF.Silu),
               wait=[t_xz, tyy, S.tok("Sxn")])
            S.drain("scalar")
            tsz = S.tok("SC")
            if dir_ == 0:
                ve(lambda e: e.tensor_tensor(out=B_u1, in0=B_y[:, 0:NDT * CH],
                                             in1=B_sz[:, 0:NDT * CH], op=OP.mult),
                   wait=[tsz, tyy, S.tok("GP"), S.tok("Su")])
            else:
                ve(lambda e: e.tensor_tensor(
                    out=B_u1, in0=B_y[:, 0:NDT * CH],
                    in1=rap(B_sz, CH - 1, [[20480, 128], [CH, NDT], [-1, CH]]),
                    op=OP.mult), wait=[tsz, tyy, S.tok("GP"), S.tok("Su")])
            S.drain("vector")
            tu = S.tok("VE")
            if pass_ == 0:
                sto(uf_dram[c, :, :], B_u1[:, :], wait=[tu], eng="sync", inc="Su")
            else:
                sto(ub_dram[s_, :, :], B_u1[:, :], wait=[tu], eng="sync",
                    inc="Su")

    tB_st = [S.tok("Su")]

    for c in (range(NCH) if not ffn_only else ()):
        buf = c % 2
        t1 = dma(C_ua[:, :], uf_dram[c, :, :],
                 wait=tB_st + [S.tok("VE"), S.tok("Sag")] +
                      ([S.tok("SG")] if dbg else []),
                 inc="Luf")
        t2 = dma(C_ub[:, :], ub_dram[NCH - 1 - c, :, :], wait=tB_st, inc="Lub")
        ve(lambda e: e.tensor_tensor(
            out=C_ua[:, :], in0=C_ua[:, :],
            in1=rap(C_ub, CH - 1, [[20480, 128], [CH, NDT], [-1, CH]]),
            op=OP.add), wait=[t1, t2])
        S.drain("vector")
        tcmb = S.tok("VE")
        tst = sto(agi_d[buf].ap().rearrange("(d p) t -> p d t", p=128),
                  C_ua[:, :].rearrange("p (d t) -> p d t", t=CH),
                  wait=[tcmb, S.tok("CC")], eng="sync", inc="Sag")
        S.op("gpsimd", (lambda b, c: (lambda e: e.collective_compute(
            "AllGather", OP.bypass, replica_groups=groups,
            ins=[agi_d[b].ap().opt()],
            outs=[ago_d[c, :, :].opt()])))(buf, c),
            wait=[tst], inc="CC", k=1)
    tC1 = S.tok("CC")

    if mixer_only:
        CHJ = min(CH, OWN)
        TJ = CHJ // 128
        twout = dma(C_wout[:, :], w_out,
                    wait=[S.tok("SC"), S.tok("VE"), S.tok("PE")], inc="Wout")

        def gp_agl_m(j):
            def f(e):
                r = e.alloc_register()
                e.reg_load(r, sb_qm[0:1, 4:5])
                if j:
                    e.reg_add(r, r, j * (4 * DQ) * CH)
                return e.dma_start(
                    out=C_agl[:, :].rearrange("p (k t) -> p k t", t=CHJ)
                        [:, 0:16, :],
                    in_=bass.AP(ago_d.ap().tensor, r,
                                [[CH, 128], [128 * CH, 16], [1, CHJ]]))
            return f

        for j in range(OWN // CHJ):
            S.op("gpsimd", gp_agl_m(j), wait=[tC1, S.tok("PE"), S.tok("SC")],
                 inc="Ga", k=16)
            tagl = S.tok("Ga")
            for m in range(8):
                bank = ps_a if m % 2 == 0 else ps_b
                for kc in range(16):
                    pe((lambda m, kc, bank: (lambda e: e.matmul(
                        bank[:, 0:CHJ],
                        C_wout[:, 1024 * kc + 128 * m:1024 * kc + 128 * (m + 1)],
                        C_agl[:, CHJ * kc:CHJ * (kc + 1)],
                        start=(kc == 0), stop=(kc == 15))))(m, kc, bank),
                       wait=[twout, tagl] + ([S.tok("SC")] if kc == 0 else []))
                tp = S.tok("PE")
                sc((lambda m, bank: (lambda e: e.mul(
                    out=C_sfm[:, CHJ * m:CHJ * (m + 1)],
                    in_=bank[:, 0:CHJ], mul=0.5)))(m, bank),
                   wait=[tp])
            S.drain("scalar")
            tsf = S.tok("SC")
            for tt in range(TJ):
                for kc in range(8):
                    pe((lambda tt, kc: (lambda e: e.transpose(
                        ps_t[:, 128 * (kc % 4):128 * (kc % 4) + 128],
                        C_sfm[:, CHJ * kc + 128 * tt:CHJ * kc + 128 * (tt + 1)],
                        sb_id[:, :])))(tt, kc),
                       wait=[tsf] + ([S.tok("SC")] if kc % 4 == 0 else []))
                    if kc % 4 == 3:
                        tp = S.tok("PE")
                        sc((lambda tt, kc: (lambda e: e.copy(
                            out=C_stm.rearrange("p (t d) -> p t d", d=DM)
                                [:, tt:tt + 1, 128 * (kc - 3):128 * (kc + 1)],
                            in_=ps_t[:, 0:512].unsqueeze(1))))(tt, kc),
                           wait=[tp] + ([S.tok("SoM")] if tt == 0 else []))
            S.drain("scalar")
            tstm = S.tok("SC")
            sto(out[CHJ * j:CHJ * (j + 1), :].rearrange("(t p) d -> p t d", p=128),
                C_stm.rearrange("p (t d) -> p t d", d=DM)[:, 0:TJ, :],
                wait=[tstm], eng="sync", inc="SoM")
        S.op("sync", lambda e: e.nop(), wait=S.all_toks())
        nc._dbg_out = dbg_out
        return _finish(nc, S, mybir)

    REGS = {}

    def regs_load(ename, mult):
        def f(e):
            for t in range(NTILE):
                r = e.alloc_register()
                e.reg_load(r, sb_etil_i[0:1, t:t + 1])
                if mult != 1:
                    e.reg_mul(r, r, mult)
                REGS[(ename, t)] = r
            rtmp = e.alloc_register()
            REGS[(ename, "tmp")] = rtmp
            return None
        return f

    def gp_xro(e):
        r = e.alloc_register()
        e.reg_load(r, sb_qm[0:1, 3:4])
        return e.dma_start(
            out=D_xro.rearrange("p (c t) -> p c t", t=CH),
            in_=bass.AP(xr_dram.ap().tensor, r,
                        [[CH, 16], [16 * CH, OWN // CH], [1, CH]]))

    S.op("gpsimd", gp_xro,
         wait=tA + tB_st + [S.tok("Sxr"), S.tok("SC"), S.tok("VE"), S.tok("PE")],
         inc="Gx", k=16)
    txro = S.tok("Gx")

    for tt in range(NOTL):
        pe(lambda e, tt=tt: e.matmul(
            ps_s[:, 8 * tt:8 * (tt + 1)],
            D_xro[:, 128 * tt:128 * (tt + 1)], sb_wguT[:, :],
            start=True, stop=True),
           wait=[txro, tA_done] + ([S.tok("SC")] if tt == 0 else []))
    tp = S.tok("PE")
    sc(lambda e: e.copy(out=sb_logit[:, :], in_=ps_s[:, 0:8 * NOTL]), wait=[tp])
    S.drain("scalar")
    tlg = S.tok("SC")
    for tt in range(NOTL):
        ve(lambda e, tt=tt: e.reduce_max(sb_mneg[:, tt:tt + 1],
                                         sb_logit[:, 8 * tt:8 * (tt + 1)],
                                         axis=AX, negate=True), wait=[tlg])
    S.drain("vector")
    tve = S.tok("VE")
    for tt in range(NOTL):
        sc(lambda e, tt=tt: e.activation(
            out=sb_sq[:, 0:8], in_=sb_logit[:, 8 * tt:8 * (tt + 1)], func=AF.Exp,
            bias=sb_mneg[:, tt:tt + 1], accum_out=sb_ssum[:, tt:tt + 1]),
           wait=[tve])
        S.drain("scalar")
    tsc = S.tok("SC")
    ve(lambda e: e.reciprocal(out=sb_score[:, :], in_=sb_ssum[:, :]), wait=[tsc])
    ve(lambda e: e.tensor_scalar(out=sb_mpos[:, :], in0=sb_mneg[:, :], scalar1=-1.0,
                                 scalar2=None, op0=OP.mult))
    S.drain("vector")
    for tt in range(NOTL):
        ve(lambda e, tt=tt: e.tensor_scalar(
            out=sb_mtmp[:, :], in0=sb_logit[:, 8 * tt:8 * (tt + 1)],
            scalar1=sb_mpos[:, tt:tt + 1], scalar2=None, op0=OP.is_equal))
        S.drain("vector")
        ve(lambda e: e.tensor_tensor(out=sb_mtmp[:, :], in0=sb_mtmp[:, :],
                                     in1=sb_i8m[:, :], op=OP.mult))
        S.drain("vector")
        ve(lambda e, tt=tt: e.tensor_reduce(out=sb_idx[:, tt:tt + 1],
                                            in_=sb_mtmp[:, :], axis=AX, op=OP.min))
        S.drain("vector")
    ve(lambda e: e.tensor_scalar(out=sb_idx[:, :], in0=sb_idx[:, :], scalar1=BIGV,
                                 scalar2=None, op0=OP.add))
    S.drain("vector")
    tidx = S.tok("VE")
    if dbg:
        sto(dbg_out["score"], sb_score[:, :], wait=[tidx, S.tok("SG")],
            eng="sync", inc="SG")
        sto(dbg_out["idx"], sb_idx[:, :], wait=[tidx, S.tok("SG")], eng="sync",
            inc="SG")

    assert NOTL == 16
    pe(lambda e: e.transpose(ps_t[0:16, 0:128], sb_idx[:, :], sb_id[:, :]),
       wait=[tidx, S.tok("SC")])
    tp = S.tok("PE")
    sc(lambda e: e.copy(out=sb_idxT[:, :], in_=ps_t[0:16, 0:128]), wait=[tp])
    S.drain("scalar")
    tsc = S.tok("SC")
    pe(lambda e: e.matmul(ps_t[:, 0:128], sb_sel16[:, :], sb_idxT[:, :],
                          start=True, stop=True), wait=[tsc] + tA)
    tp = S.tok("PE")
    ve(lambda e: e.tensor_scalar(out=sb_mask2[:, :], in0=ps_t[:, 0:128],
                                 scalar1=sb_iotaE[:, 0:1], scalar2=None,
                                 op0=OP.is_equal), wait=[tp])
    S.drain("vector")
    ve(lambda e: e.tensor_tensor_scan(out=sb_cum2[:, :], data0=sb_mask2[:, :],
                                      data1=sb_zer[:, 0:128], initial=0.0,
                                      op0=OP.add, op1=OP.add))
    S.drain("vector")
    ve(lambda e: e.tensor_copy(out=sb_cnt2[:, :], in_=sb_cum2[:, 127:128]))
    S.drain("vector")
    tve = S.tok("VE")
    pe(lambda e: e.transpose(ps_t[0:1, 0:128], sb_cnt2[:, :], sb_id[:, :]),
       wait=[tve, S.tok("SC")])
    tp = S.tok("PE")
    sc(lambda e: e.copy(out=sb_crow[:, :], in_=ps_t[0:1, 0:128]), wait=[tp])
    S.drain("scalar")
    tsc = S.tok("SC")
    ve(lambda e: e.tensor_tensor_scan(out=sb_kin[:, :], data0=sb_colk0[:, :],
                                      data1=sb_crow[:, :], initial=0.0,
                                      op0=OP.mult, op1=OP.add), wait=[tsc])
    S.drain("vector")
    ve(lambda e: e.tensor_tensor(out=sb_kex[:, :], in0=sb_kin[:, :],
                                 in1=sb_crow[:, :], op=OP.subtract))
    ve(lambda e: e.tensor_copy(out=sb_r8[:, :],
                               in_=rap(sb_kin, 15, [[16384, 1], [16, 8]])))
    S.drain("vector")
    tve = S.tok("VE")
    pe(lambda e: e.transpose(ps_t[0:8, 0:1], sb_r8[:, :], sb_id[0:1, 0:1]),
       wait=[tve, S.tok("SC")])
    tp = S.tok("PE")
    sc(lambda e: e.copy(out=sb_g8[:, :], in_=ps_t[0:8, 0:1]), wait=[tp])
    S.drain("scalar")
    tsc = S.tok("SC")
    ve(lambda e: e.tensor_scalar(out=sb_p8c[:, :], in0=sb_iPAD[:, :],
                                 scalar1=sb_g8[:, 0:1], scalar2=None, op0=OP.is_lt),
       wait=[tsc])
    S.drain("vector")
    ve(lambda e: e.reduce_sum(sb_p8[:, :], sb_p8c[:, :], axis=AX))
    S.drain("vector")
    ve(lambda e: e.tensor_scalar(out=sb_p8[:, :], in0=sb_p8[:, :], scalar1=128.0,
                                 scalar2=None, op0=OP.mult))
    S.drain("vector")
    tve = S.tok("VE")
    pe(lambda e: e.transpose(ps_t[0:1, 0:8], sb_p8[:, :], sb_id[0:8, 0:8]),
       wait=[tve, S.tok("SC")])
    tp = S.tok("PE")
    sc(lambda e: e.copy(out=sb_r8[:, :], in_=ps_t[0:1, 0:8]), wait=[tp])
    S.drain("scalar")
    tsc = S.tok("SC")
    ve(lambda e: e.tensor_tensor_scan(out=sb_eb[:, :], data0=sb_ones[0:1, 0:8],
                                      data1=sb_r8[:, :], initial=0.0,
                                      op0=OP.mult, op1=OP.add), wait=[tsc])
    S.drain("vector")
    ve(lambda e: e.tensor_tensor(out=sb_eb[:, :], in0=sb_eb[:, :], in1=sb_r8[:, :],
                                 op=OP.subtract))
    S.drain("vector")
    tve = S.tok("VE")
    pe(lambda e: e.transpose(ps_t[0:8, 0:1], sb_eb[:, :], sb_id[0:1, 0:1]),
       wait=[tve, S.tok("SC")])
    tp = S.tok("PE")
    sc(lambda e: e.copy(out=sb_eb8[:, :], in_=ps_t[0:8, 0:1]), wait=[tp])
    S.drain("scalar")
    tsc8 = S.tok("SC")
    ve(lambda e: e.tensor_tensor(
        out=sb_kin[:, :], in0=sb_kex[:, :],
        in1=rap(sb_eb, 0, [[8, 1], [1, 8], [0, 16]]),
        op=OP.add), wait=[tve])
    S.drain("vector")
    tve = S.tok("VE")
    pe(lambda e: e.transpose(ps_t[:, 0:1], sb_kin[:, :], sb_id[0:1, 0:1]),
       wait=[tve, S.tok("SC")])
    tp = S.tok("PE")
    sc(lambda e: e.copy(out=sb_base2[:, :], in_=ps_t[:, 0:1]), wait=[tp])
    S.drain("scalar")
    tsc = S.tok("SC")
    ve(lambda e: e.tensor_scalar(out=sb_stt2[:, :], in0=sb_cum2[:, :],
                                 scalar1=sb_base2[:, 0:1], scalar2=-1.0,
                                 op0=OP.add, op1=OP.add), wait=[tsc])
    S.drain("vector")
    ve(lambda e: e.tensor_tensor(out=sb_stt2[:, :], in0=sb_stt2[:, :],
                                 in1=sb_mask2[:, :], op=OP.mult))
    S.drain("vector")
    tve = S.tok("VE")
    pe(lambda e: e.matmul(ps_t[0:16, 0:128], sb_sel2[:, :], sb_stt2[:, :],
                          start=True, stop=True), wait=[tve, S.tok("SC")])
    tp = S.tok("PE")
    sc(lambda e: e.copy(out=sb_sidxT16[0:16, 0:128], in_=ps_t[0:16, 0:128]),
       wait=[tp])
    S.drain("scalar")
    tsidxT = S.tok("SC")
    ROW16 = 2 * 128
    for cl in range(16):
        S.op("sync",
             (lambda d, s: (lambda e: e.dma_start(
                 out=d, in_=s, allow_slow_non_contiguous=True)))(
                 rap(sb_sidx, cl * ROW16, [[ROW16, 1], [8, 16], [1, 8]]),
                 rap(sb_sidxT16, cl, [[ROW16, 16], [16, 8]])),
             wait=[tsidxT, S.tok("Pm")] + ([S.tok("GD")] if cl == 0 else []),
             inc="Pm", k=16)
    tsidx = S.tok("Pm")
    if dbg:
        ve(lambda e: e.tensor_copy(out=sb_sq[0:16, 0:128], in_=sb_sidx[0:16, 0:128]),
           wait=[tsidx])
        S.drain("vector")
        tvd = S.tok("VE")
        sto(dbg_out["destT"], sb_sq[0:16, 0:128], wait=[tvd, S.tok("SG")],
            eng="sync", inc="SG")
    ve(lambda e: e.tensor_scalar(out=sb_et8[:, :], in0=sb_iTILE[:, :],
                                 scalar1=sb_eb8[:, 0:1], scalar2=None, op0=OP.is_ge),
       wait=[tsc8])
    S.drain("vector")
    tve = S.tok("VE")
    pe(lambda e: e.matmul(ps_t[0:1, 0:NTILE], sb_ones8[:, :], sb_et8[:, :],
                          start=True, stop=True), wait=[tve, S.tok("SC")])
    tp = S.tok("PE")
    sc(lambda e: e.add(out=sb_etil[:, :], in_=ps_t[0:1, 0:NTILE], add=-1.0),
       wait=[tp])
    S.drain("scalar")
    sc(lambda e: e.copy(out=sb_etil_i[:, :], in_=sb_etil[:, :]))
    S.drain("scalar")
    tet = S.tok("SC")
    if dbg:
        sto(dbg_out["etile"], sb_etil[:, :], wait=[tet, S.tok("SG")],
            eng="sync", inc="SG")
    for k in range(1, 8):
        dma(sb_sidx[16 * k:16 * (k + 1), :], sb_sidx[0:16, :],
            wait=[tsidx, S.tok("Rp")], eng="gpsimd", inc="Rp")
    tsrep = [S.tok("Rp")]

    S.op("gpsimd", lambda e: e.load_library(_LIB_MLP), wait=[S.tok("GP")])
    tzero = []
    zoff = 0
    zi = 0
    while zoff < NS:
        zrows = min(512, NS - zoff)
        tzero = [dma(
            xg_dram[zoff:zoff + zrows, :].rearrange("(a p) d -> p a d", p=128),
            zer_dram[:, 0:8 * zrows].rearrange("p (a d) -> p a d", d=DM),
            wait=tzf + [S.tok("ZG")], inc="ZG")]
        zoff += zrows
        zi += 1

    def gp_ssrc(cg, bb):
        def f(e):
            r = e.alloc_register()
            e.reg_load(r, sb_qm[0:1, 0:1])
            if cg or bb:
                e.reg_add(r, r, (512 * cg + 128 * bb) * DM)
            return e.dma_start(
                out=rap(D_ssrc, 4096 * cg + 1024 * bb,
                        [[20480, 128], [1, DM]]),
                in_=bass.AP(xn_dram.ap().tensor, r,
                            [[1, 128 * DM]]).rearrange("(p d) -> p d", d=DM))
        return f

    for cg in range(NGC4):
        for bb in range(4):
            S.op("gpsimd", gp_ssrc(cg, bb),
                 wait=(tB_st + [S.tok("Sxn"), S.tok("VE"), S.tok("SC"),
                                S.tok("CC")] if bb == 0 else []) +
                      [S.tok(f"Gs{cg}")],
                 inc=f"Gs{cg}", k=16)
    tssrc = [S.tok(f"Gs{cg}") for cg in range(NGC4)]
    for cg in range(NGC4):
        S.op("gpsimd", (lambda cg: (lambda e: e.dma_scatter_add(
            xg_dram.ap(),
            D_ssrc[:, 4096 * cg:4096 * (cg + 1)].rearrange("p (b d) -> p b d", d=DM),
            sb_sidx[:, 32 * cg:32 * (cg + 1)], 512, 512, DM)))(cg),
            wait=tssrc + tzero + tsrep, inc=f"Sc{cg}", k=16)
    tscat = [S.tok(f"Sc{cg}") for cg in range(NGC4)]

    for t in range(NTILE):
        tld = dma(D_xg[:, :], xg_dram[128 * t:128 * (t + 1), :],
                  wait=tscat + [S.tok("SC")] + ([S.tok("VE")] if t == 0 else []),
                  inc="Lxg")
        for kc in range(8):
            pe(lambda e, kc=kc: e.transpose(
                ps_x[:, 128 * kc:128 * (kc + 1)],
                D_xg[:, 128 * kc:128 * (kc + 1)], sb_id[:, :]),
               wait=[tld] + ([S.tok("SC")] if kc == 0 else []))
        tp = S.tok("PE")
        for kc in range(8):
            sc(lambda e, kc=kc, t=t: e.copy(
                out=xgf_blk(kc * NT128 + t * 128),
                in_=ps_x[:, 128 * kc:128 * (kc + 1)]), wait=[tp])
        S.drain("scalar")
    txgf = S.tok("SC")

    S.op("tensor", regs_load("tensor", 128), wait=[tet])

    w1v = w1_d.rearrange("e p c -> p e c")
    for mc in range(32):
        tw = []
        for kc in range(8):
            tw.append(dma(D_w1b[:, 1024 * kc:1024 * (kc + 1)]
                          .rearrange("p (e c) -> p e c", c=128),
                          w1v[:, :, 4096 * kc + 128 * mc:4096 * kc + 128 * (mc + 1)],
                          wait=([S.tok("PE")] +
                                ([S.tok("SC"), S.tok("VE"), S.tok("GP"),
                                  S.tok("SG")]
                                 if mc == 0 else [])) if kc == 0 else (),
                          inc=f"W{kc}"))
        tb1 = dma(sb_b1r, b1h_in[mc:mc + 1, :],
                  wait=[S.tok("PE")] +
                       ([S.tok("SC"), S.tok("VE"), S.tok("GP")] if mc == 0
                        else []), inc="B1")
        for t in range(NTILE):
            for kc in range(8):
                def mm1(t, kc):
                    def f(e):
                        rt = REGS[("tensor", "tmp")]
                        e.reg_add(rt, REGS[("tensor", t)],
                                  D_w1b.offset + 1024 * kc)
                        return e.matmul(
                            ps_f[:, 128 * t:128 * (t + 1)],
                            xgf_blk(kc * NT128 + t * 128),
                            bass.AP(D_w1b.tensor, rt, [[16384, 128], [1, 128]]),
                            start=(kc == 0), stop=False)
                    return f
                pe(mm1(t, kc),
                   wait=[tw[kc], txgf] + ([S.tok("SC")] if t == 0 and kc == 0 else []))
            def mb1(t):
                def f(e):
                    rt = REGS[("tensor", "tmp")]
                    e.reg_add(rt, REGS[("tensor", t)], B1R_OFF)
                    return e.matmul(
                        ps_f[:, 128 * t:128 * (t + 1)], sb_ones[0:1, 0:128],
                        bass.AP(ar1, rt, [[16384, 1], [1, 128]]),
                        start=False, stop=True)
                return f
            pe(mb1(t), wait=[tb1])
        for t in range(NTILE):
            tp = S.tok("PE")
            sc((lambda t: (lambda e: e.copy(
                out=D_t1b[:, 128 * (t % 2):128 * (t % 2 + 1)],
                in_=ps_f[:, 128 * t:128 * (t + 1)])))(t),
               wait=[tp] + ([S.tok("Sh")] if t in (0, 1) else []))
            S.drain("scalar")
            tev = S.tok("SC")
            pe((lambda t: (lambda e: e.transpose(
                ps_x[:, 128 * (t % 2):128 * (t % 2 + 1)],
                D_t1b[:, 128 * (t % 2):128 * (t % 2 + 1)], sb_id[:, :])))(t),
               wait=[tev])
            tpx = S.tok("PE")
            sc((lambda t: (lambda e: e.activation(
                out=D_hst[:, 128 * t:128 * (t + 1)],
                in_=ps_x[:, 128 * (t % 2):128 * (t % 2 + 1)],
                func=AF.Gelu)))(t), wait=[tpx, S.tok("Sh")])
        S.drain("scalar")
        tsc = S.tok("SC")
        sto(h_dram[mc, :, :], D_hst[:, :], wait=[tsc], eng="sync", inc="Sh")
    tS1 = S.tok("Sh")

    w2v = w2_d.rearrange("e p c -> p e c")
    nsup = (32 + SUP - 1) // SUP
    for sup in range(nsup):
        kcs = list(range(SUP * sup, min(SUP * (sup + 1), 32)))
        tld = []
        for i, kc2 in enumerate(kcs):
            tld.append(dma(D_h3[:, NT128 * i:NT128 * (i + 1)], h_dram[kc2, :, :],
                           wait=[tS1, S.tok("PE")] if i == 0 else [tS1],
                           inc=f"H{i}"))
        for mc2 in range(8):
            tw = []
            for i, kc2 in enumerate(kcs):
                tw.append(dma(D_w2b[:, 1024 * i:1024 * (i + 1)]
                              .rearrange("p (e c) -> p e c", c=128),
                              w2v[:, :, 1024 * kc2 + 128 * mc2:
                                  1024 * kc2 + 128 * (mc2 + 1)],
                              wait=[S.tok("PE")] if i == 0 else (), inc=f"V{i}"))
            if sup == 0:
                tb2 = dma(sb_b2r, b2h_in[mc2:mc2 + 1, :],
                          wait=[S.tok("PE")], inc="B2")
            for t in range(NTILE):
                if sup == 0:
                    def mb2(t):
                        def f(e):
                            return e.matmul(
                                ps_f[:, 128 * t:128 * (t + 1)],
                                sb_ones[0:1, 0:128],
                                bass.AP(sb_xdbl, REGS[("tensor", t)],
                                        [[4 * CH, 1], [1, 128]]),
                                start=True, stop=False)
                        return f
                    pe(mb2(t), wait=[tb2] +
                       ([S.tok("VE")] if t == 0 else []))
                for i in range(len(kcs)):
                    def mm2(t, i, last, st):
                        def f(e):
                            rt = REGS[("tensor", "tmp")]
                            e.reg_add(rt, REGS[("tensor", t)],
                                      D_w2b.offset + 1024 * i)
                            return e.matmul(
                                ps_f[:, 128 * t:128 * (t + 1)],
                                D_h3[:, NT128 * i + 128 * t:
                                     NT128 * i + 128 * (t + 1)],
                                bass.AP(D_w2b.tensor, rt,
                                        [[16384, 128], [1, 128]]),
                                start=st, stop=last)
                        return f
                    pe(mm2(t, i, i == len(kcs) - 1, sup != 0 and i == 0),
                       wait=[tw[i], tld[i]] +
                            ([S.tok("VE")] if sup != 0 and i == 0 and t == 0
                             else []))
            tp = S.tok("PE")
            if sup == 0:
                ve((lambda mc2: (lambda e: e.tensor_copy(
                    out=acc_blk(mc2 * NT128, NT128), in_=ps_f[:, 0:NT128])))(mc2),
                   wait=[tp])
            else:
                ve((lambda mc2: (lambda e: e.scalar_tensor_tensor(
                    out=acc_blk(mc2 * NT128, NT128), in0=ps_f[:, 0:NT128],
                    scalar=1.0, in1=acc_blk(mc2 * NT128, NT128),
                    op0=OP.mult, op1=OP.add)))(mc2), wait=[tp])
            S.drain("vector")
    tS2 = S.tok("VE")

    for t in range(NTILE):
        dst = so_dram[128 * t:128 * (t + 1), :]
        sto(dst[:, 0:768].rearrange("s (b d) -> s b d", d=128),
            bass.AP(ar2, 128 * t, [[20480, 128], [NT128, 6], [1, 128]]),
            wait=[tS2, S.tok("So")], eng="sync", inc="So")
        sto(dst[:, 768:1024].rearrange("s (b d) -> s b d", d=128),
            bass.AP(sb_winp, 2048 + 128 * t, [[8192, 128], [NT128, 2], [1, 128]]),
            wait=[tS2, S.tok("So")], eng="sync", inc="So")
    tSO = S.tok("So")

    for cg in range(NGC4):
        S.op("gpsimd", (lambda cg: (lambda e: e.dma_gather(
            D_ga[:, 4096 * cg:4096 * (cg + 1)].rearrange("p (b d) -> p b d", d=DM),
            so_dram.ap(), sb_sidx[:, 32 * cg:32 * (cg + 1)], 512, 512, DM,
            elem_step=DM)))(cg),
            wait=[tSO] + tsrep + [S.tok("VE")], inc=f"Gg{cg}", k=16)
    tga = [S.tok(f"Gg{cg}") for cg in range(NGC4)]
    dma(g_dram.ap(), D_ga[:, :].rearrange("p (b d) -> p b d", d=DM), wait=tga,
        eng="gpsimd", inc="Sgd")
    tgd = S.tok("Sgd")

    twout = dma(C_wout[:, :], w_out, wait=[tS2, S.tok("SC"), S.tok("VE"), tgd],
                inc="Wout")

    def gp_agl(j):
        def f(e):
            r = e.alloc_register()
            e.reg_load(r, sb_qm[0:1, 4:5])
            if j:
                e.reg_add(r, r, j * (4 * DQ) * CH)
            return e.dma_start(
                out=C_agl[:, :].rearrange("p (k t) -> p k t", t=CH),
                in_=bass.AP(ago_d.ap().tensor, r,
                            [[CH, 128], [128 * CH, 16], [1, CH]]))
        return f

    for j in range(NGC4):
        S.op("gpsimd", gp_agl(j), wait=[tC1, tgd, S.tok("PE"), S.tok("SC")],
             inc="Ga", k=16)
        tagl = S.tok("Ga")
        for m in range(8):
            bank = ps_a if m % 2 == 0 else ps_b
            for kc in range(16):
                pe((lambda m, kc, bank: (lambda e: e.matmul(
                    bank, C_wout[:, 1024 * kc + 128 * m:1024 * kc + 128 * (m + 1)],
                    C_agl[:, CH * kc:CH * (kc + 1)],
                    start=(kc == 0), stop=(kc == 15))))(m, kc, bank),
                   wait=[twout, tagl] + ([S.tok("SC")] if kc == 0 else []))
            tp = S.tok("PE")
            sc((lambda m, bank: (lambda e: e.mul(out=C_sfm[:, CH * m:CH * (m + 1)],
                                                 in_=bank, mul=0.5)))(m, bank),
               wait=[tp])
        S.drain("scalar")
        tsf = S.tok("SC")
        for tt in range(4):
            for kc in range(8):
                pe((lambda tt, kc: (lambda e: e.transpose(
                    ps_t[:, 128 * (kc % 4):128 * (kc % 4) + 128],
                    C_sfm[:, CH * kc + 128 * tt:CH * kc + 128 * (tt + 1)],
                    sb_id[:, :])))(tt, kc),
                   wait=[tsf] + ([S.tok("SC")] if kc % 4 == 0 else []))
                if kc % 4 == 3:
                    tp = S.tok("PE")
                    sc((lambda tt, kc: (lambda e: e.copy(
                        out=C_stm.rearrange("p (t d) -> p t d", d=DM)
                            [:, tt:tt + 1, 128 * (kc - 3):128 * (kc + 1)],
                        in_=ps_t[:, 0:512].unsqueeze(1))))(tt, kc),
                       wait=[tp] + ([S.tok("Ss")] if tt == 0 else []))
        S.drain("scalar")
        tstm = S.tok("SC")
        sto(s_dram[CH * j:CH * (j + 1), :].rearrange("(t p) d -> p t d", p=128),
            C_stm.rearrange("p (t d) -> p t d", d=DM), wait=[tstm], eng="sync",
            inc="Ss")
    tC2 = S.tok("Ss")

    def gp_eg(tt):
        def f(e):
            return e.dma_start(
                out=E_g[:, :],
                in_=bass.AP(g_dram.ap().tensor, tt * DM,
                            [[NOTL * DM, 128], [1, DM]]))
        return f

    def gp_ex(tt):
        def f(e):
            r = e.alloc_register()
            e.reg_load(r, sb_qm[0:1, 0:1])
            if tt:
                e.reg_add(r, r, tt * 128 * DM)
            return e.dma_start(out=E_x[:, :],
                               in_=bass.AP(x_in.tensor, r, [[DM, 128], [1, DM]]))
        return f

    for tt in range(NOTL):
        S.op("gpsimd", gp_eg(tt),
             wait=[tgd, tC2, S.tok("VE"), S.tok("SC")], inc="Eg", k=16)
        S.op("gpsimd", gp_ex(tt), wait=[S.tok("VE")], inc="Ex", k=16)
        tEg = S.tok("Eg")
        tEx = S.tok("Ex")
        t2 = dma(E_s[:, :], s_dram[128 * tt:128 * (tt + 1), :],
                 wait=[tC2, S.tok("VE")], inc="Ls")
        ve((lambda tt: (lambda e: e.tensor_scalar(
            out=E_t1[:, :], in0=E_g[:, :], scalar1=sb_score[:, tt:tt + 1],
            scalar2=None, op0=OP.mult)))(tt), wait=[tEg, tidx, S.tok("Sout")])
        S.drain("vector")
        tve = S.tok("VE")
        sc(lambda e: e.activation(out=E_g[:, :], in_=E_t1[:, :], func=AF.Silu),
           wait=[tve])
        S.drain("scalar")
        tsc = S.tok("SC")
        ve(lambda e: e.tensor_tensor(out=E_t1[:, :], in0=E_s[:, :], in1=E_g[:, :],
                                     op=OP.mult), wait=[tsc, t2])
        S.drain("vector")
        ve(lambda e: e.tensor_tensor(out=E_t1[:, :], in0=E_t1[:, :], in1=E_x[:, :],
                                     op=OP.add), wait=[tEx])
        S.drain("vector")
        tve = S.tok("VE")
        sto(out[128 * tt:128 * (tt + 1), :], E_t1[:, :], wait=[tve], eng="sync",
            inc="Sout")
    S.op("sync", lambda e: e.nop(), wait=S.all_toks())

    nc._dbg_out = dbg_out
    return _finish(nc, S, mybir)


def _finish(nc, S, _mybir):
    global _SEMS
    sems = {k: nc.alloc_semaphore(name="s" + k) for k in S.cnt}
    _SEMS = sems
    with nc.Block() as blk:

        @blk.sync
        def _(e):
            S.emit("sync", e, sems)

        @blk.tensor
        def _(e):
            S.emit("tensor", e, sems)

        @blk.vector
        def _(e):
            S.emit("vector", e, sems)

        @blk.scalar
        def _(e):
            S.emit("scalar", e, sems)

        @blk.gpsimd
        def _(e):
            S.emit("gpsimd", e, sems)

    _bass_rust.alloc_regs(nc)
    _bass_rust.fuse_regops(nc)
    _bass_rust.replace_nops_with_events(nc)
    _bass_rust.fuse_nops(nc.main_func)
    _bass_rust.remove_dead_nops(nc.main_func)
    _mybir.codegen_inst_isa_subclasses(nc)
    return nc


_SEMS = {}


def _prep_inputs(ins, L=8192):
    x = np.ascontiguousarray(np.asarray(ins["L0_attn_hidden_states"], np.float32))
    inproj = np.asarray(ins["L0_attn_in_proj_w"], np.float32)
    outproj = np.asarray(ins["L0_attn_out_proj_w"], np.float32)
    OWN = L // 4
    NS = OWN + EF * 128
    NTILE = NS // 128
    CAPIO = OWN // 128 + 1

    def lhsT_tile(w, K, M):
        t = np.ascontiguousarray(w.T)
        return np.ascontiguousarray(
            t.reshape(K // 128, 128, M // 128, 128).transpose(1, 0, 2, 3)
            .reshape(128, -1))

    w1 = np.asarray(ins["L0_ffn_w1"], np.float32)
    w2 = np.asarray(ins["L0_ffn_w2"], np.float32)
    w1h = np.stack([
        np.ascontiguousarray(w1[e].reshape(8, 128, 32, 128).transpose(1, 0, 2, 3)
                             .reshape(128, -1)) for e in range(EF)])
    w2h = np.stack([
        np.ascontiguousarray(w2[e].reshape(32, 128, 8, 128).transpose(1, 0, 2, 3)
                             .reshape(128, -1)) for e in range(EF)])
    b1 = np.asarray(ins["L0_ffn_b1"], np.float32)
    b2 = np.asarray(ins["L0_ffn_b2"], np.float32)
    b1h = np.ascontiguousarray(b1.reshape(8, 32, 128).transpose(1, 0, 2).reshape(32, 1024))
    b2h = np.ascontiguousarray(b2.reshape(8, 8, 128).transpose(1, 0, 2).reshape(8, 1024))

    sel16 = np.zeros((16, 128), np.float32)
    for k in range(16):
        for e in range(8):
            sel16[k, e * 16 + k] = 1.0
    sel2 = np.zeros((128, 16), np.float32)
    for e in range(8):
        for k in range(16):
            sel2[e * 16 + k, k] = 1.0
    iotaE = np.repeat(np.arange(8, dtype=np.float32), 16)[:, None]
    iotaPAD = np.broadcast_to(np.arange(CAPIO, dtype=np.float32) * 128, (8, CAPIO)).copy()
    iotaTILE = np.broadcast_to(np.arange(NTILE, dtype=np.float32) * 128, (8, NTILE)).copy()
    colk0 = np.ones((1, 128), np.float32)
    colk0[0, 0::16] = 0.0
    selb16 = np.zeros((16, 16 * 128), np.float32)
    for n in range(16):
        selb16[n, 128 * n:128 * (n + 1)] = 1.0
    iota8m = np.broadcast_to(np.arange(8, dtype=np.float32) - BIGV, (128, 8)).copy()

    def dslice(w, q, dirb):
        return w[DQ * q:DQ * (q + 1)] if not dirb else w

    maps = []
    for core in range(8):
        b, q = core // 4, core % 4
        xrows = inproj[DQ * q:DQ * (q + 1)]
        zrows = inproj[2048 + DQ * q:2048 + DQ * (q + 1)]
        winp = lhsT_tile(np.concatenate([xrows, zrows]), 1024, 1024)

        def dirpack(wf, wb, fn):
            return np.stack([fn(wf), fn(wb)])

        xw = np.asarray(ins["L0_attn_x_proj_w"], np.float32)
        xwb = np.asarray(ins["L0_attn_x_proj_w_b"], np.float32)
        def xp(w):
            t = np.ascontiguousarray(w[:, DQ * q:DQ * (q + 1)].T)
            return np.ascontiguousarray(
                t.reshape(4, 128, 96).transpose(1, 0, 2).reshape(128, 4 * 96))
        wxp = np.stack([xp(xw), xp(xwb)])

        dtw = np.asarray(ins["L0_attn_dt_proj_w"], np.float32)
        dtwb = np.asarray(ins["L0_attn_dt_proj_w_b"], np.float32)
        def dtp(w):
            return np.ascontiguousarray(w[DQ * q:DQ * (q + 1)].T)
        wdt = np.stack([dtp(dtw), dtp(dtwb)])

        def q4(v):
            return np.ascontiguousarray(
                v[DQ * q:DQ * (q + 1)].reshape(4, 128).T)

        def q4k(v, K):
            return np.ascontiguousarray(
                v[DQ * q:DQ * (q + 1)].reshape(4, 128, K).transpose(1, 0, 2)
                .reshape(128, 4 * K))

        wcv = np.stack([q4k(np.asarray(ins["L0_attn_conv_w"], np.float32), 4),
                        q4k(np.asarray(ins["L0_attn_conv_w_b"], np.float32), 4)])
        bcv = np.stack([q4(np.asarray(ins["L0_attn_conv_b"], np.float32)),
                        q4(np.asarray(ins["L0_attn_conv_b_b"], np.float32))])
        bdt = np.stack([q4(np.asarray(ins["L0_attn_dt_proj_b"], np.float32)),
                        q4(np.asarray(ins["L0_attn_dt_proj_b_b"], np.float32))])
        alog = np.stack([q4k(np.asarray(ins["L0_attn_A_log"], np.float32), NST),
                         q4k(np.asarray(ins["L0_attn_A_b_log"], np.float32), NST)])
        dvec = np.stack([q4(np.asarray(ins["L0_attn_D"], np.float32)),
                         q4(np.asarray(ins["L0_attn_D_b"], np.float32))])
        wout = lhsT_tile(outproj, 2048, 1024)
        _wgr = np.ascontiguousarray(np.asarray(ins["L0_ffn_wg_red"], np.float32).T)
        wgred = np.ascontiguousarray(
            _wgr.reshape(8, 128, 16).transpose(1, 0, 2).reshape(128, 128))
        agoff = (q * OWN // 512) * (2048 * 512) + (q * OWN % 512)
        qmeta = np.array([[q * OWN * DM, q * OWN, q * OWN, q * OWN * 16,
                           agoff, 0]],
                         np.int32)
        maps.append({
            "x_b": np.ascontiguousarray(x[b, :L]),
            "w_inproj": winp, "w_xproj": wxp, "w_dt": wdt, "w_conv": wcv,
            "b_conv": bcv, "b_dt": bdt, "a_log": alog, "d_vec": dvec,
            "w_out": wout, "w_gred": wgred,
            "wg_in": np.asarray(ins["L0_ffn_wg"], np.float32),
            "w1_d": w1h, "w2_d": w2h, "b1h": b1h, "b2h": b2h,
            "ident_in": np.eye(128, dtype=np.float32),
            "sel16_in": sel16, "sel2_in": sel2, "iotaE_in": iotaE,
            "iotaPAD_in": iotaPAD, "iotaTILE_in": iotaTILE,
            "colk0_in": colk0, "iota8m_in": iota8m, "qmeta_in": qmeta,
            "selb16_in": selb16,
        })
    return maps


def kernel(**inputs):
    L = int(np.asarray(inputs["L0_attn_hidden_states"]).shape[1])
    nc = build_nc(L=L)
    maps = _prep_inputs(inputs, L=L)
    res = run_bass_kernel_spmd(nc, maps, core_ids=list(range(8)))
    OWN = L // 4
    full = np.zeros((2, L, DM), np.float32)
    for core in range(8):
        b, q = core // 4, core % 4
        full[b, OWN * q:OWN * (q + 1), :] = res.results[core]["out"]
    return full


# version 06
# speedup vs baseline: 1.2836x; 1.0030x over previous
import numpy as np
import bass_rust as _bass_rust
import concourse.bass as bass
import concourse.mybir as mybir
from concourse.bass_utils import run_bass_kernel_spmd
from concourse.library_config import mlp as _LIB_MLP

F32 = mybir.dt.float32
I16 = mybir.dt.int16
I32 = mybir.dt.int32
AF = mybir.ActivationFunctionType
OP = mybir.AluOpType
AX = mybir.AxisListType.X

DM = 1024
DQ = 512
NDT = 4
NST = 16
CH = 512
EF = 8
BIGV = 100000.0
NG = 4
SUP = 4


class _BassD(bass.Bass):
    _defer_register_allocation = True


class Sched:
    def __init__(self):
        self.ops = []
        self.cnt = {}

    def op(self, eng, fn, wait=(), inc=None, k=1):
        self.ops.append((eng, fn, tuple(wait), inc, k))
        if inc:
            self.cnt[inc] = self.cnt.get(inc, 0) + k
            return (inc, self.cnt[inc])
        return None

    def drain(self, eng, inc=None):
        return self.op(eng, "drain", inc=inc, k=1)

    def tok(self, sem):
        return (sem, self.cnt.get(sem, 0))

    def all_toks(self):
        return [(s, v) for s, v in self.cnt.items()]

    def bump(self, sem, k):
        self.cnt[sem] = self.cnt.get(sem, 0) + k
        return (sem, self.cnt[sem])

    def emit(self, name, eng, sems):
        hw = {}
        for e, fn, waits, inc, k in self.ops:
            if e != name:
                continue
            for (s, v) in waits:
                if v > 0 and hw.get(s, 0) < v:
                    eng.wait_ge(sems[s], v)
                    hw[s] = v
            if fn == "drain":
                inst = eng.drain()
            else:
                inst = fn(eng)
            if inc is not None and inst is not None:
                inst.then_inc(sems[inc], k)


def rap(ap_or_handle, off, pattern):
    if isinstance(ap_or_handle, bass.AP):
        return bass.AP(ap_or_handle.tensor, ap_or_handle.offset + off, pattern)
    return bass.AP(ap_or_handle, off, pattern)


def build_nc(L=8192, dbg=False, mixer_only=False, ffn_only=False):
    NCH = L // CH
    OWN = L // 4
    NOTL = OWN // 128
    NS = OWN + EF * 128
    NTILE = NS // 128
    NT128 = NTILE * 128
    CAPIO = OWN // 128 + 1
    NGC4 = OWN // 512
    assert NOTL == 16 * (L // 8192) or L == 8192 or True

    nc = _BassD("TRN2", target_bir_lowering=False)

    def din(name, shape, dt=F32):
        return nc.dram_tensor(name, shape, dt, kind="ExternalInput").ap()

    x_in = din("x_b", [L, DM])
    w_inproj = din("w_inproj", [128, 8192])
    w_xproj = din("w_xproj", [2, 128, 4 * 96])
    w_dt = din("w_dt", [2, 64, DQ])
    w_conv = din("w_conv", [2, 128, NDT * 4])
    b_conv = din("b_conv", [2, 128, NDT])
    b_dt = din("b_dt", [2, 128, NDT])
    a_log = din("a_log", [2, 128, NDT * NST])
    d_vec = din("d_vec", [2, 128, NDT])
    w_out = din("w_out", [128, 16384])
    w_gred = din("w_gred", [128, 128])
    wg_in = din("wg_in", [8, 16])
    w1_d = din("w1_d", [EF, 128, 8 * 32 * 128])
    w2_d = din("w2_d", [EF, 128, 32 * 8 * 128])
    b1h_in = din("b1h", [32, 1024])
    b2h_in = din("b2h", [8, 1024])
    ident_in = din("ident_in", [128, 128])
    sel16_in = din("sel16_in", [16, 128])
    sel2_in = din("sel2_in", [128, 16])
    iotaE_in = din("iotaE_in", [128, 1])
    iotaPAD_in = din("iotaPAD_in", [8, CAPIO])
    iotaTILE_in = din("iotaTILE_in", [8, NTILE])
    colk0_in = din("colk0_in", [1, 128])
    iota8m_in = din("iota8m_in", [128, 8])
    selb16_in = din("selb16_in", [16, 16 * 128])
    qmeta_in = din("qmeta_in", [1, 6], I32)

    out = nc.dram_tensor("out", [OWN, DM], F32, kind="ExternalOutput").ap()
    dbg_out = {}
    if dbg:
        for k, shp in [("y", [NCH, 128, 2 * NDT * CH]), ("md", [NCH, 128, 2 * NDT * CH]),
                       ("xt", [NCH, 128, 2 * NDT * CH]), ("xdbl", [NCH, 192, CH]),
                       ("score", [128, NOTL]), ("idx", [128, NOTL]),
                       ("destT", [16, 128]), ("etile", [1, NTILE])]:
            dbg_out[k] = nc.dram_tensor("dbg_" + k, shp, F32, kind="ExternalOutput").ap()

    xn_dram = nc.dram_tensor("xn_dram", [L, DM], F32)
    uf_dram = nc.dram_tensor("uf_dram", [NCH, 128, NDT * CH], F32)
    ub_dram = nc.dram_tensor("ub_dram", [NCH, 128, NDT * CH], F32)
    ar_in = nc.dram_tensor("ar_in", [96, CH], F32)
    ar_out = nc.dram_tensor("ar_out", [96, CH], F32)
    xz_dram = nc.dram_tensor("xz_dram", [NCH, 128, 8 * CH], F32)
    agi_d = [nc.dram_tensor(f"agi{i}", [DQ, CH], F32) for i in range(2)]
    ago_d = nc.dram_tensor("ago", [NCH, 4 * DQ, CH], F32)
    xg_dram = nc.dram_tensor("xg_dram", [NS, DM], F32)
    h_dram = nc.dram_tensor("h_dram", [32, 128, NT128], F32)
    so_dram = nc.dram_tensor("so_dram", [NS, DM], F32)
    g_dram = nc.dram_tensor("g_dram", [128, NOTL, DM], F32)
    s_dram = nc.dram_tensor("s_dram", [OWN, DM], F32)
    xr_dram = nc.dram_tensor("xr_dram", [NCH, 16, CH], F32)

    groups = [[0, 1, 2, 3], [4, 5, 6, 7]]

    A = nc.alloc_sbuf_tensor
    ar1 = A("ar1", [128, 16384], F32)
    ar2 = A("ar2", [128, 20480], F32)
    sb_winp = A("sb_winp", [128, 8192], F32)
    sb_wxp = A("sb_wxp", [128, 2 * 4 * 96], F32)
    sb_wdt = A("sb_wdt", [64, 2 * DQ], F32)
    sb_wcv = A("sb_wcv", [128, 2 * NDT * 4], F32)
    sb_bcv = A("sb_bcv", [128, 2 * NDT], F32)
    sb_bdtn = A("sb_bdtn", [128, 2 * NDT], F32)
    sb_apos = A("sb_apos", [128, 2 * NDT * NST], F32)
    sb_dvec = A("sb_dvec", [128, 2 * NDT], F32)
    sb_wgr = A("sb_wgr", [128, 128], F32)
    sb_wg = A("sb_wg", [8, 16], F32)
    sb_wguT = A("sb_wguT", [16, 8], F32)
    sb_gn = A("sb_gn", [8, 2], F32)
    sb_id = A("sb_id", [128, 128], F32)
    sb_sel16 = A("sb_sel16", [16, 128], F32)
    sb_sel2 = A("sb_sel2", [128, 16], F32)
    sb_iotaE = A("sb_iotaE", [128, 1], F32)
    sb_iPAD = A("sb_iPAD", [8, CAPIO], F32)
    sb_iTILE = A("sb_iTILE", [8, NTILE], F32)
    sb_colk0 = A("sb_colk0", [1, 128], F32)
    sb_i8m = A("sb_i8m", [128, 8], F32)
    sb_ones = A("sb_ones", [1, 128], F32)
    sb_selb = A("sb_selb", [16, 16 * 128], F32)
    sb_ones8 = A("sb_ones8", [8, 1], F32)
    sb_zer = A("sb_zer", [128, 512], F32)
    sb_sq = A("sb_sq", [128, 512], F32)
    sb_hold = A("sb_hold", [128, 2 * NDT * 4], F32)
    sb_xdbl = A("sb_xdbl", [96, 4 * CH], F32)
    sb_hin = A("sb_hin", [128, 2 * NDT * NST], F32)
    sb_stat = A("sb_stat", [128, 24], F32)
    sb_score = A("sb_score", [128, NOTL], F32)
    sb_cnt2 = A("sb_cnt2", [128, 1], F32)
    sb_g8 = A("sb_g8", [8, 1], F32)
    sb_p8c = A("sb_p8c", [8, CAPIO], F32)
    sb_p8 = A("sb_p8", [8, 1], F32)
    sb_r8 = A("sb_r8", [1, 8], F32)
    sb_eb = A("sb_eb", [1, 8], F32)
    sb_eb8 = A("sb_eb8", [8, 1], F32)
    sb_base2 = A("sb_base2", [128, 1], F32)
    sb_qm = A("sb_qm", [1, 6], I32)
    sb_sidx = sb_apos[:, 0:(NOTL * 8) // 2].bitcast(I16)
    sb_sidxT16 = sb_apos[:, 64:64 + (NOTL * 8) // 2].bitcast(I16)
    sb_etil_i = sb_hin[0:1, 0:NTILE].bitcast(I32)
    sb_cst = A("sb_cst", [128, 2], F32)
    nc.const_aps.register(F32, 1e-5, sb_cst[:, 0:1])
    nc.const_aps.register(F32, -1.0, sb_cst[:, 1:2])

    ps = nc.alloc_psum_tensor("ps", [128, 4096], F32)
    ps_a = ps[:, 0:512]
    ps_b = ps[:, 512:1024]
    ps_t = ps[:, 1024:1536]
    ps_s = ps[:, 1536:2048]
    ps_c = ps[:, 2048:2560]
    ps_d = ps[:, 2560:3072]
    ps_f = ps[:, 0:NT128]
    ps_x = ps[:, 3072:4096]

    def v1(c, w):
        return ar1[:, c:c + w]

    def v2(c, w):
        return ar2[:, c:c + w]

    B_brep = v1(0, NG * CH)
    B_crep = v1(2048, NG * CH)
    B_dbu = v1(4096, NG * CH)
    B_h = v1(6144, NG * CH)
    B_hc = v1(8192, NG * CH)
    B_u1 = v1(8192, NDT * CH)
    B_da = v1(10240, NG * CH)
    B_tmp = v1(10240, NDT * CH)
    B_u2 = v1(10240, NDT * CH)
    B_md = v1(12288, NDT * CH)
    B_dxp = v1(14336, NDT * CH)
    B_xz = [v2(0, 8 * CH), v2(4096, 8 * CH)]
    B_xt = v2(8192, 2 * NDT * CH)
    B_xtm = v2(12288, 4 * DM)
    B_sz = v2(12288, 2 * NDT * CH)
    B_xnfm = v2(16384, 8 * CH)
    B_y = v2(16384, 2 * NDT * CH)

    C_wout = v1(0, 16384)
    C_agl = v2(0, 16 * CH)
    C_sfm = v2(8192, 8 * CH)
    C_stm = v2(12288, 4 * DM)
    C_ua = v2(16384, NDT * CH)
    C_ub = v2(18432, NDT * CH)

    D_ssrc = v2(0, NOTL * DM)
    D_w1b = v1(0, 8192)
    D_hst = v1(8192, NT128)
    D_xg = v1(8192 + NT128, DM)
    D_t1b = v1(8192 + NT128 + DM, 256)
    B1R_OFF = 8192 + NT128 + DM + 256
    sb_b1r = ar1[0:1, B1R_OFF:B1R_OFF + 1024]
    sb_b2r = sb_xdbl[0:1, 0:1024]
    XGF_LO = 20480
    D_h3 = v1(0, SUP * NT128)
    D_w2b = v1(SUP * NT128, SUP * 1024)
    D_otm = v1(0, 1024)
    D_ga = v2(0, NOTL * DM)
    sb_wgu = ar1[0:8, 0:16]
    sb_xext = v1(0, CH + 8)
    sb_bB = sb_xdbl[0:16, 2 * CH:3 * CH]
    sb_bC = sb_xdbl[0:16, 3 * CH:4 * CH]
    sb_xrc = sb_xdbl[0:16, CH:2 * CH]
    D_xro = ar1[0:16, 0:OWN]
    sb_idxT = ar1[0:16, 2048:2176]
    sb_mask2 = v1(2176, 128)
    sb_cum2 = v1(2304, 128)
    sb_crow = ar1[0:1, 2432:2560]
    sb_kin = ar1[0:1, 2560:2688]
    sb_kex = ar1[0:1, 2688:2816]
    sb_stt2 = v1(2816, 128)
    sb_logit = v1(2944, NOTL * 8)
    sb_mneg = v1(3072, NOTL)
    sb_mpos = v1(3072 + NOTL, NOTL)
    sb_ssum = v1(3072 + 2 * NOTL, NOTL)
    sb_idx = v1(3072 + 3 * NOTL, NOTL)
    sb_mtmp = v1(3072 + 4 * NOTL, 8)
    sb_et8 = ar1[0:8, 3208:3208 + NTILE]
    sb_etil = ar1[0:1, 3240:3240 + NTILE]
    E_g = v1(0, DM)
    E_x = v1(1024, DM)
    E_s = v1(2048, DM)
    E_t1 = v1(3072, DM)

    def _split(col, w, base, ovf):
        if col + w <= XGF_LO:
            return ar2[:, col:col + w]
        c = col - XGF_LO + base
        return ovf[:, c:c + w]

    xgf_blk = lambda col, w=128: _split(col, w, 0, sb_winp)
    acc_blk = lambda col, w=128: _split(col, w, 4096, sb_winp)

    S = Sched()

    def dma(dst, src, wait=(), eng="sync", inc="LD"):
        return S.op(eng, (lambda d, s: (lambda e: e.dma_start(out=d, in_=s)))(dst, src),
                    wait=wait, inc=inc, k=16)

    def sto(dst, src, wait=(), eng="sync", inc="ST"):
        return dma(dst, src, wait=wait, eng=eng, inc=inc)

    def gdma(dst, src, wait=()):
        return dma(dst, src, wait=wait, eng="gpsimd", inc="GD")

    def pe(fn, wait=()):
        return S.op("tensor", fn, wait=wait, inc="PE")

    def sc(fn, wait=()):
        return S.op("scalar", fn, wait=wait, inc="SC")

    def ve(fn, wait=()):
        return S.op("vector", fn, wait=wait, inc="VE")

    def gp(fn, wait=()):
        return S.op("gpsimd", fn, wait=wait, inc="GP")

    for d, s in [
        (sb_winp[:, :], w_inproj), (sb_wxp[:, :].rearrange("p (d c) -> p d c", d=2), w_xproj.rearrange("d p c -> p d c")),
        (sb_wdt[:, :].rearrange("p (d c) -> p d c", d=2), w_dt.rearrange("d p c -> p d c")),
        (sb_wcv[:, :].rearrange("p (d c) -> p d c", d=2), w_conv.rearrange("d p c -> p d c")),
        (sb_bcv[:, :].rearrange("p (d c) -> p d c", d=2), b_conv.rearrange("d p c -> p d c")),
        (sb_bdtn[:, :].rearrange("p (d c) -> p d c", d=2), b_dt.rearrange("d p c -> p d c")),
        (sb_apos[:, :].rearrange("p (d c) -> p d c", d=2), a_log.rearrange("d p c -> p d c")),
        (sb_dvec[:, :].rearrange("p (d c) -> p d c", d=2), d_vec.rearrange("d p c -> p d c")),
        (sb_wgr[:, :], w_gred), (sb_wg[:, :], wg_in),
        (sb_id[:, :], ident_in), (sb_sel16[:, :], sel16_in),
        (sb_sel2[:, :], sel2_in), (sb_iotaE[:, :], iotaE_in),
        (sb_iPAD[:, :], iotaPAD_in), (sb_iTILE[:, :], iotaTILE_in),
        (sb_colk0[:, :], colk0_in), (sb_i8m[:, :], iota8m_in),
        (sb_selb[:, :], selb16_in),
        (sb_qm[:, :], qmeta_in),
    ]:
        dma(d, s, wait=[S.tok("LA")], inc="LA")
    tA = [S.tok("LA")]
    zer_dram = nc.dram_tensor("zer_dram", [128, 4096], F32)

    ve(lambda e: e.memset(sb_cst[:, 0:1], 1e-5))
    ve(lambda e: e.memset(sb_cst[:, 1:2], -1.0))
    ve(lambda e: e.memset(sb_ones[:, :], 1.0))
    ve(lambda e: e.memset(sb_ones8[:, :], 1.0))
    ve(lambda e: e.memset(sb_zer[:, :], 0.0))
    S.drain("vector")
    tzm = S.tok("VE")
    for zk in range(8):
        dma(zer_dram[:, 512 * zk:512 * (zk + 1)], sb_zer[:, :],
            wait=[tzm, S.tok("Zf")], inc="Zf")
    tzf = [S.tok("Zf")]
    ve(lambda e: e.memset(sb_hin[:, :], 0.0))
    ve(lambda e: e.memset(sb_hold[:, :], 0.0))
    ve(lambda e: e.tensor_scalar(out=sb_bdtn[:, :], in0=sb_bdtn[:, :], scalar1=-1.0,
                                 scalar2=None, op0=OP.mult), wait=tA)
    sc(lambda e: e.activation(out=sb_apos[:, :], in_=sb_apos[:, :], func=AF.Exp),
       wait=tA)
    sc(lambda e: e.activation(out=sb_wgu[:, :], in_=sb_wg[:, :], func=AF.Square,
                              accum_out=sb_gn[:, 0:1]))
    S.drain("scalar")
    sc(lambda e: e.activation(out=sb_gn[:, 1:2], in_=sb_gn[:, 0:1], func=AF.Sqrt))
    S.drain("scalar")
    tA_sc = S.tok("SC")
    ve(lambda e: e.tensor_scalar_max(sb_gn[:, 1:2], sb_gn[:, 1:2], 1e-4), wait=[tA_sc])
    S.drain("vector")
    ve(lambda e: e.reciprocal(out=sb_gn[:, 0:1], in_=sb_gn[:, 1:2]))
    S.drain("vector")
    ve(lambda e: e.tensor_scalar(out=sb_wgu[:, :], in0=sb_wg[:, :],
                                 scalar1=sb_gn[:, 0:1], scalar2=None, op0=OP.mult))
    S.drain("vector")
    tAv = S.tok("VE")
    pe(lambda e: e.transpose(ps_t[0:16, 0:8], sb_wgu[:, :], sb_id[0:8, 0:8]),
       wait=[tAv])
    tp = S.tok("PE")
    sc(lambda e: e.copy(out=sb_wguT[:, :], in_=ps_t[0:16, 0:8]), wait=[tp])
    S.drain("scalar")
    tA_done = S.tok("SC")

    xz_tok = {}
    for pass_ in ((0, 1) if not ffn_only else ()):
        dir_ = pass_
        for s_ in range(NCH):
            c = s_ if pass_ == 0 else NCH - 1 - s_
            ci = 0
            xext4 = [v1(520 * dt, 520) for dt in range(NDT)]
            sq4 = [v1(2080 + 512 * dt, 512) for dt in range(NDT)]
            if pass_ == 0:
                tld = dma(B_xtm.rearrange("p (t d) -> p t d", d=DM),
                          x_in[CH * c:CH * (c + 1), :].rearrange("(t p) d -> p t d", p=128),
                          wait=[S.tok("VE"), S.tok("SC"), S.tok("PE"),
                                S.tok("Sxn"), S.tok("GP")], inc="Lx")
                for tt in range(4):
                    ve(lambda e, tt=tt: e.reduce_sum(sb_stat[:, 2 * tt:2 * tt + 1],
                                                     B_xtm[:, DM * tt:DM * (tt + 1)], axis=AX),
                       wait=[tld, tAv] if tt == 0 else [])
                for tt in range(4):
                    sc(lambda e, tt=tt: e.activation(out=sq4[tt][:, :],
                                                     in_=B_xtm[:, DM * tt:DM * tt + 512],
                                                     func=AF.Square,
                                                     accum_out=sb_stat[:, 2 * tt + 1:2 * tt + 2]),
                       wait=[tld, tA_done] if tt == 0 else [])
                S.drain("scalar")
                for tt in range(4):
                    sc(lambda e, tt=tt: e.activation(out=sq4[tt][:, :],
                                                     in_=B_xtm[:, DM * tt + 512:DM * (tt + 1)],
                                                     func=AF.Square,
                                                     accum_out=sb_stat[:, 16 + tt:17 + tt]))
                S.drain("scalar")
                S.drain("vector")
                tve, tsc = S.tok("VE"), S.tok("SC")
                ve(lambda e: e.tensor_tensor(out=sb_stat[:, 1:8:2], in0=sb_stat[:, 1:8:2],
                                             in1=sb_stat[:, 16:20], op=OP.add),
                   wait=[tve, tsc])
                S.drain("vector")
                ve(lambda e: e.tensor_scalar(out=sb_stat[:, 8:12], in0=sb_stat[:, 0:8:2],
                                             scalar1=1.0 / DM, scalar2=None, op0=OP.mult))
                S.drain("vector")
                ve(lambda e: e.tensor_tensor(out=sb_stat[:, 12:16], in0=sb_stat[:, 8:12],
                                             in1=sb_stat[:, 8:12], op=OP.mult))
                ve(lambda e: e.tensor_scalar(out=sb_stat[:, 0:8:2], in0=sb_stat[:, 1:8:2],
                                             scalar1=1.0 / DM, scalar2=None, op0=OP.mult))
                S.drain("vector")
                ve(lambda e: e.tensor_tensor(out=sb_stat[:, 12:16], in0=sb_stat[:, 0:8:2],
                                             in1=sb_stat[:, 12:16], op=OP.subtract))
                S.drain("vector")
                tve = S.tok("VE")
                sc(lambda e: e.activation(out=sb_stat[:, 0:8:2], in_=sb_stat[:, 12:16],
                                          func=AF.Sqrt, bias=1e-5), wait=[tve])
                S.drain("scalar")
                tsc = S.tok("SC")
                ve(lambda e: e.reciprocal(out=sb_stat[:, 12:16], in_=sb_stat[:, 0:8:2]),
                   wait=[tsc])
                S.drain("vector")
                for tt in range(4):
                    ve(lambda e, tt=tt: e.tensor_scalar(
                        out=B_xtm[:, DM * tt:DM * (tt + 1)],
                        in0=B_xtm[:, DM * tt:DM * (tt + 1)],
                        scalar1=sb_stat[:, 8 + tt:9 + tt],
                        scalar2=sb_stat[:, 12 + tt:13 + tt],
                        op0=OP.subtract, op1=OP.mult))
                S.drain("vector")
                txn = S.tok("VE")
                sto(xn_dram[CH * c:CH * (c + 1), :].rearrange("(t p) d -> p t d", p=128),
                    B_xtm.rearrange("p (t d) -> p t d", d=DM), wait=[txn],
                    eng="sync", inc="Sxn")
                for tt in range(4):
                    for kc in range(8):
                        pe(lambda e, tt=tt, kc=kc: e.transpose(
                            ps_t[:, 128 * (kc % 4):128 * (kc % 4) + 128],
                            B_xtm[:, DM * tt + 128 * kc:DM * tt + 128 * (kc + 1)],
                            sb_id[:, :]),
                           wait=[txn] + ([S.tok("SC")] if kc % 4 == 0 else []))
                        if kc % 4 == 3:
                            tp = S.tok("PE")
                            extra2 = ([S.tok("VE")] +
                                      ([S.tok("SG")] if dbg else [])
                                      if tt == 0 and kc == 3 else [])
                            sc(lambda e, tt=tt, kc=kc: e.copy(
                                out=B_xnfm.rearrange("p (k t) -> p k t", t=CH)
                                    [:, kc - 3:kc + 1, 128 * tt:128 * (tt + 1)],
                                in_=ps_t[:, 0:512].rearrange("p (k t) -> p k t", t=128)),
                               wait=[tp] + extra2)
                S.drain("scalar")
                ttr = S.tok("SC")
                for m in range(8):
                    bank = ps_a if m % 2 == 0 else ps_b
                    for kc in range(8):
                        pe(lambda e, m=m, kc=kc, bank=bank: e.matmul(
                            bank,
                            sb_winp[:, 1024 * kc + 128 * m:1024 * kc + 128 * (m + 1)],
                            B_xnfm[:, CH * kc:CH * (kc + 1)],
                            start=(kc == 0), stop=(kc == 7)),
                           wait=[ttr] + tA + ([S.tok("SC")] if kc == 0 else []))
                    tp = S.tok("PE")
                    extra = ([S.tok("VE"), S.tok("GP"), S.tok("Sxz")]
                             if m == 0 else [])
                    sc(lambda e, m=m, bank=bank, ci=ci: e.copy(
                        out=B_xz[ci][:, CH * m:CH * (m + 1)], in_=bank),
                       wait=[tp] + extra)
                S.drain("scalar")
                t_xz = S.tok("SC")
                xz_tok[c] = sto(xz_dram[c, :, :], B_xz[ci][:, :], wait=[t_xz],
                                eng="sync", inc="Sxz")
                for kc in range(8):
                    pe(lambda e, kc=kc: e.matmul(
                        ps_s[0:16, :], sb_wgr[:, 16 * kc:16 * (kc + 1)],
                        B_xnfm[:, CH * kc:CH * (kc + 1)],
                        start=(kc == 0), stop=(kc == 7)),
                       wait=[ttr] + tA + ([S.tok("SC")] if kc == 0 else []))
                tp = S.tok("PE")
                sc(lambda e: e.copy(out=sb_xrc[:, :], in_=ps_s[0:16, :]),
                   wait=[tp, S.tok("Sxr")])
                S.drain("scalar")
                txr = S.tok("SC")
                sto(xr_dram[c, :, :], sb_xrc[:, :], wait=[txr], eng="sync",
                    inc="Sxr")
            else:
                tld = dma(B_xz[ci][:, :], xz_dram[c, :, :],
                          wait=[S.tok("VE"), S.tok("SC"), S.tok("GP"),
                                xz_tok[c], S.tok("Su")], inc="Lz")
                t_xz = tld

            for dt in range(NDT):
                srcx = B_xz[ci][:, CH * dt:CH * (dt + 1)]
                if dir_ == 1:
                    srcx = srcx[:, ::-1]
                ve(lambda e, srcx=srcx, dt=dt: e.tensor_copy(
                    out=xext4[dt][:, 3:3 + CH], in_=srcx),
                   wait=[t_xz] if dt == 0 else [])
            for dt in range(NDT):
                ve(lambda e, dir_=dir_, dt=dt: e.tensor_copy(
                    out=xext4[dt][:, 0:3],
                    in_=sb_hold[:, 4 * (NDT * dir_ + dt):
                                4 * (NDT * dir_ + dt) + 3]))
            S.drain("vector")
            for dt in range(NDT):
                ve(lambda e, dir_=dir_, dt=dt: e.tensor_copy(
                    out=sb_hold[:, 4 * (NDT * dir_ + dt):
                                4 * (NDT * dir_ + dt) + 3],
                    in_=xext4[dt][:, CH:CH + 3]))
            for dt in range(NDT):
                ve(lambda e, dir_=dir_, dt=dt: e.tensor_scalar(
                    out=sq4[dt][:, :], in0=xext4[dt][:, 0:CH],
                    scalar1=sb_wcv[:, 4 * (NDT * dir_ + dt):
                                   4 * (NDT * dir_ + dt) + 1],
                    scalar2=None, op0=OP.mult))
            S.drain("vector")
            for k in range(1, 4):
                for dt in range(NDT):
                    ve(lambda e, k=k, dir_=dir_, dt=dt: e.scalar_tensor_tensor(
                        out=sq4[dt][:, :], in0=xext4[dt][:, k:k + CH],
                        scalar=sb_wcv[:, 4 * (NDT * dir_ + dt) + k:
                                      4 * (NDT * dir_ + dt) + k + 1],
                        in1=sq4[dt][:, :], op0=OP.mult, op1=OP.add))
                S.drain("vector")
            tve = S.tok("VE")
            for dt in range(NDT):
                sc(lambda e, dir_=dir_, dt=dt: e.activation(
                    out=B_xt[:, CH * dt:CH * (dt + 1)],
                    in_=sq4[dt][:, :], func=AF.Silu,
                    bias=sb_bcv[:, NDT * dir_ + dt:NDT * dir_ + dt + 1]),
                   wait=[tve] + ([S.tok("SG")] if dbg and dt == 0 else []))
            S.drain("scalar")
            t_xt = S.tok("SC")
            for kc in range(NDT):
                pe(lambda e, dir_=dir_, kc=kc: e.matmul(
                    ps_s[0:96, :],
                    sb_wxp[:, 96 * (4 * dir_ + kc):96 * (4 * dir_ + kc + 1)],
                    B_xt[:, CH * kc:CH * (kc + 1)],
                    start=(kc == 0), stop=(kc == 3)),
                   wait=[t_xt] + ([S.tok("SC")] if kc == 0 else []))
            tp = S.tok("PE")
            sc(lambda e: e.copy(out=sb_xdbl[:, 0:CH], in_=ps_s[0:96, :]),
               wait=[tp, S.tok("Lxd"), S.tok("LbB"), S.tok("LbC"), S.tok("CC")])
            S.drain("scalar")
            tsc = S.tok("SC")
            sto(ar_in.ap(), sb_xdbl[:, 0:CH], wait=[tsc, S.tok("CC")],
                eng="sync", inc="Sar")
            tst = S.tok("Sar")
            S.op("gpsimd", lambda e: e.collective_compute(
                "AllReduce", OP.add, replica_groups=groups,
                ins=[ar_in.ap().opt()], outs=[ar_out.ap().opt()]),
                wait=[tst, S.tok("Lxd"), S.tok("LbB"), S.tok("LbC")],
                inc="CC", k=1)
            tcc = S.tok("CC")
            txda = dma(sb_xdbl[:, 0:CH], ar_out.ap(),
                       wait=[tcc, S.tok("SC"), S.tok("PE")], inc="Lxd")
            tbB = dma(sb_bB[:, 0:CH], ar_out[64:80, :],
                      wait=[tcc, S.tok("PE")], inc="LbB")
            tbC = dma(sb_bC[:, 0:CH], ar_out[80:96, :], wait=[tcc], inc="LbC")
            txd = [txda, tbB, tbC]
            if dbg:
                sto(dbg_out["xdbl"][c, 96 * dir_:96 * (dir_ + 1), :],
                    ar_out.ap(), wait=[tcc, S.tok("SG")], eng="sync", inc="SG")

            for dt in range(NDT):
                pe(lambda e, dir_=dir_, dt=dt: e.matmul(
                    ps_s, sb_wdt[:, DQ * dir_ + 128 * dt:DQ * dir_ + 128 * (dt + 1)],
                    sb_xdbl[0:64, 0:CH],
                    start=True, stop=True),
                   wait=txd + [S.tok("SC")])
                tp = S.tok("PE")
                extra = ([S.tok("VE"), S.tok("GP")] +
                         ([S.tok("SG")] if dbg else []) if dt == 0 else [])
                sc(lambda e, dir_=dir_, dt=dt: e.activation(
                    out=B_tmp[:, CH * dt:CH * (dt + 1)], in_=ps_s,
                    func=AF.Sigmoid, scale=-1.0,
                    bias=sb_bdtn[:, NDT * dir_ + dt:NDT * dir_ + dt + 1]),
                   wait=[tp] + extra)
                S.drain("scalar")
            for dt in range(NDT):
                sc(lambda e, dt=dt: e.activation(
                    out=B_md[:, CH * dt:CH * (dt + 1)],
                    in_=B_tmp[:, CH * dt:CH * (dt + 1)], func=AF.Ln))
            S.drain("scalar")
            tmd = S.tok("SC")
            ve(lambda e, dir_=dir_: e.scalar_tensor_tensor(
                out=B_dxp, in0=B_md, scalar=-1.0,
                in1=B_xt[:, 0:NDT * CH],
                op0=OP.mult, op1=OP.mult), wait=[tmd, t_xt])
            S.drain("vector")
            tdxp = S.tok("VE")
            if dbg:
                sto(dbg_out["md"][c, :, CH * NDT * dir_:CH * NDT * (dir_ + 1)],
                    B_md[:, :], wait=[tmd, S.tok("SG")], eng="sync", inc="SG")
                sto(dbg_out["xt"][c, :, CH * NDT * dir_:CH * NDT * (dir_ + 1)],
                    B_xt[:, 0:NDT * CH], wait=[t_xt, S.tok("SG")], eng="sync",
                    inc="SG")
            tok_hc_prev = None
            for dt in range(NDT):
                for g in range(NST // NG):
                    for j in range(NG):
                        n = NG * g + j
                        for off, srcbc in ((0, sb_bB), (2048, sb_bC)):
                            pe(lambda e, n=n, srcbc=srcbc, off=off, j=j:
                               e.matmul(
                                   ps[:, off + 512 * j:off + 512 * j + 512],
                                   sb_selb[:, 128 * n:128 * (n + 1)],
                                   srcbc[:, 0:CH],
                                   start=True, stop=True),
                               wait=txd +
                                    ([tok_hc_prev] if tok_hc_prev is not None
                                     and j == 0 and off == 0 else []) +
                                    ([S.tok("SC"), S.tok("VE"), S.tok("GP")]
                                     if tok_hc_prev is None and j == 0 and
                                     off == 0 else []))
                    tp = S.tok("PE")
                    for j in range(NG):
                        n = NG * g + j
                        sc(lambda e, n=n, j=j, dt=dt, dir_=dir_: e.activation(
                            out=B_da[:, CH * j:CH * (j + 1)],
                            in_=B_md[:, CH * dt:CH * (dt + 1)], func=AF.Exp,
                            scale=sb_apos[:, NST * (NDT * dir_ + dt) + n:
                                          NST * (NDT * dir_ + dt) + n + 1]),
                           wait=[tmd] +
                                ([tok_hc_prev] if tok_hc_prev is not None and
                                 j == 0 else []))
                    S.drain("scalar")
                    tda = S.tok("SC")
                    ve(lambda e, dt=dt: e.tensor_tensor(
                        out=B_dbu.rearrange("p (j t) -> p j t", t=CH),
                        in0=B_dxp[:, CH * dt:CH * (dt + 1)].unsqueeze(1)
                            .broadcast_to([128, NG, CH]),
                        in1=ps[:, 0:2048].rearrange("p (j t) -> p j t", t=CH),
                        op=OP.mult), wait=[tp, tdxp])
                    S.drain("vector")
                    for j in range(NG):
                        n = NG * g + j
                        col = NST * (NDT * dir_ + dt) + n
                        ve(lambda e, j=j, col=col: e.tensor_tensor_scan(
                            out=B_h[:, CH * j:CH * (j + 1)],
                            data0=B_da[:, CH * j:CH * (j + 1)],
                            data1=B_dbu[:, CH * j:CH * (j + 1)],
                            initial=sb_hin[:, col:col + 1],
                            op0=OP.mult, op1=OP.add), wait=[tda])
                    S.drain("vector")
                    for j in range(NG):
                        n = NG * g + j
                        col = NST * (NDT * dir_ + dt) + n
                        ve(lambda e, j=j, col=col: e.tensor_copy(
                            out=sb_hin[:, col:col + 1],
                            in_=B_h[:, CH * j + CH - 1:CH * (j + 1)]))
                    ve(lambda e: e.tensor_tensor(
                        out=B_hc, in0=B_h,
                        in1=ps[:, 2048:4096].rearrange("p (j t) -> p j t", t=CH),
                        op=OP.mult),
                       wait=([S.tok("Su")] if dt == 0 and g == 0 else []))
                    S.drain("vector")
                    tok_hc_prev = S.tok("VE")
                    thc = tok_hc_prev
                    if g == 0:
                        ve(lambda e, dt=dt: e.tensor_reduce(
                            out=B_y[:, CH * dt:CH * (dt + 1)],
                            in_=rap(B_hc, 0, [[16384, 128], [1, CH], [CH, NG]]),
                            axis=AX, op=OP.add),
                           wait=[thc, S.tok("GP")] +
                                ([S.tok("SG")] if dbg else []))
                        S.drain("vector")
                    else:
                        ve(lambda e: e.tensor_reduce(
                            out=sb_sq[:, 0:CH],
                            in_=rap(B_hc, 0, [[16384, 128], [1, CH], [CH, NG]]),
                            axis=AX, op=OP.add), wait=[thc, S.tok("GP")])
                        S.drain("vector")
                        tvr = S.tok("VE")
                        gp(lambda e, dt=dt: e.tensor_tensor(
                            out=B_y[:, CH * dt:CH * (dt + 1)],
                            in0=B_y[:, CH * dt:CH * (dt + 1)],
                            in1=sb_sq[:, 0:CH], op=OP.add), wait=[tvr])
                        S.drain("gpsimd")
                tydt = S.tok("GP")
                ve(lambda e, dt=dt, dir_=dir_: e.scalar_tensor_tensor(
                    out=B_y[:, CH * dt:CH * (dt + 1)],
                    in0=B_xt[:, CH * dt:CH * (dt + 1)],
                    scalar=sb_dvec[:, NDT * dir_ + dt:NDT * dir_ + dt + 1],
                    in1=B_y[:, CH * dt:CH * (dt + 1)],
                    op0=OP.mult, op1=OP.add), wait=[tydt])
                S.drain("vector")
            tyy = S.tok("VE")
            if dbg:
                sto(dbg_out["y"][c, :, CH * NDT * dir_:CH * NDT * (dir_ + 1)],
                    B_y[:, 0:NDT * CH], wait=[tyy, S.tok("SG")], eng="sync",
                    inc="SG")
            sc(lambda e, ci=ci: e.activation(
                out=B_sz[:, 0:NDT * CH],
                in_=B_xz[ci][:, NDT * CH:2 * NDT * CH], func=AF.Silu),
               wait=[t_xz, tyy, S.tok("Sxn")])
            S.drain("scalar")
            tsz = S.tok("SC")
            if dir_ == 0:
                ve(lambda e: e.tensor_tensor(out=B_u1, in0=B_y[:, 0:NDT * CH],
                                             in1=B_sz[:, 0:NDT * CH], op=OP.mult),
                   wait=[tsz, tyy, S.tok("GP"), S.tok("Su")])
            else:
                ve(lambda e: e.tensor_tensor(
                    out=B_u1, in0=B_y[:, 0:NDT * CH],
                    in1=rap(B_sz, CH - 1, [[20480, 128], [CH, NDT], [-1, CH]]),
                    op=OP.mult), wait=[tsz, tyy, S.tok("GP"), S.tok("Su")])
            S.drain("vector")
            tu = S.tok("VE")
            if pass_ == 0:
                sto(uf_dram[c, :, :], B_u1[:, :], wait=[tu], eng="sync", inc="Su")
            else:
                sto(ub_dram[s_, :, :], B_u1[:, :], wait=[tu], eng="sync",
                    inc="Su")

    tB_st = [S.tok("Su")]

    for c in (range(NCH) if not ffn_only else ()):
        buf = c % 2
        t1 = dma(C_ua[:, :], uf_dram[c, :, :],
                 wait=tB_st + [S.tok("VE"), S.tok("Sag")] +
                      ([S.tok("SG")] if dbg else []),
                 inc="Luf")
        t2 = dma(C_ub[:, :], ub_dram[NCH - 1 - c, :, :], wait=tB_st, inc="Lub")
        ve(lambda e: e.tensor_tensor(
            out=C_ua[:, :], in0=C_ua[:, :],
            in1=rap(C_ub, CH - 1, [[20480, 128], [CH, NDT], [-1, CH]]),
            op=OP.add), wait=[t1, t2])
        S.drain("vector")
        tcmb = S.tok("VE")
        tst = sto(agi_d[buf].ap().rearrange("(d p) t -> p d t", p=128),
                  C_ua[:, :].rearrange("p (d t) -> p d t", t=CH),
                  wait=[tcmb, S.tok("CC")], eng="sync", inc="Sag")
        S.op("gpsimd", (lambda b, c: (lambda e: e.collective_compute(
            "AllGather", OP.bypass, replica_groups=groups,
            ins=[agi_d[b].ap().opt()],
            outs=[ago_d[c, :, :].opt()])))(buf, c),
            wait=[tst], inc="CC", k=1)
    tC1 = S.tok("CC")

    if mixer_only:
        CHJ = min(CH, OWN)
        TJ = CHJ // 128
        twout = dma(C_wout[:, :], w_out,
                    wait=[S.tok("SC"), S.tok("VE"), S.tok("PE")], inc="Wout")

        def gp_agl_m(j):
            def f(e):
                r = e.alloc_register()
                e.reg_load(r, sb_qm[0:1, 4:5])
                if j:
                    e.reg_add(r, r, j * (4 * DQ) * CH)
                return e.dma_start(
                    out=C_agl[:, :].rearrange("p (k t) -> p k t", t=CHJ)
                        [:, 0:16, :],
                    in_=bass.AP(ago_d.ap().tensor, r,
                                [[CH, 128], [128 * CH, 16], [1, CHJ]]))
            return f

        for j in range(OWN // CHJ):
            S.op("gpsimd", gp_agl_m(j), wait=[tC1, S.tok("PE"), S.tok("SC")],
                 inc="Ga", k=16)
            tagl = S.tok("Ga")
            for m in range(8):
                bank = ps_a if m % 2 == 0 else ps_b
                for kc in range(16):
                    pe((lambda m, kc, bank: (lambda e: e.matmul(
                        bank[:, 0:CHJ],
                        C_wout[:, 1024 * kc + 128 * m:1024 * kc + 128 * (m + 1)],
                        C_agl[:, CHJ * kc:CHJ * (kc + 1)],
                        start=(kc == 0), stop=(kc == 15))))(m, kc, bank),
                       wait=[twout, tagl] + ([S.tok("SC")] if kc == 0 else []))
                tp = S.tok("PE")
                sc((lambda m, bank: (lambda e: e.mul(
                    out=C_sfm[:, CHJ * m:CHJ * (m + 1)],
                    in_=bank[:, 0:CHJ], mul=0.5)))(m, bank),
                   wait=[tp])
            S.drain("scalar")
            tsf = S.tok("SC")
            for tt in range(TJ):
                for kc in range(8):
                    pe((lambda tt, kc: (lambda e: e.transpose(
                        ps_t[:, 128 * (kc % 4):128 * (kc % 4) + 128],
                        C_sfm[:, CHJ * kc + 128 * tt:CHJ * kc + 128 * (tt + 1)],
                        sb_id[:, :])))(tt, kc),
                       wait=[tsf] + ([S.tok("SC")] if kc % 4 == 0 else []))
                    if kc % 4 == 3:
                        tp = S.tok("PE")
                        sc((lambda tt, kc: (lambda e: e.copy(
                            out=C_stm.rearrange("p (t d) -> p t d", d=DM)
                                [:, tt:tt + 1, 128 * (kc - 3):128 * (kc + 1)],
                            in_=ps_t[:, 0:512].unsqueeze(1))))(tt, kc),
                           wait=[tp] + ([S.tok("SoM")] if tt == 0 else []))
            S.drain("scalar")
            tstm = S.tok("SC")
            sto(out[CHJ * j:CHJ * (j + 1), :].rearrange("(t p) d -> p t d", p=128),
                C_stm.rearrange("p (t d) -> p t d", d=DM)[:, 0:TJ, :],
                wait=[tstm], eng="sync", inc="SoM")
        S.op("sync", lambda e: e.nop(), wait=S.all_toks())
        nc._dbg_out = dbg_out
        return _finish(nc, S, mybir)

    REGS = {}

    def regs_load(ename, mult):
        def f(e):
            for t in range(NTILE):
                r = e.alloc_register()
                e.reg_load(r, sb_etil_i[0:1, t:t + 1])
                if mult != 1:
                    e.reg_mul(r, r, mult)
                REGS[(ename, t)] = r
            rtmp = e.alloc_register()
            REGS[(ename, "tmp")] = rtmp
            return None
        return f

    def gp_xro(e):
        r = e.alloc_register()
        e.reg_load(r, sb_qm[0:1, 3:4])
        return e.dma_start(
            out=D_xro.rearrange("p (c t) -> p c t", t=CH),
            in_=bass.AP(xr_dram.ap().tensor, r,
                        [[CH, 16], [16 * CH, OWN // CH], [1, CH]]))

    S.op("gpsimd", gp_xro,
         wait=tA + tB_st + [S.tok("Sxr"), S.tok("SC"), S.tok("VE"), S.tok("PE")],
         inc="Gx", k=16)
    txro = S.tok("Gx")

    for tt in range(NOTL):
        pe(lambda e, tt=tt: e.matmul(
            ps_s[:, 8 * tt:8 * (tt + 1)],
            D_xro[:, 128 * tt:128 * (tt + 1)], sb_wguT[:, :],
            start=True, stop=True),
           wait=[txro, tA_done] + ([S.tok("SC")] if tt == 0 else []))
    tp = S.tok("PE")
    sc(lambda e: e.copy(out=sb_logit[:, :], in_=ps_s[:, 0:8 * NOTL]), wait=[tp])
    S.drain("scalar")
    tlg = S.tok("SC")
    for tt in range(NOTL):
        ve(lambda e, tt=tt: e.reduce_max(sb_mneg[:, tt:tt + 1],
                                         sb_logit[:, 8 * tt:8 * (tt + 1)],
                                         axis=AX, negate=True), wait=[tlg])
    S.drain("vector")
    tve = S.tok("VE")
    for tt in range(NOTL):
        sc(lambda e, tt=tt: e.activation(
            out=sb_sq[:, 0:8], in_=sb_logit[:, 8 * tt:8 * (tt + 1)], func=AF.Exp,
            bias=sb_mneg[:, tt:tt + 1], accum_out=sb_ssum[:, tt:tt + 1]),
           wait=[tve])
        S.drain("scalar")
    tsc = S.tok("SC")
    ve(lambda e: e.reciprocal(out=sb_score[:, :], in_=sb_ssum[:, :]), wait=[tsc])
    ve(lambda e: e.tensor_scalar(out=sb_mpos[:, :], in0=sb_mneg[:, :], scalar1=-1.0,
                                 scalar2=None, op0=OP.mult))
    S.drain("vector")
    for tt in range(NOTL):
        ve(lambda e, tt=tt: e.tensor_scalar(
            out=sb_mtmp[:, :], in0=sb_logit[:, 8 * tt:8 * (tt + 1)],
            scalar1=sb_mpos[:, tt:tt + 1], scalar2=None, op0=OP.is_equal))
        S.drain("vector")
        ve(lambda e: e.tensor_tensor(out=sb_mtmp[:, :], in0=sb_mtmp[:, :],
                                     in1=sb_i8m[:, :], op=OP.mult))
        S.drain("vector")
        ve(lambda e, tt=tt: e.tensor_reduce(out=sb_idx[:, tt:tt + 1],
                                            in_=sb_mtmp[:, :], axis=AX, op=OP.min))
        S.drain("vector")
    ve(lambda e: e.tensor_scalar(out=sb_idx[:, :], in0=sb_idx[:, :], scalar1=BIGV,
                                 scalar2=None, op0=OP.add))
    S.drain("vector")
    tidx = S.tok("VE")
    if dbg:
        sto(dbg_out["score"], sb_score[:, :], wait=[tidx, S.tok("SG")],
            eng="sync", inc="SG")
        sto(dbg_out["idx"], sb_idx[:, :], wait=[tidx, S.tok("SG")], eng="sync",
            inc="SG")

    assert NOTL == 16
    pe(lambda e: e.transpose(ps_t[0:16, 0:128], sb_idx[:, :], sb_id[:, :]),
       wait=[tidx, S.tok("SC")])
    tp = S.tok("PE")
    sc(lambda e: e.copy(out=sb_idxT[:, :], in_=ps_t[0:16, 0:128]), wait=[tp])
    S.drain("scalar")
    tsc = S.tok("SC")
    pe(lambda e: e.matmul(ps_t[:, 0:128], sb_sel16[:, :], sb_idxT[:, :],
                          start=True, stop=True), wait=[tsc] + tA)
    tp = S.tok("PE")
    ve(lambda e: e.tensor_scalar(out=sb_mask2[:, :], in0=ps_t[:, 0:128],
                                 scalar1=sb_iotaE[:, 0:1], scalar2=None,
                                 op0=OP.is_equal), wait=[tp])
    S.drain("vector")
    ve(lambda e: e.tensor_tensor_scan(out=sb_cum2[:, :], data0=sb_mask2[:, :],
                                      data1=sb_zer[:, 0:128], initial=0.0,
                                      op0=OP.add, op1=OP.add))
    S.drain("vector")
    ve(lambda e: e.tensor_copy(out=sb_cnt2[:, :], in_=sb_cum2[:, 127:128]))
    S.drain("vector")
    tve = S.tok("VE")
    pe(lambda e: e.transpose(ps_t[0:1, 0:128], sb_cnt2[:, :], sb_id[:, :]),
       wait=[tve, S.tok("SC")])
    tp = S.tok("PE")
    sc(lambda e: e.copy(out=sb_crow[:, :], in_=ps_t[0:1, 0:128]), wait=[tp])
    S.drain("scalar")
    tsc = S.tok("SC")
    ve(lambda e: e.tensor_tensor_scan(out=sb_kin[:, :], data0=sb_colk0[:, :],
                                      data1=sb_crow[:, :], initial=0.0,
                                      op0=OP.mult, op1=OP.add), wait=[tsc])
    S.drain("vector")
    ve(lambda e: e.tensor_tensor(out=sb_kex[:, :], in0=sb_kin[:, :],
                                 in1=sb_crow[:, :], op=OP.subtract))
    ve(lambda e: e.tensor_copy(out=sb_r8[:, :],
                               in_=rap(sb_kin, 15, [[16384, 1], [16, 8]])))
    S.drain("vector")
    tve = S.tok("VE")
    pe(lambda e: e.transpose(ps_t[0:8, 0:1], sb_r8[:, :], sb_id[0:1, 0:1]),
       wait=[tve, S.tok("SC")])
    tp = S.tok("PE")
    sc(lambda e: e.copy(out=sb_g8[:, :], in_=ps_t[0:8, 0:1]), wait=[tp])
    S.drain("scalar")
    tsc = S.tok("SC")
    ve(lambda e: e.tensor_scalar(out=sb_p8c[:, :], in0=sb_iPAD[:, :],
                                 scalar1=sb_g8[:, 0:1], scalar2=None, op0=OP.is_lt),
       wait=[tsc])
    S.drain("vector")
    ve(lambda e: e.reduce_sum(sb_p8[:, :], sb_p8c[:, :], axis=AX))
    S.drain("vector")
    ve(lambda e: e.tensor_scalar(out=sb_p8[:, :], in0=sb_p8[:, :], scalar1=128.0,
                                 scalar2=None, op0=OP.mult))
    S.drain("vector")
    tve = S.tok("VE")
    pe(lambda e: e.transpose(ps_t[0:1, 0:8], sb_p8[:, :], sb_id[0:8, 0:8]),
       wait=[tve, S.tok("SC")])
    tp = S.tok("PE")
    sc(lambda e: e.copy(out=sb_r8[:, :], in_=ps_t[0:1, 0:8]), wait=[tp])
    S.drain("scalar")
    tsc = S.tok("SC")
    ve(lambda e: e.tensor_tensor_scan(out=sb_eb[:, :], data0=sb_ones[0:1, 0:8],
                                      data1=sb_r8[:, :], initial=0.0,
                                      op0=OP.mult, op1=OP.add), wait=[tsc])
    S.drain("vector")
    ve(lambda e: e.tensor_tensor(out=sb_eb[:, :], in0=sb_eb[:, :], in1=sb_r8[:, :],
                                 op=OP.subtract))
    S.drain("vector")
    tve = S.tok("VE")
    pe(lambda e: e.transpose(ps_t[0:8, 0:1], sb_eb[:, :], sb_id[0:1, 0:1]),
       wait=[tve, S.tok("SC")])
    tp = S.tok("PE")
    sc(lambda e: e.copy(out=sb_eb8[:, :], in_=ps_t[0:8, 0:1]), wait=[tp])
    S.drain("scalar")
    tsc8 = S.tok("SC")
    ve(lambda e: e.tensor_tensor(
        out=sb_kin[:, :], in0=sb_kex[:, :],
        in1=rap(sb_eb, 0, [[8, 1], [1, 8], [0, 16]]),
        op=OP.add), wait=[tve])
    S.drain("vector")
    tve = S.tok("VE")
    pe(lambda e: e.transpose(ps_t[:, 0:1], sb_kin[:, :], sb_id[0:1, 0:1]),
       wait=[tve, S.tok("SC")])
    tp = S.tok("PE")
    sc(lambda e: e.copy(out=sb_base2[:, :], in_=ps_t[:, 0:1]), wait=[tp])
    S.drain("scalar")
    tsc = S.tok("SC")
    ve(lambda e: e.tensor_scalar(out=sb_stt2[:, :], in0=sb_cum2[:, :],
                                 scalar1=sb_base2[:, 0:1], scalar2=-1.0,
                                 op0=OP.add, op1=OP.add), wait=[tsc])
    S.drain("vector")
    ve(lambda e: e.tensor_tensor(out=sb_stt2[:, :], in0=sb_stt2[:, :],
                                 in1=sb_mask2[:, :], op=OP.mult))
    S.drain("vector")
    tve = S.tok("VE")
    pe(lambda e: e.matmul(ps_t[0:16, 0:128], sb_sel2[:, :], sb_stt2[:, :],
                          start=True, stop=True), wait=[tve, S.tok("SC")])
    tp = S.tok("PE")
    sc(lambda e: e.copy(out=sb_sidxT16[0:16, 0:128], in_=ps_t[0:16, 0:128]),
       wait=[tp])
    S.drain("scalar")
    tsidxT = S.tok("SC")
    ROW16 = 2 * 128
    for cl in range(16):
        S.op("sync",
             (lambda d, s: (lambda e: e.dma_start(
                 out=d, in_=s, allow_slow_non_contiguous=True)))(
                 rap(sb_sidx, cl * ROW16, [[ROW16, 1], [8, 16], [1, 8]]),
                 rap(sb_sidxT16, cl, [[ROW16, 16], [16, 8]])),
             wait=[tsidxT, S.tok("Pm")] + ([S.tok("GD")] if cl == 0 else []),
             inc="Pm", k=16)
    tsidx = S.tok("Pm")
    if dbg:
        ve(lambda e: e.tensor_copy(out=sb_sq[0:16, 0:128], in_=sb_sidx[0:16, 0:128]),
           wait=[tsidx])
        S.drain("vector")
        tvd = S.tok("VE")
        sto(dbg_out["destT"], sb_sq[0:16, 0:128], wait=[tvd, S.tok("SG")],
            eng="sync", inc="SG")
    ve(lambda e: e.tensor_scalar(out=sb_et8[:, :], in0=sb_iTILE[:, :],
                                 scalar1=sb_eb8[:, 0:1], scalar2=None, op0=OP.is_ge),
       wait=[tsc8])
    S.drain("vector")
    tve = S.tok("VE")
    pe(lambda e: e.matmul(ps_t[0:1, 0:NTILE], sb_ones8[:, :], sb_et8[:, :],
                          start=True, stop=True), wait=[tve, S.tok("SC")])
    tp = S.tok("PE")
    sc(lambda e: e.add(out=sb_etil[:, :], in_=ps_t[0:1, 0:NTILE], add=-1.0),
       wait=[tp])
    S.drain("scalar")
    sc(lambda e: e.copy(out=sb_etil_i[:, :], in_=sb_etil[:, :]))
    S.drain("scalar")
    tet = S.tok("SC")
    if dbg:
        sto(dbg_out["etile"], sb_etil[:, :], wait=[tet, S.tok("SG")],
            eng="sync", inc="SG")
    for k in range(1, 8):
        dma(sb_sidx[16 * k:16 * (k + 1), :], sb_sidx[0:16, :],
            wait=[tsidx, S.tok("Rp")], eng="gpsimd", inc="Rp")
    tsrep = [S.tok("Rp")]

    S.op("gpsimd", lambda e: e.load_library(_LIB_MLP), wait=[S.tok("GP")])
    tzero = []
    zoff = 0
    zi = 0
    while zoff < NS:
        zrows = min(512, NS - zoff)
        tzero = [dma(
            xg_dram[zoff:zoff + zrows, :].rearrange("(a p) d -> p a d", p=128),
            zer_dram[:, 0:8 * zrows].rearrange("p (a d) -> p a d", d=DM),
            wait=tzf + [S.tok("ZG")], inc="ZG")]
        zoff += zrows
        zi += 1

    def gp_ssrc(cg, bb):
        def f(e):
            r = e.alloc_register()
            e.reg_load(r, sb_qm[0:1, 0:1])
            if cg or bb:
                e.reg_add(r, r, (512 * cg + 128 * bb) * DM)
            return e.dma_start(
                out=rap(D_ssrc, 4096 * cg + 1024 * bb,
                        [[20480, 128], [1, DM]]),
                in_=bass.AP(xn_dram.ap().tensor, r,
                            [[1, 128 * DM]]).rearrange("(p d) -> p d", d=DM))
        return f

    for cg in range(NGC4):
        for bb in range(4):
            S.op("gpsimd", gp_ssrc(cg, bb),
                 wait=(tB_st + [S.tok("Sxn"), S.tok("VE"), S.tok("SC"),
                                S.tok("CC")] if bb == 0 else []) +
                      [S.tok(f"Gs{cg}")],
                 inc=f"Gs{cg}", k=16)
    tssrc = [S.tok(f"Gs{cg}") for cg in range(NGC4)]
    for cg in range(NGC4):
        S.op("gpsimd", (lambda cg: (lambda e: e.dma_scatter_add(
            xg_dram.ap(),
            D_ssrc[:, 4096 * cg:4096 * (cg + 1)].rearrange("p (b d) -> p b d", d=DM),
            sb_sidx[:, 32 * cg:32 * (cg + 1)], 512, 512, DM)))(cg),
            wait=tssrc + tzero + tsrep, inc=f"Sc{cg}", k=16)
    tscat = [S.tok(f"Sc{cg}") for cg in range(NGC4)]

    for t in range(NTILE):
        tld = dma(D_xg[:, :], xg_dram[128 * t:128 * (t + 1), :],
                  wait=tscat + [S.tok("SC")] + ([S.tok("VE")] if t == 0 else []),
                  inc="Lxg")
        for kc in range(8):
            pe(lambda e, kc=kc: e.transpose(
                ps_x[:, 128 * kc:128 * (kc + 1)],
                D_xg[:, 128 * kc:128 * (kc + 1)], sb_id[:, :]),
               wait=[tld] + ([S.tok("SC")] if kc == 0 else []))
        tp = S.tok("PE")
        for kc in range(8):
            sc(lambda e, kc=kc, t=t: e.copy(
                out=xgf_blk(kc * NT128 + t * 128),
                in_=ps_x[:, 128 * kc:128 * (kc + 1)]), wait=[tp])
        S.drain("scalar")
    txgf = S.tok("SC")

    S.op("tensor", regs_load("tensor", 128), wait=[tet])

    w1v = w1_d.rearrange("e p c -> p e c")
    for mc in range(32):
        tw = []
        for kc in range(8):
            tw.append(dma(D_w1b[:, 1024 * kc:1024 * (kc + 1)]
                          .rearrange("p (e c) -> p e c", c=128),
                          w1v[:, :, 4096 * kc + 128 * mc:4096 * kc + 128 * (mc + 1)],
                          wait=([S.tok("PE")] +
                                ([S.tok("SC"), S.tok("VE"), S.tok("GP"),
                                  S.tok("SG")]
                                 if mc == 0 else [])) if kc == 0 else (),
                          inc=f"W{kc}"))
        tb1 = dma(sb_b1r, b1h_in[mc:mc + 1, :],
                  wait=[S.tok("PE")] +
                       ([S.tok("SC"), S.tok("VE"), S.tok("GP")] if mc == 0
                        else []), inc="B1")
        for t in range(NTILE):
            for kc in range(8):
                def mm1(t, kc):
                    def f(e):
                        rt = REGS[("tensor", "tmp")]
                        e.reg_add(rt, REGS[("tensor", t)],
                                  D_w1b.offset + 1024 * kc)
                        return e.matmul(
                            ps_f[:, 128 * t:128 * (t + 1)],
                            xgf_blk(kc * NT128 + t * 128),
                            bass.AP(D_w1b.tensor, rt, [[16384, 128], [1, 128]]),
                            start=(kc == 0), stop=False)
                    return f
                pe(mm1(t, kc),
                   wait=[tw[kc], txgf] + ([S.tok("SC")] if t == 0 and kc == 0 else []))
            def mb1(t):
                def f(e):
                    rt = REGS[("tensor", "tmp")]
                    e.reg_add(rt, REGS[("tensor", t)], B1R_OFF)
                    return e.matmul(
                        ps_f[:, 128 * t:128 * (t + 1)], sb_ones[0:1, 0:128],
                        bass.AP(ar1, rt, [[16384, 1], [1, 128]]),
                        start=False, stop=True)
                return f
            pe(mb1(t), wait=[tb1])
        for t in range(NTILE):
            tp = S.tok("PE")
            sc((lambda t: (lambda e: e.copy(
                out=D_t1b[:, 128 * (t % 2):128 * (t % 2 + 1)],
                in_=ps_f[:, 128 * t:128 * (t + 1)])))(t),
               wait=[tp] + ([S.tok("Sh")] if t in (0, 1) else []))
            S.drain("scalar")
            tev = S.tok("SC")
            pe((lambda t: (lambda e: e.transpose(
                ps_x[:, 128 * (t % 2):128 * (t % 2 + 1)],
                D_t1b[:, 128 * (t % 2):128 * (t % 2 + 1)], sb_id[:, :])))(t),
               wait=[tev])
            tpx = S.tok("PE")
            sc((lambda t: (lambda e: e.activation(
                out=D_hst[:, 128 * t:128 * (t + 1)],
                in_=ps_x[:, 128 * (t % 2):128 * (t % 2 + 1)],
                func=AF.Gelu)))(t), wait=[tpx, S.tok("Sh")])
        S.drain("scalar")
        tsc = S.tok("SC")
        sto(h_dram[mc, :, :], D_hst[:, :], wait=[tsc], eng="sync", inc="Sh")
    tS1 = S.tok("Sh")

    w2v = w2_d.rearrange("e p c -> p e c")
    nsup = (32 + SUP - 1) // SUP
    for sup in range(nsup):
        kcs = list(range(SUP * sup, min(SUP * (sup + 1), 32)))
        tld = []
        for i, kc2 in enumerate(kcs):
            tld.append(dma(D_h3[:, NT128 * i:NT128 * (i + 1)], h_dram[kc2, :, :],
                           wait=[tS1, S.tok("PE")] if i == 0 else [tS1],
                           inc=f"H{i}"))
        for mc2 in range(8):
            tw = []
            for i, kc2 in enumerate(kcs):
                tw.append(dma(D_w2b[:, 1024 * i:1024 * (i + 1)]
                              .rearrange("p (e c) -> p e c", c=128),
                              w2v[:, :, 1024 * kc2 + 128 * mc2:
                                  1024 * kc2 + 128 * (mc2 + 1)],
                              wait=[S.tok("PE")] if i == 0 else (), inc=f"V{i}"))
            if sup == 0:
                tb2 = dma(sb_b2r, b2h_in[mc2:mc2 + 1, :],
                          wait=[S.tok("PE")], inc="B2")
            for t in range(NTILE):
                if sup == 0:
                    def mb2(t):
                        def f(e):
                            return e.matmul(
                                ps_f[:, 128 * t:128 * (t + 1)],
                                sb_ones[0:1, 0:128],
                                bass.AP(sb_xdbl, REGS[("tensor", t)],
                                        [[4 * CH, 1], [1, 128]]),
                                start=True, stop=False)
                        return f
                    pe(mb2(t), wait=[tb2] +
                       ([S.tok("VE")] if t == 0 else []))
                for i in range(len(kcs)):
                    def mm2(t, i, last, st):
                        def f(e):
                            rt = REGS[("tensor", "tmp")]
                            e.reg_add(rt, REGS[("tensor", t)],
                                      D_w2b.offset + 1024 * i)
                            return e.matmul(
                                ps_f[:, 128 * t:128 * (t + 1)],
                                D_h3[:, NT128 * i + 128 * t:
                                     NT128 * i + 128 * (t + 1)],
                                bass.AP(D_w2b.tensor, rt,
                                        [[16384, 128], [1, 128]]),
                                start=st, stop=last)
                        return f
                    pe(mm2(t, i, i == len(kcs) - 1, sup != 0 and i == 0),
                       wait=[tw[i], tld[i]] +
                            ([S.tok("VE")] if sup != 0 and i == 0 and t == 0
                             else []))
            tp = S.tok("PE")
            if sup == 0:
                ve((lambda mc2: (lambda e: e.tensor_copy(
                    out=acc_blk(mc2 * NT128, NT128), in_=ps_f[:, 0:NT128])))(mc2),
                   wait=[tp])
            else:
                ve((lambda mc2: (lambda e: e.scalar_tensor_tensor(
                    out=acc_blk(mc2 * NT128, NT128), in0=ps_f[:, 0:NT128],
                    scalar=1.0, in1=acc_blk(mc2 * NT128, NT128),
                    op0=OP.mult, op1=OP.add)))(mc2), wait=[tp])
            S.drain("vector")
    tS2 = S.tok("VE")

    for t in range(NTILE):
        dst = so_dram[128 * t:128 * (t + 1), :]
        sto(dst[:, 0:768].rearrange("s (b d) -> s b d", d=128),
            bass.AP(ar2, 128 * t, [[20480, 128], [NT128, 6], [1, 128]]),
            wait=[tS2, S.tok("So")], eng="sync", inc="So")
        sto(dst[:, 768:1024].rearrange("s (b d) -> s b d", d=128),
            bass.AP(sb_winp, 2048 + 128 * t, [[8192, 128], [NT128, 2], [1, 128]]),
            wait=[tS2, S.tok("So")], eng="sync", inc="So")
    tSO = S.tok("So")

    for cg in range(NGC4):
        S.op("gpsimd", (lambda cg: (lambda e: e.dma_gather(
            D_ga[:, 4096 * cg:4096 * (cg + 1)].rearrange("p (b d) -> p b d", d=DM),
            so_dram.ap(), sb_sidx[:, 32 * cg:32 * (cg + 1)], 512, 512, DM,
            elem_step=DM)))(cg),
            wait=[tSO] + tsrep + [S.tok("VE")], inc=f"Gg{cg}", k=16)
    tga = [S.tok(f"Gg{cg}") for cg in range(NGC4)]
    dma(g_dram.ap(), D_ga[:, :].rearrange("p (b d) -> p b d", d=DM), wait=tga,
        eng="gpsimd", inc="Sgd")
    tgd = S.tok("Sgd")

    twout = dma(C_wout[:, :], w_out, wait=[tS2, S.tok("SC"), S.tok("VE"), tgd],
                inc="Wout")

    def gp_agl(j):
        def f(e):
            r = e.alloc_register()
            e.reg_load(r, sb_qm[0:1, 4:5])
            if j:
                e.reg_add(r, r, j * (4 * DQ) * CH)
            return e.dma_start(
                out=C_agl[:, :].rearrange("p (k t) -> p k t", t=CH),
                in_=bass.AP(ago_d.ap().tensor, r,
                            [[CH, 128], [128 * CH, 16], [1, CH]]))
        return f

    for j in range(NGC4):
        S.op("gpsimd", gp_agl(j), wait=[tC1, tgd, S.tok("PE"), S.tok("SC")],
             inc="Ga", k=16)
        tagl = S.tok("Ga")
        for m in range(8):
            bank = ps_a if m % 2 == 0 else ps_b
            for kc in range(16):
                pe((lambda m, kc, bank: (lambda e: e.matmul(
                    bank, C_wout[:, 1024 * kc + 128 * m:1024 * kc + 128 * (m + 1)],
                    C_agl[:, CH * kc:CH * (kc + 1)],
                    start=(kc == 0), stop=(kc == 15))))(m, kc, bank),
                   wait=[twout, tagl] + ([S.tok("SC")] if kc == 0 else []))
            tp = S.tok("PE")
            sc((lambda m, bank: (lambda e: e.mul(out=C_sfm[:, CH * m:CH * (m + 1)],
                                                 in_=bank, mul=0.5)))(m, bank),
               wait=[tp])
        S.drain("scalar")
        tsf = S.tok("SC")
        for tt in range(4):
            for kc in range(8):
                pe((lambda tt, kc: (lambda e: e.transpose(
                    ps_t[:, 128 * (kc % 4):128 * (kc % 4) + 128],
                    C_sfm[:, CH * kc + 128 * tt:CH * kc + 128 * (tt + 1)],
                    sb_id[:, :])))(tt, kc),
                   wait=[tsf] + ([S.tok("SC")] if kc % 4 == 0 else []))
                if kc % 4 == 3:
                    tp = S.tok("PE")
                    sc((lambda tt, kc: (lambda e: e.copy(
                        out=C_stm.rearrange("p (t d) -> p t d", d=DM)
                            [:, tt:tt + 1, 128 * (kc - 3):128 * (kc + 1)],
                        in_=ps_t[:, 0:512].unsqueeze(1))))(tt, kc),
                       wait=[tp] + ([S.tok("Ss")] if tt == 0 else []))
        S.drain("scalar")
        tstm = S.tok("SC")
        sto(s_dram[CH * j:CH * (j + 1), :].rearrange("(t p) d -> p t d", p=128),
            C_stm.rearrange("p (t d) -> p t d", d=DM), wait=[tstm], eng="sync",
            inc="Ss")
    tC2 = S.tok("Ss")

    def gp_eg(tt):
        def f(e):
            return e.dma_start(
                out=E_g[:, :],
                in_=bass.AP(g_dram.ap().tensor, tt * DM,
                            [[NOTL * DM, 128], [1, DM]]))
        return f

    def gp_ex(tt):
        def f(e):
            r = e.alloc_register()
            e.reg_load(r, sb_qm[0:1, 0:1])
            if tt:
                e.reg_add(r, r, tt * 128 * DM)
            return e.dma_start(out=E_x[:, :],
                               in_=bass.AP(x_in.tensor, r, [[DM, 128], [1, DM]]))
        return f

    for tt in range(NOTL):
        S.op("gpsimd", gp_eg(tt),
             wait=[tgd, tC2, S.tok("VE"), S.tok("SC")], inc="Eg", k=16)
        S.op("gpsimd", gp_ex(tt), wait=[S.tok("VE")], inc="Ex", k=16)
        tEg = S.tok("Eg")
        tEx = S.tok("Ex")
        t2 = dma(E_s[:, :], s_dram[128 * tt:128 * (tt + 1), :],
                 wait=[tC2, S.tok("VE")], inc="Ls")
        ve((lambda tt: (lambda e: e.tensor_scalar(
            out=E_t1[:, :], in0=E_g[:, :], scalar1=sb_score[:, tt:tt + 1],
            scalar2=None, op0=OP.mult)))(tt), wait=[tEg, tidx, S.tok("Sout")])
        S.drain("vector")
        tve = S.tok("VE")
        sc(lambda e: e.activation(out=E_g[:, :], in_=E_t1[:, :], func=AF.Silu),
           wait=[tve])
        S.drain("scalar")
        tsc = S.tok("SC")
        ve(lambda e: e.tensor_tensor(out=E_t1[:, :], in0=E_s[:, :], in1=E_g[:, :],
                                     op=OP.mult), wait=[tsc, t2])
        S.drain("vector")
        ve(lambda e: e.tensor_tensor(out=E_t1[:, :], in0=E_t1[:, :], in1=E_x[:, :],
                                     op=OP.add), wait=[tEx])
        S.drain("vector")
        tve = S.tok("VE")
        sto(out[128 * tt:128 * (tt + 1), :], E_t1[:, :], wait=[tve], eng="sync",
            inc="Sout")
    S.op("sync", lambda e: e.nop(), wait=S.all_toks())

    nc._dbg_out = dbg_out
    return _finish(nc, S, mybir)


def _finish(nc, S, _mybir):
    global _SEMS
    sems = {k: nc.alloc_semaphore(name="s" + k) for k in S.cnt}
    _SEMS = sems
    with nc.Block() as blk:

        @blk.sync
        def _(e):
            S.emit("sync", e, sems)

        @blk.tensor
        def _(e):
            S.emit("tensor", e, sems)

        @blk.vector
        def _(e):
            S.emit("vector", e, sems)

        @blk.scalar
        def _(e):
            S.emit("scalar", e, sems)

        @blk.gpsimd
        def _(e):
            S.emit("gpsimd", e, sems)

    _bass_rust.alloc_regs(nc)
    _bass_rust.fuse_regops(nc)
    _bass_rust.replace_nops_with_events(nc)
    _bass_rust.fuse_nops(nc.main_func)
    _bass_rust.remove_dead_nops(nc.main_func)
    _mybir.codegen_inst_isa_subclasses(nc)
    return nc


_SEMS = {}


def _prep_inputs(ins, L=8192):
    x = np.ascontiguousarray(np.asarray(ins["L0_attn_hidden_states"], np.float32))
    inproj = np.asarray(ins["L0_attn_in_proj_w"], np.float32)
    outproj = np.asarray(ins["L0_attn_out_proj_w"], np.float32)
    OWN = L // 4
    NS = OWN + EF * 128
    NTILE = NS // 128
    CAPIO = OWN // 128 + 1

    def lhsT_tile(w, K, M):
        t = np.ascontiguousarray(w.T)
        return np.ascontiguousarray(
            t.reshape(K // 128, 128, M // 128, 128).transpose(1, 0, 2, 3)
            .reshape(128, -1))

    w1 = np.asarray(ins["L0_ffn_w1"], np.float32)
    w2 = np.asarray(ins["L0_ffn_w2"], np.float32)
    w1h = np.stack([
        np.ascontiguousarray(w1[e].reshape(8, 128, 32, 128).transpose(1, 0, 2, 3)
                             .reshape(128, -1)) for e in range(EF)])
    w2h = np.stack([
        np.ascontiguousarray(w2[e].reshape(32, 128, 8, 128).transpose(1, 0, 2, 3)
                             .reshape(128, -1)) for e in range(EF)])
    b1 = np.asarray(ins["L0_ffn_b1"], np.float32)
    b2 = np.asarray(ins["L0_ffn_b2"], np.float32)
    b1h = np.ascontiguousarray(b1.reshape(8, 32, 128).transpose(1, 0, 2).reshape(32, 1024))
    b2h = np.ascontiguousarray(b2.reshape(8, 8, 128).transpose(1, 0, 2).reshape(8, 1024))

    sel16 = np.zeros((16, 128), np.float32)
    for k in range(16):
        for e in range(8):
            sel16[k, e * 16 + k] = 1.0
    sel2 = np.zeros((128, 16), np.float32)
    for e in range(8):
        for k in range(16):
            sel2[e * 16 + k, k] = 1.0
    iotaE = np.repeat(np.arange(8, dtype=np.float32), 16)[:, None]
    iotaPAD = np.broadcast_to(np.arange(CAPIO, dtype=np.float32) * 128, (8, CAPIO)).copy()
    iotaTILE = np.broadcast_to(np.arange(NTILE, dtype=np.float32) * 128, (8, NTILE)).copy()
    colk0 = np.ones((1, 128), np.float32)
    colk0[0, 0::16] = 0.0
    selb16 = np.zeros((16, 16 * 128), np.float32)
    for n in range(16):
        selb16[n, 128 * n:128 * (n + 1)] = 1.0
    iota8m = np.broadcast_to(np.arange(8, dtype=np.float32) - BIGV, (128, 8)).copy()

    def dslice(w, q, dirb):
        return w[DQ * q:DQ * (q + 1)] if not dirb else w

    maps = []
    for core in range(8):
        b, q = core // 4, core % 4
        xrows = inproj[DQ * q:DQ * (q + 1)]
        zrows = inproj[2048 + DQ * q:2048 + DQ * (q + 1)]
        winp = lhsT_tile(np.concatenate([xrows, zrows]), 1024, 1024)

        def dirpack(wf, wb, fn):
            return np.stack([fn(wf), fn(wb)])

        xw = np.asarray(ins["L0_attn_x_proj_w"], np.float32)
        xwb = np.asarray(ins["L0_attn_x_proj_w_b"], np.float32)
        def xp(w):
            t = np.ascontiguousarray(w[:, DQ * q:DQ * (q + 1)].T)
            return np.ascontiguousarray(
                t.reshape(4, 128, 96).transpose(1, 0, 2).reshape(128, 4 * 96))
        wxp = np.stack([xp(xw), xp(xwb)])

        dtw = np.asarray(ins["L0_attn_dt_proj_w"], np.float32)
        dtwb = np.asarray(ins["L0_attn_dt_proj_w_b"], np.float32)
        def dtp(w):
            return np.ascontiguousarray(w[DQ * q:DQ * (q + 1)].T)
        wdt = np.stack([dtp(dtw), dtp(dtwb)])

        def q4(v):
            return np.ascontiguousarray(
                v[DQ * q:DQ * (q + 1)].reshape(4, 128).T)

        def q4k(v, K):
            return np.ascontiguousarray(
                v[DQ * q:DQ * (q + 1)].reshape(4, 128, K).transpose(1, 0, 2)
                .reshape(128, 4 * K))

        wcv = np.stack([q4k(np.asarray(ins["L0_attn_conv_w"], np.float32), 4),
                        q4k(np.asarray(ins["L0_attn_conv_w_b"], np.float32), 4)])
        bcv = np.stack([q4(np.asarray(ins["L0_attn_conv_b"], np.float32)),
                        q4(np.asarray(ins["L0_attn_conv_b_b"], np.float32))])
        bdt = np.stack([q4(np.asarray(ins["L0_attn_dt_proj_b"], np.float32)),
                        q4(np.asarray(ins["L0_attn_dt_proj_b_b"], np.float32))])
        alog = np.stack([q4k(np.asarray(ins["L0_attn_A_log"], np.float32), NST),
                         q4k(np.asarray(ins["L0_attn_A_b_log"], np.float32), NST)])
        dvec = np.stack([q4(np.asarray(ins["L0_attn_D"], np.float32)),
                         q4(np.asarray(ins["L0_attn_D_b"], np.float32))])
        wout = lhsT_tile(outproj, 2048, 1024)
        _wgr = np.ascontiguousarray(np.asarray(ins["L0_ffn_wg_red"], np.float32).T)
        wgred = np.ascontiguousarray(
            _wgr.reshape(8, 128, 16).transpose(1, 0, 2).reshape(128, 128))
        agoff = (q * OWN // 512) * (2048 * 512) + (q * OWN % 512)
        qmeta = np.array([[q * OWN * DM, q * OWN, q * OWN, q * OWN * 16,
                           agoff, 0]],
                         np.int32)
        maps.append({
            "x_b": np.ascontiguousarray(x[b, :L]),
            "w_inproj": winp, "w_xproj": wxp, "w_dt": wdt, "w_conv": wcv,
            "b_conv": bcv, "b_dt": bdt, "a_log": alog, "d_vec": dvec,
            "w_out": wout, "w_gred": wgred,
            "wg_in": np.asarray(ins["L0_ffn_wg"], np.float32),
            "w1_d": w1h, "w2_d": w2h, "b1h": b1h, "b2h": b2h,
            "ident_in": np.eye(128, dtype=np.float32),
            "sel16_in": sel16, "sel2_in": sel2, "iotaE_in": iotaE,
            "iotaPAD_in": iotaPAD, "iotaTILE_in": iotaTILE,
            "colk0_in": colk0, "iota8m_in": iota8m, "qmeta_in": qmeta,
            "selb16_in": selb16,
        })
    return maps


def kernel(**inputs):
    L = int(np.asarray(inputs["L0_attn_hidden_states"]).shape[1])
    nc = build_nc(L=L)
    maps = _prep_inputs(inputs, L=L)
    res = run_bass_kernel_spmd(nc, maps, core_ids=list(range(8)))
    OWN = L // 4
    full = np.zeros((2, L, DM), np.float32)
    for core in range(8):
        b, q = core // 4, core % 4
        full[b, OWN * q:OWN * (q + 1), :] = res.results[core]["out"]
    return full
